# version 05
# baseline (speedup 1.0000x reference)
import os as _os
import numpy as np
import concourse.bass as bass
import concourse.mybir as mybir
from concourse.bass_utils import run_bass_kernel_spmd

F32 = mybir.dt.float32
BF16 = mybir.dt.bfloat16
AF = mybir.ActivationFunctionType
ALU = mybir.AluOpType
AX = mybir.AxisListType

D = 2048
H = 16
NB = 4
FF = 768
NE = 8
KT = D // 128
IT = FF // 128
C = 512
NT = C // 128
EPS = 1e-6
LNEPS = 1e-5
LN_INV = 1.0 / D


class Sem:
    def __init__(self, handle, step):
        self.h = handle
        self.step = step
        self.n = 0


def build_program(n_chunks):
    nc = bass.Bass("TRN2", target_bir_lowering=False,
                   detect_race_conditions=bool(_os.environ.get("K_RACE")))

    xt = nc.dram_tensor("xt", [n_chunks, 128, KT, C], F32, kind="ExternalInput").ap()
    wqt = nc.dram_tensor("wqt", [KT, 128, KT, 128], F32, kind="ExternalInput").ap()
    wot = nc.dram_tensor("wot", [KT, 128, KT, 128], F32, kind="ExternalInput").ap()
    wg_i = nc.dram_tensor("wg", [NE, IT, 128, KT, 128], F32, kind="ExternalInput").ap()
    wu_i = nc.dram_tensor("wu", [NE, IT, 128, KT, 128], F32, kind="ExternalInput").ap()
    wd_i = nc.dram_tensor("wd", [KT, 128, NE, IT, 128], F32, kind="ExternalInput").ap()
    wgt = nc.dram_tensor("wgt", [128, KT * 8], F32, kind="ExternalInput").ap()
    mem_i = nc.dram_tensor("mem", [128, NB * H * 128], F32, kind="ExternalInput").ap()
    norms_i = nc.dram_tensor("norms", [128, H * NB], F32, kind="ExternalInput").ap()
    bw_i = nc.dram_tensor("bw", [H, NB], F32, kind="ExternalInput").ap()
    ident_i = nc.dram_tensor("ident", [128, 128], F32, kind="ExternalInput").ap()
    out_d = nc.dram_tensor("out", [n_chunks, KT, 128, C], F32, kind="ExternalOutput").ap()

    wq_s = nc.dram_tensor("wq_s", [KT, 128, KT, 128], BF16, kind="Internal").ap()
    wo_s = nc.dram_tensor("wo_s", [KT, 128, KT, 128], BF16, kind="Internal").ap()
    wg_s = nc.dram_tensor("wg_s", [NE, IT, 128, KT, 128], BF16, kind="Internal").ap()
    wu_s = nc.dram_tensor("wu_s", [NE, IT, 128, KT, 128], BF16, kind="Internal").ap()
    wd_s = nc.dram_tensor("wd_s", [KT, 128, NE, IT, 128], BF16, kind="Internal").ap()

    from contextlib import ExitStack
    es = ExitStack()
    with es:
        sb = lambda name, shape, dt: es.enter_context(nc.sbuf_tensor(name, shape, dt))
        pool = sb("pool", [128, 24576], BF16)
        xbuf = sb("xbuf", [128, 8192], BF16)
        x1 = sb("x1", [128, KT * C], BF16)
        r2 = sb("r2", [128, KT * C], BF16)
        sigma = sb("sigma", [128, 2 * C], BF16)
        sq = sb("sq", [128, 2 * C], BF16)
        rsb = sb("rsb", [128, 2 * C], F32)
        outt = sb("outt", [128, 2 * C], F32)
        wqb = sb("wqb", [128, 2 * KT * 128], BF16)
        wob = sb("wob", [128, 2 * KT * 128], BF16)
        wgb = sb("wgb", [128, 2 * KT * 128], BF16)
        wub = sb("wub", [128, 2 * KT * 128], BF16)
        wdb = sb("wdb", [128, 2 * NE * IT * 128], BF16)
        memb = sb("memb", [128, NB * H * 128], BF16)
        normb = sb("normb", [128, H * NB], BF16)
        wgtb = sb("wgtb", [128, KT * 8], BF16)
        identb = sb("identb", [128, 128], BF16)
        identf = sb("identf", [128, 128], F32)
        onesc = sb("onesc", [128, 1], BF16)
        onescf = sb("onescf", [128, 1], F32)
        onesrf = sb("onesrf", [1, 128], F32)
        onesr = sb("onesr", [1, 128], BF16)
        normf = sb("normf", [128, H * NB], F32)
        bwf = sb("bwf", [H, NB], F32)
        bsc1 = sb("bsc1", [H, 3 * NB], F32)
        sumr = sb("sumr", [1, H * NB], F32)
        snr = sb("snr", [1, 2 * NB], F32)
        wflat = sb("wflat", [1, H * NB], F32)
        weff = sb("weff", [H, NB], F32)
        wbc = sb("wbc", [128, H * NB], F32)
        call = sb("call", [128, 2 * 16], F32)
        bsc = sb("bsc", [128, 2 * 512], BF16)
        ct = sb("ct", [128, 2 * NT * 128], BF16)
        rowsb = sb("rowsb", [1, 2 * C], F32)
        rows2 = sb("rows2", [1, 2 * C], F32)
        logitsb = sb("logitsb", [8, C], F32)
        rt = sb("rt", [128, NT * 16], F32)
        crow = sb("crow", [1, 2 * C], F32)

        psum = [es.enter_context(nc.psum_tensor(f"P{i}", [128, 512], F32))
                for i in range(8)]

        def mksem(name, step=16):
            return Sem(es.enter_context(nc.semaphore(name)), step)

        SSET = mksem("sset")
        BWS = mksem("bws")
        WFS = mksem("wfs")
        CVA = mksem("cva")
        CVB = mksem("cvb")
        GSET = mksem("gset")
        XS = mksem("xs")
        WQ = [mksem("wq0"), mksem("wq1")]
        WO = [mksem("wo0"), mksem("wo1")]
        WG = [mksem("wgs0"), mksem("wgs1")]
        WU = [mksem("wus0"), mksem("wus1")]
        WD = [mksem("wds0"), mksem("wds1")]
        CR = [mksem("cr0"), mksem("cr1")]
        OS = [mksem("os0"), mksem("os1")]
        PE = mksem("pe", 1)
        DV = mksem("dv", 1)
        AC = mksem("ac", 1)
        PL = mksem("pl", 1)
        TOTAL_SEMS = (SSET, BWS, WFS, CVA, CVB, GSET)
        ALL_DMA_SEMS = [SSET, BWS, WFS, CVA, CVB, GSET, XS] + WQ + WO + WG + WU + WD + CR + OS

        plan = {"t": [], "v": [], "s": [], "g": [], "y": []}
        SEM_OF = {"t": PE, "v": DV, "s": AC, "p": PL}
        PLAN_OF = {"t": "t", "v": "v", "s": "s", "p": "g"}
        QUEUE_PLAN = {"y": "y", "g": "g", "sd": "s"}
        EV = {}
        waited = {k: {} for k in plan}

        def _emit(ep, sem, build_fn, deps, event):
            cap = []
            hw = waited[ep]
            for d in deps:
                s, v = EV[d]
                if s in TOTAL_SEMS:
                    v = s.n
                key = id(s)
                if hw.get(key, -1) >= v:
                    continue
                hw[key] = v
                cap.append((s.h, v))
            if event is not None:
                sem.n += sem.step
                EV[event] = (sem, sem.n)

                def closure(e, cap=cap, build_fn=build_fn, sh=sem.h, st=sem.step):
                    for (h, v) in cap:
                        e.wait_ge(h, v)
                    build_fn(e).then_inc(sh, st)
            else:
                def closure(e, cap=cap, build_fn=build_fn):
                    for (h, v) in cap:
                        e.wait_ge(h, v)
                    build_fn(e)
            plan[ep].append(closure)

        def emit(eng, build_fn, deps=(), event=None):
            _emit(PLAN_OF[eng], SEM_OF[eng], build_fn, deps, event)

        def dma(q, sem, out_ap, in_ap, deps=(), event=None):
            _emit(QUEUE_PLAN[q], sem,
                  lambda e, o=out_ap, i=in_ap: e.dma_start(out=o, in_=i),
                  deps, event)

        dma("y", SSET, normf[:], norms_i, event="ld_normf")
        dma("y", BWS, bwf[:], bw_i, event="ld_bw")
        dma("y", SSET, identf[:], ident_i, event="ld_identf")
        dma("g", GSET, identb[:], ident_i, event="ld_identb")
        dma("g", GSET, wgtb[:], wgt, event="ld_wgt")
        dma("g", GSET, normb[:], norms_i, event="ld_normb")
        mem_stage = pool[:, 0:16384].bitcast(F32)
        dma("y", SSET, mem_stage, mem_i, event="ld_mem")

        emit("v", lambda e: e.memset(onesc[:], 1.0), event="ones1")
        emit("v", lambda e: e.memset(onesr[:], 1.0), event="ones2")
        emit("v", lambda e: e.memset(onesrf[:], 1.0), event="ones3")
        emit("v", lambda e: e.memset(onescf[:], 1.0), event="ones4")

        emit("v", lambda e: e.reduce_max(bsc1[:, 0:1], bwf[:], axis=AX.X),
             deps=["ld_bw"], event="bs_m")
        emit("v", lambda e: e.tensor_scalar(bsc1[:, NB:2 * NB], bwf[:],
                                            bsc1[:, 0:1], None, ALU.subtract),
             deps=["bs_m"], event="bs_d")
        emit("s", lambda e: e.activation(bsc1[:, NB:2 * NB], bsc1[:, NB:2 * NB],
                                         AF.Exp),
             deps=["bs_d"], event="bs_e")
        emit("v", lambda e: e.reduce_sum(bsc1[:, 1:2], bsc1[:, NB:2 * NB],
                                         axis=AX.X),
             deps=["bs_e"], event="bs_s")
        emit("v", lambda e: e.reciprocal(bsc1[:, 2:3], bsc1[:, 1:2]),
             deps=["bs_s"], event="bs_r")
        emit("v", lambda e: e.tensor_scalar(bsc1[:, 2 * NB:3 * NB],
                                            bsc1[:, NB:2 * NB], bsc1[:, 2:3],
                                            None, ALU.mult),
             deps=["bs_r"], event="bs_sm")
        emit("t", lambda t: t.matmul(psum[0][0:1, 0:H * NB], onescf[:], normf[:],
                                     start=True, stop=True),
             deps=["ld_normf", "ones4"], event="act_cs")
        emit("v", lambda e: e.tensor_copy(sumr[:], psum[0][0:1, 0:H * NB]),
             deps=["act_cs"], event="act_cp")
        sum_v = sumr[0:1, :].rearrange("p (h n) -> p n h", h=H, n=NB)
        emit("v", lambda e: e.tensor_reduce(snr[0:1, 0:NB], sum_v, axis=AX.X,
                                            op=ALU.add),
             deps=["act_cp"], event="act_sn")
        emit("v", lambda e: e.tensor_scalar(snr[0:1, NB:2 * NB], snr[0:1, 0:NB],
                                            float(EPS), None, ALU.is_ge),
             deps=["act_sn"], event="act_a")
        emit("t", lambda t: t.matmul(psum[1][0:H, 0:NB], onesrf[0:1, 0:H],
                                     snr[0:1, NB:2 * NB], start=True, stop=True),
             deps=["act_a", "ones3"], event="act_bc")
        emit("v", lambda e: e.tensor_tensor(weff[:], bsc1[:, 2 * NB:3 * NB],
                                            psum[1][0:H, 0:NB], ALU.mult),
             deps=["bs_sm", "act_bc"], event="weff_ok")
        for h_ in range(H):
            dma("y", WFS, wflat[0:1, h_ * NB:(h_ + 1) * NB], weff[h_:h_ + 1, :],
                deps=["weff_ok"], event=f"wflat{h_}")
        EV["wflat_ok"] = EV[f"wflat{H-1}"]
        emit("t", lambda t: t.matmul(psum[2][:, 0:H * NB], onesrf[:],
                                     wflat[0:1, :], start=True, stop=True),
             deps=["wflat_ok", "ones3"], event="wbc_mm")
        emit("v", lambda e: e.tensor_copy(wbc[:], psum[2][:, 0:H * NB]),
             deps=["wbc_mm"], event="wbc_ok")
        for n_ in range(NB):
            for h_ in range(H):
                blk = (n_ * H + h_) * 128
                col = h_ * NB + n_
                emit("v", lambda e, blk=blk, col=col: e.tensor_scalar(
                    memb[:, blk:blk + 128], mem_stage[:, blk:blk + 128],
                    wbc[:, col:col + 1], None, ALU.mult),
                    deps=["wbc_ok", "ld_mem"],
                    event=("memb_ok" if (n_ == NB - 1 and h_ == H - 1) else None))

        dma("g", XS, xbuf[:, :], xt[0].rearrange("p k t -> p (k t)"),
            deps=[], event="xld@0")

        dma("g", CVA, wq_s, wqt, event="cv_wq")
        dma("g", CVA, wo_s, wot, event="cv_wo")
        EV["cvA"] = EV["cv_wo"]
        for e_ in range(NE):
            dma("g", CVB, wg_s[e_], wg_i[e_], event=f"cv_wg{e_}")
            dma("g", CVB, wu_s[e_], wu_i[e_], event=f"cv_wu{e_}")
        for o_ in range(KT):
            dma("g", CVB, wd_s[o_], wd_i[o_], event=f"cv_wd{o_}")
        EV["cvB"] = EV[f"cv_wd{KT-1}"]

        x_bf = xbuf[:, :]
        comb = pool[:, 0:8192]
        r1 = pool[:, 8192:16384]

        for c in range(n_chunks):
            P = c > 0


            def load_wq(h_):
                par = h_ % 2
                deps = [] if P else ["cvA"]
                if P or h_ >= 2:
                    pc, ph = (c, h_ - 2) if h_ >= 2 else (c - 1, h_ + 14)
                    deps.append(f"q{ph}@{pc}")
                dma("y", WQ[par], wqb[:, par * 2048:(par + 1) * 2048],
                    wq_s[h_].rearrange("p k o -> p (k o)"),
                    deps=deps, event=f"wqld{h_}@{c}")

            def load_wo(o_):
                par = o_ % 2
                deps = [] if P else ["cvA"]
                if P or o_ >= 2:
                    pc, po = (c, o_ - 2) if o_ >= 2 else (c - 1, o_ + 14)
                    deps.append(f"attn{po}@{pc}")
                dma("y", WO[par], wob[:, par * 2048:(par + 1) * 2048],
                    wo_s[o_].rearrange("p k o -> p (k o)"),
                    deps=deps, event=f"wold{o_}@{c}")

            def load_wgu(pidx):
                e_, it = divmod(pidx, IT)
                par = pidx % 2
                deps, deps2 = [], []
                if not P:
                    deps.append("cvB")
                    deps2.append("cvB")
                if P or pidx >= 2:
                    qidx, pc = pidx - 2, c
                    if qidx < 0:
                        qidx, pc = qidx + NE * IT, c - 1
                    pe_, pit = divmod(qidx, IT)
                    deps.append(f"g{pe_}_{pit}@{pc}")
                    deps2.append(f"u{pe_}_{pit}@{pc}")
                dma("sd", WG[par], wgb[:, par * 2048:(par + 1) * 2048],
                    wg_s[e_, it].rearrange("p k o -> p (k o)"),
                    deps=deps, event=f"wgld{e_}_{it}@{c}")
                dma("sd", WU[par], wub[:, par * 2048:(par + 1) * 2048],
                    wu_s[e_, it].rearrange("p k o -> p (k o)"),
                    deps=deps2, event=f"wuld{e_}_{it}@{c}")

            def load_wd(o_):
                par = o_ % 2
                deps = [] if P else ["cvB"]
                if P or o_ >= 2:
                    pc, po = (c, o_ - 2) if o_ >= 2 else (c - 1, o_ + 14)
                    deps.append(f"y{po}@{pc}")
                dma("y", WD[par], wdb[:, par * 6144:(par + 1) * 6144],
                    wd_s[o_].rearrange("p e i o -> p (e i o)"),
                    deps=deps, event=f"wdld{o_}@{c}")

            load_wq(0)
            load_wq(1)

            def q_head(h_):
                par = h_ % 2
                deps = [f"xld@{c}", f"wqld{h_}@{c}"]
                if P or h_ >= 2:
                    pc, ph = (c, h_ - 2) if h_ >= 2 else (c - 1, h_ + 14)
                    deps.append(f"sigrd{ph}@{pc}")
                    deps.append(f"smin{ph}@{pc}")
                if P and h_ < 2:
                    deps.append(f"sg{NE-1}_{IT-3}@{c-1}" if h_ == 0
                                else f"up{NE-1}_{IT-3}@{c-1}")
                if not P and h_ == 0:
                    deps.append("act_cp")
                if not P and h_ == 1:
                    deps.append("weff_ok")

                def fn(t, par=par):
                    mm = None
                    for kt in range(KT):
                        mm = t.matmul(
                            psum[par][:, 0:512],
                            wqb[:, par * 2048 + kt * 128:par * 2048 + (kt + 1) * 128],
                            x_bf[:, kt * 512:(kt + 1) * 512],
                            start=(kt == 0), stop=(kt == KT - 1))
                    return mm
                emit("t", fn, deps=deps, event=f"q{h_}@{c}")

            def sigma_head(h_):
                par = h_ % 2
                deps = [f"q{h_}@{c}"]
                if h_ > 0:
                    deps.append(f"sig{h_-1}@{c}")
                elif P:
                    deps += [f"st4_{KT-1}@{c-1}", f"sig{H-1}@{c-1}"]
                emit("v", lambda e, par=par: e.tensor_scalar(
                    sq[:, 0:512], psum[par][:, 0:512], 0.0, None, ALU.min),
                    deps=deps, event=f"smin{h_}@{c}")
                emit("s", lambda e: e.activation(sq[:, 0:512], sq[:, 0:512],
                                                 AF.Exp),
                     deps=[f"smin{h_}@{c}"], event=f"sexp{h_}@{c}")
                rdeps = [f"q{h_}@{c}", f"smin{h_}@{c}"]
                if h_ > 0:
                    rdeps.append(f"sig{h_-1}@{c}")
                elif P:
                    rdeps.append(f"sig{H-1}@{c-1}")
                emit("s", lambda e, par=par: e.activation(
                    sq[:, 512:1024], psum[par][:, 0:512], AF.Relu),
                    deps=rdeps, event=f"srelu{h_}@{c}")
                sdeps = [f"sexp{h_}@{c}", f"srelu{h_}@{c}"]
                if P or h_ >= 2:
                    pc, ph = (c, h_ - 2) if h_ >= 2 else (c - 1, h_ + 14)
                    sdeps.append(f"bank{ph}@{pc}")
                emit("v", lambda e, par=par: e.tensor_tensor(
                    sigma[:, par * 512:(par + 1) * 512], sq[:, 0:512],
                    sq[:, 512:1024], ALU.add),
                    deps=sdeps, event=f"sig{h_}@{c}")
                EV[f"sigrd{h_}@{c}"] = EV[f"srelu{h_}@{c}"]

            def bank_head(h_):
                par = h_ % 2
                deps = [f"sig{h_}@{c}"]
                if P or h_ >= 1:
                    pc, ph = (c, h_ - 1) if h_ >= 1 else (c - 1, H - 1)
                    for t_ in range(NT):
                        deps += [f"t2_{ph}_{t_}@{pc}", f"scrd{ph}_{t_}@{pc}",
                                 f"s0_{ph}_{t_}@{pc}", f"s1_{ph}_{t_}@{pc}"]
                if P or h_ >= 2:
                    pc2, ph2 = (c, h_ - 2) if h_ >= 2 else (c - 1, h_ + 14)
                    deps.append(f"cp2_{ph2}@{pc2}")
                    deps += [f"evac{ph2}_{t_}@{pc2}" for t_ in range(NT)]
                if h_ <= 1:
                    if P:
                        deps += [f"l2@{c-1}_mean", f"l2@{c-1}_msq",
                                 f"R2sb@{c-1}", f"M2sb@{c-1}"]
                        if h_ == 0:
                            deps += [f"sg{NE-1}_{IT-1}@{c-1}",
                                     f"up{NE-1}_{IT-1}@{c-1}",
                                     f"sg{NE-1}_{IT-2}@{c-1}",
                                     f"up{NE-1}_{IT-2}@{c-1}",
                                     f"r2_{KT-2}@{c-1}", f"r2_{KT-1}@{c-1}"]
                    else:
                        if h_ == 0:
                            deps += ["wbc_ok", "memb_ok", "ld_normb"]

                def fn(t, h_=h_, par=par):
                    mm = None
                    for t_ in range(NT):
                        for n_ in range(NB):
                            mm = t.matmul(
                                psum[2 + t_][:, n_ * 128:(n_ + 1) * 128],
                                sigma[:, par * 512 + t_ * 128:par * 512 + (t_ + 1) * 128],
                                memb[:, (n_ * H + h_) * 128:(n_ * H + h_ + 1) * 128],
                                start=(n_ == 0), stop=(n_ == NB - 1))
                        mm = t.matmul(
                            psum[6 + par][:, t_ * 4:(t_ + 1) * 4],
                            sigma[:, par * 512 + t_ * 128:par * 512 + (t_ + 1) * 128],
                            normb[:, h_ * NB:(h_ + 1) * NB],
                            start=True, stop=True)
                    return mm
                emit("t", fn, deps=deps, event=f"bank{h_}@{c}")

                cdeps = [f"bank{h_}@{c}"]
                if h_ >= 2:
                    cdeps.append(f"s1_{h_-2}_{NT-1}@{c}")
                elif P:
                    cdeps.append(f"s1_{H-2+h_}_{NT-1}@{c-1}")
                emit("v", lambda e, par=par: e.tensor_scalar(
                    call[:, par * 16:(par + 1) * 16], psum[6 + par][:, 0:16],
                    float(EPS), None, ALU.max),
                    deps=cdeps, event=f"cprep{h_}@{c}")
                emit("v", lambda e, par=par: e.reciprocal(
                    call[:, par * 16:(par + 1) * 16],
                    call[:, par * 16:(par + 1) * 16]),
                    deps=[f"cprep{h_}@{c}"], event=f"cp2_{h_}@{c}")

                for t_ in range(NT):
                    sp = par * 512

                    def cb(n_, par=par, t_=t_):
                        return call[:, par * 16 + t_ * 4 + n_:
                                    par * 16 + t_ * 4 + n_ + 1]
                    s0d = [f"cp2_{h_}@{c}"]
                    s1d = [f"cp2_{h_}@{c}"]
                    if t_ > 0:
                        s0d.append(f"t2_{h_}_{t_-1}@{c}")
                        s1d.append(f"scrd{h_}_{t_-1}@{c}")
                    elif h_ >= 2:
                        s0d.append(f"t2_{h_-2}_{NT-1}@{c}")
                        s1d.append(f"scrd{h_-2}_{NT-1}@{c}")
                    elif P:
                        s0d.append(f"t2_{H-2+h_}_{NT-1}@{c-1}")
                        s1d.append(f"scrd{H-2+h_}_{NT-1}@{c-1}")
                    emit("s", lambda e, t_=t_, sp=sp, cb=cb: e.activation(
                        bsc[:, sp:sp + 128], psum[2 + t_][:, 0:128], AF.Copy,
                        bias=0.0, scale=cb(0)),
                        deps=s0d, event=f"s0_{h_}_{t_}@{c}")
                    emit("s", lambda e, t_=t_, sp=sp, cb=cb: e.activation(
                        bsc[:, sp + 128:sp + 256], psum[2 + t_][:, 128:256],
                        AF.Copy, bias=0.0, scale=cb(1)),
                        deps=s1d, event=f"s1_{h_}_{t_}@{c}")
                    warc = []
                    if t_ > 0:
                        warc.append(f"comb{h_}_{t_-1}@{c}")
                    elif h_ >= 2:
                        warc.append(f"comb{h_-2}_{NT-1}@{c}")
                    elif P:
                        warc.append(f"comb{H-2+h_}_{NT-1}@{c-1}")
                    emit("v", lambda e, t_=t_, sp=sp, cb=cb: e.scalar_tensor_tensor(
                        bsc[:, sp + 256:sp + 384], psum[2 + t_][:, 256:384],
                        cb(2), bsc[:, sp:sp + 128], ALU.mult, ALU.add),
                        deps=[f"s0_{h_}_{t_}@{c}", f"cp2_{h_}@{c}"] + warc,
                        event=f"t2_{h_}_{t_}@{c}")
                    emit("v", lambda e, t_=t_, sp=sp, cb=cb: e.scalar_tensor_tensor(
                        bsc[:, sp + 384:sp + 512], psum[2 + t_][:, 384:512],
                        cb(3), bsc[:, sp + 128:sp + 256], ALU.mult, ALU.add),
                        deps=[f"s1_{h_}_{t_}@{c}"] + warc,
                        event=f"scrd{h_}_{t_}@{c}")
                    combd = [f"t2_{h_}_{t_}@{c}", f"scrd{h_}_{t_}@{c}"]
                    if h_ >= 2:
                        combd.append(f"tr{h_-2}_{t_}@{c}")
                    elif P:
                        combd.append(f"tr{H-2+h_}_{t_}@{c-1}")
                    cto = par * 512 + t_ * 128
                    emit("p", lambda e, cto=cto, sp=sp: e.tensor_tensor(
                        ct[:, cto:cto + 128], bsc[:, sp + 256:sp + 384],
                        bsc[:, sp + 384:sp + 512], ALU.add),
                        deps=combd, event=f"comb{h_}_{t_}@{c}")

            def bank_trs(h_):
                par = h_ % 2
                pv = psum[6 + par].ap().bitcast(BF16)
                for t_ in range(NT):
                    trd = [f"comb{h_}_{t_}@{c}", "ld_identb", f"cp2_{h_}@{c}"]
                    if h_ >= 2:
                        trd.append(f"evac{h_-2}_{t_}@{c}")
                    elif P:
                        trd.append(f"evac{h_+14}_{t_}@{c-1}")
                    emit("t", lambda t, t_=t_, par=par, pv=pv: t.transpose(
                        pv[:, 512 + t_ * 128:512 + (t_ + 1) * 128],
                        ct[:, par * 512 + t_ * 128:par * 512 + (t_ + 1) * 128],
                        identb[:]),
                        deps=trd, event=f"tr{h_}_{t_}@{c}")
                for t_ in range(NT):
                    emit("s", lambda e, t_=t_, h_=h_, pv=pv: e.activation(
                        comb[:, h_ * 512 + t_ * 128:h_ * 512 + (t_ + 1) * 128],
                        pv[:, 512 + t_ * 128:512 + (t_ + 1) * 128], AF.Copy),
                        deps=[f"tr{h_}_{NT-1}@{c}"], event=f"evac{h_}_{t_}@{c}")

            q_head(0)
            load_wq(2)
            sigma_head(0)
            q_head(1)
            load_wq(3)
            sigma_head(1)
            for h_ in range(H):
                bank_head(h_)
                if h_ + 2 < H:
                    q_head(h_ + 2)
                if h_ + 4 < H:
                    load_wq(h_ + 4)
                if h_ == 0:
                    load_wo(0)
                    load_wo(1)
                if h_ + 2 < H:
                    sigma_head(h_ + 2)
                if h_ >= 1:
                    bank_trs(h_ - 1)
            bank_trs(H - 1)

            comb_done = [f"evac{H-1}_{t_}@{c}" for t_ in range(NT)]
            for o_ in range(KT):
                par = o_ % 2
                deps = comb_done + [f"wold{o_}@{c}"]
                if o_ < 2:
                    deps.append(f"sigrd{14 + o_}@{c}")
                else:
                    deps.append(f"r1_{o_-2}@{c}")

                def fn(t, par=par):
                    mm = None
                    for kt in range(KT):
                        mm = t.matmul(
                            psum[par][:, 0:512],
                            wob[:, par * 2048 + kt * 128:par * 2048 + (kt + 1) * 128],
                            comb[:, kt * 512:(kt + 1) * 512],
                            start=(kt == 0), stop=(kt == KT - 1))
                    return mm
                emit("t", fn, deps=deps, event=f"attn{o_}@{c}")
                if o_ + 2 < KT:
                    load_wo(o_ + 2)
                emit("v", lambda e, o_=o_, par=par: e.tensor_tensor(
                    r1[:, o_ * 512:(o_ + 1) * 512],
                    x_bf[:, o_ * 512:(o_ + 1) * 512],
                    psum[par][:, 0:512], ALU.add),
                    deps=[f"attn{o_}@{c}"], event=f"r1_{o_}@{c}")
                emit("v", lambda e, o_=o_: e.tensor_tensor(
                    sq[:, (o_ % 2) * 512:(o_ % 2 + 1) * 512],
                    r1[:, o_ * 512:(o_ + 1) * 512],
                    r1[:, o_ * 512:(o_ + 1) * 512], ALU.mult),
                    deps=[f"r1_{o_}@{c}"] +
                         ([f"st2_{o_-2}@{c}"] if o_ >= 2 else []),
                    event=f"sq_{o_}@{c}")
                def emit_st12(o_):
                    sdeps = [f"r1_{o_}@{c}"]
                    if o_ == 0:
                        sdeps += [f"t2_{H-1}_0@{c}", f"scrd{H-1}_0@{c}",
                                  f"s0_{H-1}_0@{c}", f"s1_{H-1}_0@{c}"]
                    emit("t", lambda t, o_=o_: t.matmul(
                        psum[2][0:1, 0:512], onesc[:],
                        r1[:, o_ * 512:(o_ + 1) * 512],
                        start=(o_ == 0), stop=(o_ == KT - 1)),
                        deps=sdeps, event=f"st1_{o_}@{c}")
                    s2deps = [f"sq_{o_}@{c}"]
                    if o_ == 0:
                        s2deps += [f"t2_{H-1}_1@{c}", f"scrd{H-1}_1@{c}",
                                   f"s0_{H-1}_1@{c}", f"s1_{H-1}_1@{c}"]
                    emit("t", lambda t, o_=o_: t.matmul(
                        psum[3][0:1, 0:512], onesc[:],
                        sq[:, (o_ % 2) * 512:(o_ % 2 + 1) * 512],
                        start=(o_ == 0), stop=(o_ == KT - 1)),
                        deps=s2deps, event=f"st2_{o_}@{c}")
                if o_ >= 1:
                    emit_st12(o_ - 1)
                if o_ == KT - 1:
                    emit_st12(o_)

            def ln_rows(tag, sum_bank, sq_bank, dep_sum, dep_sq, war_deps):
                emit("v", lambda e: e.tensor_scalar(
                    rows2[0:1, 0:512], psum[sum_bank][0:1, 0:512], LN_INV, None,
                    ALU.mult), deps=[dep_sum], event=f"{tag}_mean")
                emit("v", lambda e: e.tensor_scalar(
                    rows2[0:1, 512:1024], psum[sq_bank][0:1, 0:512], LN_INV,
                    None, ALU.mult), deps=[dep_sq], event=f"{tag}_msq")
                emit("v", lambda e: e.tensor_tensor(
                    rowsb[0:1, 0:512], rows2[0:1, 0:512], rows2[0:1, 0:512],
                    ALU.mult), deps=[f"{tag}_mean"] + war_deps,
                    event=f"{tag}_m2")
                emit("v", lambda e: e.tensor_tensor(
                    rowsb[0:1, 0:512], rows2[0:1, 512:1024], rowsb[0:1, 0:512],
                    ALU.subtract), deps=[f"{tag}_m2", f"{tag}_msq"],
                    event=f"{tag}_var")
                emit("v", lambda e: e.tensor_scalar(
                    rowsb[0:1, 0:512], rowsb[0:1, 0:512], LNEPS, None, ALU.add),
                    deps=[f"{tag}_var"], event=f"{tag}_vare")
                emit("s", lambda e: e.activation(
                    rowsb[0:1, 0:512], rowsb[0:1, 0:512], AF.Sqrt),
                    deps=[f"{tag}_vare"], event=f"{tag}_sd")
                emit("v", lambda e: e.reciprocal(rowsb[0:1, 0:512],
                                                 rowsb[0:1, 0:512]),
                     deps=[f"{tag}_sd"], event=f"{tag}_rstd")
                emit("v", lambda e: e.tensor_tensor(
                    rowsb[0:1, 512:1024], rows2[0:1, 0:512], rowsb[0:1, 0:512],
                    ALU.mult), deps=[f"{tag}_rstd"] + war_deps,
                    event=f"{tag}_mr")

            war1 = [f"bcR2@{c-1}", f"bcM2@{c-1}"] if P else []
            ln_rows(f"l1@{c}", 2, 3, f"st1_{KT-1}@{c}", f"st2_{KT-1}@{c}", war1)
            emit("t", lambda t: t.matmul(psum[4][:, 0:512], onesrf[:],
                                         rowsb[0:1, 0:512], start=True,
                                         stop=True),
                 deps=[f"l1@{c}_rstd", f"t2_{H-1}_2@{c}", f"scrd{H-1}_2@{c}",
                       f"s0_{H-1}_2@{c}", f"s1_{H-1}_2@{c}"] +
                      ([f"r2_{KT-2}@{c-1}"] if P else []),
                 event=f"bcR@{c}")
            emit("t", lambda t: t.matmul(psum[5][:, 0:512], onesrf[:],
                                         rowsb[0:1, 512:1024], start=True,
                                         stop=True),
                 deps=[f"l1@{c}_mr", f"t2_{H-1}_3@{c}", f"scrd{H-1}_3@{c}",
                       f"s0_{H-1}_3@{c}", f"s1_{H-1}_3@{c}"] +
                      ([f"r2_{KT-1}@{c-1}"] if P else []),
                 event=f"bcM@{c}")
            emit("v", lambda e: e.tensor_copy(rsb[:, 0:512], psum[4][:, 0:512]),
                 deps=[f"bcR@{c}"], event=f"Rsb@{c}")
            emit("v", lambda e: e.tensor_copy(rsb[:, 512:1024],
                                              psum[5][:, 0:512]),
                 deps=[f"bcM@{c}"], event=f"Msb@{c}")
            for kt in range(KT):
                eng = "v" if kt % 2 == 0 else "p"
                apd = [f"Rsb@{c}"]
                if kt >= 2:
                    apd.append(f"x1_{kt-2}@{c}")
                elif P:
                    apd.append(f"os_{14 + kt}@{c-1}")
                emit(eng, lambda e, kt=kt: e.tensor_tensor(
                    outt[:, (kt % 2) * 512:(kt % 2 + 1) * 512],
                    r1[:, kt * 512:(kt + 1) * 512], rsb[:, 0:512], ALU.mult),
                    deps=apd, event=f"ap1_{kt}@{c}")
                x1d = [f"ap1_{kt}@{c}", f"Msb@{c}"]
                if P:
                    x1d += [f"g{NE-1}_{IT-1}@{c-1}", f"u{NE-1}_{IT-1}@{c-1}",
                            f"r2_{kt}@{c-1}", f"lg@{c-1}"]
                emit(eng, lambda e, kt=kt: e.tensor_tensor(
                    x1[:, kt * 512:(kt + 1) * 512],
                    outt[:, (kt % 2) * 512:(kt % 2 + 1) * 512],
                    rsb[:, 512:1024], ALU.subtract),
                    deps=x1d, event=f"x1_{kt}@{c}")

            PDEPTH = 3
            def ffn_mms(idx):
                e_, it = divmod(idx, IT)
                gpar = idx % PDEPTH
                gb, ub = 2 * gpar, 2 * gpar + 1
                woff = (idx % 2) * 2048
                deps = [f"wgld{e_}_{it}@{c}"]
                if idx >= PDEPTH:
                    pe_, pit = divmod(idx - PDEPTH, IT)
                    deps.append(f"hp{pe_}_{pit}@{c}")
                else:
                    deps += [f"ap1_{KT-1}@{c}"]
                    if gpar == 0:
                        deps += [f"r1_{KT-2}@{c}", f"r1_{KT-1}@{c}"]

                ktw = None
                if idx < PDEPTH:
                    ktw = [EV[f"x1_{kt}@{c}"] for kt in range(KT)]

                def fng(t, gb=gb, woff=woff, ktw=ktw):
                    mm = None
                    for kt in range(KT):
                        if ktw is not None:
                            s, v = ktw[kt]
                            t.wait_ge(s.h, v)
                        mm = t.matmul(
                            psum[gb][:, 0:512],
                            wgb[:, woff + kt * 128:woff + (kt + 1) * 128],
                            x1[:, kt * 512:(kt + 1) * 512],
                            start=(kt == 0), stop=(kt == KT - 1))
                    return mm
                emit("t", fng, deps=deps, event=f"g{e_}_{it}@{c}")
                deps_u = [f"wuld{e_}_{it}@{c}"]
                if idx >= PDEPTH:
                    pe_, pit = divmod(idx - PDEPTH, IT)
                    deps_u.append(f"hp{pe_}_{pit}@{c}")

                def fnu(t, ub=ub, woff=woff):
                    mm = None
                    for kt in range(KT):
                        mm = t.matmul(
                            psum[ub][:, 0:512],
                            wub[:, woff + kt * 128:woff + (kt + 1) * 128],
                            x1[:, kt * 512:(kt + 1) * 512],
                            start=(kt == 0), stop=(kt == KT - 1))
                    return mm
                emit("t", fnu, deps=deps_u, event=f"u{e_}_{it}@{c}")
                if idx + 2 < NE * IT:
                    load_wgu(idx + 2)

            def ffn_cons(idx):
                e_, it = divmod(idx, IT)
                gpar = idx % PDEPTH
                gb, ub = 2 * gpar, 2 * gpar + 1
                spar = idx % 2
                sgd = [f"g{e_}_{it}@{c}"]
                if idx >= 2:
                    pe_, pit = divmod(idx - 2, IT)
                    sgd.append(f"hp{pe_}_{pit}@{c}")
                else:
                    sgd.append(f"st2_{14 + idx}@{c}")
                emit("s", lambda e, gb=gb, spar=spar: e.activation(
                    sq[:, spar * 512:(spar + 1) * 512], psum[gb][:, 0:512],
                    AF.Silu),
                    deps=sgd, event=f"sg{e_}_{it}@{c}")
                upd = [f"u{e_}_{it}@{c}", f"Bev{e_}@{c}"]
                if idx >= 1:
                    pe_, pit = divmod(idx - 1, IT)
                    upd.append(f"hp{pe_}_{pit}@{c}")
                emit("v", lambda e, ub=ub, e_=e_: e.tensor_tensor(
                    outt[:, 0:512], psum[ub][:, 0:512],
                    r2[:, e_ * 512:(e_ + 1) * 512], ALU.mult),
                    deps=upd, event=f"up{e_}_{it}@{c}")
                emit("v", lambda e, e_=e_, it=it, spar=spar: e.tensor_tensor(
                    pool[:, e_ * 3072 + it * 512:e_ * 3072 + (it + 1) * 512],
                    sq[:, spar * 512:(spar + 1) * 512], outt[:, 0:512],
                    ALU.mult),
                    deps=[f"sg{e_}_{it}@{c}", f"up{e_}_{it}@{c}"],
                    event=f"hp{e_}_{it}@{c}")


            if c + 1 < n_chunks:
                dma("g", XS, x_bf, xt[c + 1].rearrange("p k t -> p (k t)"),
                    deps=[f"r1_{KT-1}@{c}", f"q{H-1}@{c}"],
                    event=f"xld@{c+1}")

            x1all = [f"x1_{kt}@{c}" for kt in range(KT)]

            lgw = [EV[f"x1_{kt}@{c}"] for kt in range(KT)]

            def fn_logits(t, lgw=lgw):
                mm = None
                for kt in range(KT):
                    s, v = lgw[kt]
                    t.wait_ge(s.h, v)
                    mm = t.matmul(psum[6][0:8, 0:512],
                                  wgtb[:, kt * 8:(kt + 1) * 8],
                                  x1[:, kt * 512:(kt + 1) * 512],
                                  start=(kt == 0), stop=(kt == KT - 1))
                return mm
            emit("t", fn_logits,
                 deps=["ld_wgt", f"cp2_{H-2}@{c}", f"cp2_{H-1}@{c}"] +
                      [f"evac{H-2}_{t_}@{c}" for t_ in range(NT)],
                 event=f"lg@{c}")
            lgsbd = [f"lg@{c}"]
            if P:
                lgsbd.append(f"crow{NE-1}@{c-1}")
            emit("v", lambda e: e.tensor_copy(logitsb[:], psum[6][0:8, 0:512]),
                 deps=lgsbd, event=f"lgsb@{c}")
            for t_ in range(NT):
                emit("t", lambda t, t_=t_: t.transpose(
                    psum[7][:, t_ * 8:(t_ + 1) * 8],
                    logitsb[:, t_ * 128:(t_ + 1) * 128], identf[0:8, 0:8]),
                    deps=[f"lgsb@{c}", "ld_identf", f"cp2_{H-1}@{c}"] +
                         [f"evac{H-1}_{t2_}@{c}" for t2_ in range(NT)],
                    event=f"lgt{t_}@{c}")
            for t_ in range(NT):
                rb = t_ * 16
                rmd = [f"lgt{NT-1}@{c}"]
                if P:
                    rmd.append(f"ctp{t_}@{c-1}")
                emit("v", lambda e, rb=rb, t_=t_: e.reduce_max(
                    rt[:, rb:rb + 1], psum[7][:, t_ * 8:t_ * 8 + 8], axis=AX.X),
                    deps=rmd, event=f"rm1_{t_}@{c}")
                emit("v", lambda e, rb=rb, t_=t_: e.tensor_scalar(
                    rt[:, rb + 8:rb + 16], psum[7][:, t_ * 8:t_ * 8 + 8],
                    rt[:, rb:rb + 1], None, ALU.subtract),
                    deps=[f"rm1_{t_}@{c}"], event=f"rd_{t_}@{c}")
                emit("v", lambda e, rb=rb: e.tensor_scalar(
                    outt[:, 0:8], rt[:, rb + 8:rb + 16], 0.0, None, ALU.is_ge),
                    deps=[f"rd_{t_}@{c}", f"x1_{KT-1}@{c}"],
                    event=f"rmask_{t_}@{c}")
                emit("v", lambda e, rb=rb: e.scalar_tensor_tensor(
                    outt[:, 8:16], outt[:, 0:8], -1e30, rt[:, rb + 8:rb + 16],
                    ALU.mult, ALU.add),
                    deps=[f"rmask_{t_}@{c}"], event=f"rmsk2_{t_}@{c}")
                emit("v", lambda e, rb=rb: e.reduce_max(
                    rt[:, rb + 1:rb + 2], outt[:, 8:16], axis=AX.X),
                    deps=[f"rmsk2_{t_}@{c}"], event=f"rm2_{t_}@{c}")
                emit("s", lambda e, rb=rb: e.activation(
                    rt[:, rb + 8:rb + 16], rt[:, rb + 8:rb + 16], AF.Exp),
                    deps=[f"rm2_{t_}@{c}"], event=f"re1_{t_}@{c}")
                emit("s", lambda e, rb=rb: e.activation(
                    rt[:, rb + 2:rb + 3], rt[:, rb + 1:rb + 2], AF.Exp),
                    deps=[f"rm2_{t_}@{c}"], event=f"re2_{t_}@{c}")
                emit("v", lambda e, rb=rb: e.tensor_scalar(
                    rt[:, rb + 3:rb + 4], rt[:, rb + 2:rb + 3], 1.0, None,
                    ALU.add),
                    deps=[f"re2_{t_}@{c}"], event=f"rden_{t_}@{c}")
                emit("v", lambda e, rb=rb: e.reciprocal(
                    rt[:, rb + 4:rb + 5], rt[:, rb + 3:rb + 4]),
                    deps=[f"rden_{t_}@{c}"], event=f"rrec_{t_}@{c}")
                emit("v", lambda e, rb=rb: e.tensor_scalar(
                    outt[:, 0:8], rt[:, rb + 8:rb + 16], rt[:, rb + 2:rb + 3],
                    None, ALU.is_ge),
                    deps=[f"re1_{t_}@{c}", f"re2_{t_}@{c}"],
                    event=f"rtm_{t_}@{c}")
                emit("v", lambda e, rb=rb: e.tensor_tensor(
                    outt[:, 8:16], rt[:, rb + 8:rb + 16], outt[:, 0:8],
                    ALU.mult),
                    deps=[f"rtm_{t_}@{c}"], event=f"rcm1_{t_}@{c}")
                emit("v", lambda e, rb=rb: e.tensor_scalar(
                    rt[:, rb + 8:rb + 16], outt[:, 8:16], rt[:, rb + 4:rb + 5],
                    None, ALU.mult),
                    deps=[f"rcm1_{t_}@{c}", f"rrec_{t_}@{c}"],
                    event=f"rcb_{t_}@{c}")
            load_wgu(0)
            load_wgu(1)
            ffn_mms(0)
            ffn_mms(1)
            ffn_mms(2)
            for t_ in range(NT):
                rb = t_ * 16
                emit("t", lambda t, t_=t_, rb=rb: t.transpose(
                    psum[6][0:8, t_ * 128:(t_ + 1) * 128],
                    rt[:, rb + 8:rb + 16], identf[:]),
                    deps=[f"rcb_{t_}@{c}", f"lgsb@{c}"], event=f"ctp{t_}@{c}")
            emit("v", lambda e: e.tensor_copy(logitsb[:], psum[6][0:8, 0:512]),
                 deps=[f"ctp{t_}@{c}" for t_ in range(NT)], event=f"cmrows@{c}")
            for e_ in range(NE):
                par = e_ % 2
                cdp = [f"cmrows@{c}"]
                if e_ >= 2:
                    cdp.append(f"Bmm{e_-2}@{c}")
                dma("y", CR[par], crow[0:1, par * 512:(par + 1) * 512],
                    logitsb[e_:e_ + 1, :], deps=cdp, event=f"crow{e_}@{c}")
                bmd = [f"crow{e_}@{c}", "ones3"]
                if e_ >= 2:
                    bmd.append(f"Bev{e_-2}@{c}")
                else:
                    bmd += [f"rd_{t_}@{c}" for t_ in range(NT)]
                    bmd += [f"ctp{t_}@{c}" for t_ in range(NT)]
                emit("t", lambda t, par=par: t.matmul(
                    psum[6 + par][:, 0:512], onesrf[:],
                    crow[0:1, par * 512:(par + 1) * 512], start=True, stop=True),
                    deps=bmd, event=f"Bmm{e_}@{c}")
                bed = [f"Bmm{e_}@{c}"]
                if P:
                    bed.append(f"ap2_{e_}@{c-1}")
                emit("v", lambda e, e_=e_, par=par: e.tensor_copy(
                    r2[:, e_ * 512:(e_ + 1) * 512], psum[6 + par][:, 0:512]),
                    deps=bed, event=f"Bev{e_}@{c}")

            ffn_cons(0)
            ffn_cons(1)
            ffn_cons(2)
            for idx in range(3, NE * IT):
                ffn_mms(idx)
                ffn_cons(idx)

            load_wd(0)
            load_wd(1)
            hall = [f"hp{e_}_{it}@{c}" for e_ in range(NE) for it in range(IT)]
            for o_ in range(KT):
                par = o_ % 2
                deps = [f"wdld{o_}@{c}"]
                if o_ == 0:
                    deps += hall + [f"Rsb@{c}", f"Msb@{c}",
                                    f"sg{NE-1}_{IT-1}@{c}",
                                    f"up{NE-1}_{IT-1}@{c}"]
                elif o_ == 1:
                    pass
                if o_ >= 2:
                    deps.append(f"r2_{o_-2}@{c}")

                def fny(t, par=par):
                    mm = None
                    for e_ in range(NE):
                        for it in range(IT):
                            mm = t.matmul(
                                psum[4 + par][:, 0:512],
                                wdb[:, par * 6144 + (e_ * IT + it) * 128:
                                    par * 6144 + (e_ * IT + it + 1) * 128],
                                pool[:, e_ * 3072 + it * 512:
                                     e_ * 3072 + (it + 1) * 512],
                                start=(e_ == 0 and it == 0),
                                stop=(e_ == NE - 1 and it == IT - 1))
                    return mm
                emit("t", fny, deps=deps, event=f"y{o_}@{c}")
                if o_ + 2 < KT:
                    load_wd(o_ + 2)
                r2d = [f"y{o_}@{c}"]
                if o_ < NE:
                    r2d += [f"up{o_}_{it}@{c}" for it in range(IT)]
                emit("v", lambda e, o_=o_, par=par: e.tensor_tensor(
                    r2[:, o_ * 512:(o_ + 1) * 512],
                    x1[:, o_ * 512:(o_ + 1) * 512],
                    psum[4 + par][:, 0:512], ALU.add),
                    deps=r2d, event=f"r2_{o_}@{c}")
                emit("v", lambda e, o_=o_: e.tensor_tensor(
                    sq[:, (o_ % 2) * 512:(o_ % 2 + 1) * 512],
                    r2[:, o_ * 512:(o_ + 1) * 512],
                    r2[:, o_ * 512:(o_ + 1) * 512], ALU.mult),
                    deps=[f"r2_{o_}@{c}"] +
                         ([f"st4_{o_-2}@{c}"] if o_ >= 2 else
                          [f"sg{NE-1}_{IT-1}@{c}"]),
                    event=f"sq2_{o_}@{c}")
                def emit_st34(o_):
                    s3d = [f"r2_{o_}@{c}"]
                    if o_ == 0:
                        s3d += [f"Bev{NE-2}@{c}", f"Bev{NE-1}@{c}",
                                f"rm1_{NT-1}@{c}", f"rd_{NT-1}@{c}",
                                f"cmrows@{c}"]
                    emit("t", lambda t, o_=o_: t.matmul(
                        psum[6][0:1, 0:512], onesc[:],
                        r2[:, o_ * 512:(o_ + 1) * 512],
                        start=(o_ == 0), stop=(o_ == KT - 1)),
                        deps=s3d, event=f"st3_{o_}@{c}")
                    s4d = [f"sq2_{o_}@{c}"]
                    if o_ == 0:
                        s4d.append(f"Bev{NE-1}@{c}")
                    emit("t", lambda t, o_=o_: t.matmul(
                        psum[7][0:1, 0:512], onesc[:],
                        sq[:, (o_ % 2) * 512:(o_ % 2 + 1) * 512],
                        start=(o_ == 0), stop=(o_ == KT - 1)),
                        deps=s4d, event=f"st4_{o_}@{c}")
                if o_ >= 1:
                    emit_st34(o_ - 1)
                if o_ == KT - 1:
                    emit_st34(o_)
            EV[f"ydone@{c}"] = EV[f"y{KT-1}@{c}"]

            war2 = [f"bcR@{c}", f"bcM@{c}"]
            ln_rows(f"l2@{c}", 6, 7, f"st3_{KT-1}@{c}", f"st4_{KT-1}@{c}", war2)
            emit("t", lambda t: t.matmul(psum[6][:, 0:512], onesrf[:],
                                         rowsb[0:1, 0:512], start=True,
                                         stop=True),
                 deps=[f"l2@{c}_rstd"],
                 event=f"bcR2@{c}")
            emit("t", lambda t: t.matmul(psum[7][:, 0:512], onesrf[:],
                                         rowsb[0:1, 512:1024], start=True,
                                         stop=True),
                 deps=[f"l2@{c}_mr"], event=f"bcM2@{c}")
            emit("v", lambda e: e.tensor_copy(rsb[:, 0:512], psum[6][:, 0:512]),
                 deps=[f"bcR2@{c}", f"ap1_{KT-1}@{c}"], event=f"R2sb@{c}")
            emit("v", lambda e: e.tensor_copy(rsb[:, 512:1024],
                                              psum[7][:, 0:512]),
                 deps=[f"bcM2@{c}", f"x1_{KT-1}@{c}"], event=f"M2sb@{c}")
            for kt in range(KT):
                eng = "v" if kt % 2 == 0 else "p"
                a2d = [f"R2sb@{c}", f"up{NE-1}_{IT-1}@{c}"]
                if kt >= 2:
                    a2d.append(f"os_{kt-2}@{c}")
                emit(eng, lambda e, kt=kt: e.tensor_tensor(
                    outt[:, (kt % 2) * 512:(kt % 2 + 1) * 512],
                    r2[:, kt * 512:(kt + 1) * 512], rsb[:, 0:512], ALU.mult),
                    deps=a2d, event=f"ap2_{kt}@{c}")
                emit(eng, lambda e, kt=kt: e.tensor_tensor(
                    outt[:, (kt % 2) * 512:(kt % 2 + 1) * 512],
                    outt[:, (kt % 2) * 512:(kt % 2 + 1) * 512],
                    rsb[:, 512:1024], ALU.subtract),
                    deps=[f"ap2_{kt}@{c}", f"M2sb@{c}"], event=f"o_{kt}@{c}")
                dma("y", OS[kt % 2], out_d[c, kt],
                    outt[:, (kt % 2) * 512:(kt % 2 + 1) * 512],
                    deps=[f"o_{kt}@{c}"], event=f"os_{kt}@{c}")

        def final(e):
            for s in ALL_DMA_SEMS:
                e.wait_ge(s.h, s.n)
        plan["y"].append(final)

        block = es.enter_context(nc.Block())

        @block.tensor
        def _(t):
            for f in plan["t"]:
                f(t)

        @block.vector
        def _(v):
            for f in plan["v"]:
                f(v)

        @block.scalar
        def _(s):
            for f in plan["s"]:
                f(s)

        @block.gpsimd
        def _(g):
            for f in plan["g"]:
                f(g)

        @block.sync
        def _(y):
            for f in plan["y"]:
                f(y)

    return nc


def layout_inputs(x, Wq, Wo, bank_weights, memories, memory_norms, W_gate,
                  Wg, Wu, Wd, n_cores, tokens_per_core, n_chunks):
    asc = np.ascontiguousarray
    f32 = np.float32
    x = np.asarray(x, f32).reshape(-1, D)
    shared = {
        "wqt": asc(np.asarray(Wq, f32).reshape(KT, 128, KT, 128)
                   .transpose(0, 3, 2, 1)),
        "wot": asc(np.asarray(Wo, f32).reshape(KT, 128, KT, 128)
                   .transpose(0, 3, 2, 1)),
        "wg": asc(np.asarray(Wg, f32).reshape(NE, KT, 128, IT, 128)
                  .transpose(0, 3, 2, 1, 4)),
        "wu": asc(np.asarray(Wu, f32).reshape(NE, KT, 128, IT, 128)
                  .transpose(0, 3, 2, 1, 4)),
        "wd": asc(np.asarray(Wd, f32).reshape(NE, IT, 128, KT, 128)
                  .transpose(3, 2, 0, 1, 4)),
        "wgt": asc(np.asarray(W_gate, f32).T.reshape(KT, 128, 8)
                   .transpose(1, 0, 2).reshape(128, KT * 8)),
        "mem": asc(np.asarray(memories, f32).transpose(2, 0, 1, 3)
                   .reshape(128, NB * H * 128)),
        "norms": asc(np.asarray(memory_norms, f32).transpose(2, 1, 0)
                     .reshape(128, H * NB)),
        "bw": asc(np.asarray(bank_weights, f32)),
        "ident": np.eye(128, dtype=f32),
    }
    in_maps = []
    for i in range(n_cores):
        xc = x[i * tokens_per_core:(i + 1) * tokens_per_core]
        xh = asc(xc.T.reshape(KT, 128, n_chunks, C).transpose(2, 1, 0, 3))
        m = dict(shared)
        m["xt"] = xh
        in_maps.append(m)
    return in_maps


def kernel(L0_attn_hidden_states, L0_attn_Wq, L0_attn_Wk, L0_attn_Wv,
           L0_attn_Wo, L0_attn_bank_weights, L0_attn_memories,
           L0_attn_memory_norms, L0_ffn_W_gate, L0_ffn_Wg, L0_ffn_Wu,
           L0_ffn_Wd):
    n_cores = 8
    total_tokens = int(np.prod(L0_attn_hidden_states.shape[:-1]))
    tokens_per_core = total_tokens // n_cores
    n_chunks = tokens_per_core // C
    in_maps = layout_inputs(L0_attn_hidden_states, L0_attn_Wq, L0_attn_Wo,
                            L0_attn_bank_weights, L0_attn_memories,
                            L0_attn_memory_norms, L0_ffn_W_gate, L0_ffn_Wg,
                            L0_ffn_Wu, L0_ffn_Wd, n_cores, tokens_per_core,
                            n_chunks)
    nc = build_program(n_chunks)
    res = run_bass_kernel_spmd(nc, in_maps, core_ids=list(range(n_cores)))
    parts = []
    for i in range(n_cores):
        arr = np.asarray(res.results[i]["out"])
        parts.append(arr.transpose(0, 3, 1, 2).reshape(tokens_per_core, D))
    full = np.concatenate(parts, axis=0)
    return np.ascontiguousarray(
        full.reshape(L0_attn_hidden_states.shape).astype(np.float32))


# version 07 (final)
# speedup vs baseline: 1.2187x; 1.2187x over previous
import os as _os
import numpy as np
import concourse.bass as bass
import concourse.mybir as mybir
from concourse.bass_utils import run_bass_kernel_spmd

F32 = mybir.dt.float32
BF16 = mybir.dt.bfloat16
AF = mybir.ActivationFunctionType
ALU = mybir.AluOpType
AX = mybir.AxisListType

D = 2048
H = 16
NB = 4
FF = 768
NE = 8
KT = D // 128
IT = FF // 128
C = 512
NT = C // 128
EPS = 1e-6
LNEPS = 1e-5
LN_INV = 1.0 / D


class Sem:
    def __init__(self, handle, step):
        self.h = handle
        self.step = step
        self.n = 0


def build_program(n_chunks):
    nc = bass.Bass("TRN2", target_bir_lowering=False,
                   detect_race_conditions=bool(_os.environ.get("K_RACE")))

    xt = nc.dram_tensor("xt", [n_chunks, 128, KT, C], F32, kind="ExternalInput").ap()
    wqt = nc.dram_tensor("wqt", [KT, 128, KT, 128], F32, kind="ExternalInput").ap()
    wot = nc.dram_tensor("wot", [KT, 128, KT, 128], F32, kind="ExternalInput").ap()
    wg_i = nc.dram_tensor("wg", [NE, IT, 128, KT, 128], F32, kind="ExternalInput").ap()
    wu_i = nc.dram_tensor("wu", [NE, IT, 128, KT, 128], F32, kind="ExternalInput").ap()
    wd_i = nc.dram_tensor("wd", [KT, 128, NE, IT, 128], F32, kind="ExternalInput").ap()
    wgt = nc.dram_tensor("wgt", [128, KT * 8], F32, kind="ExternalInput").ap()
    mem_i = nc.dram_tensor("mem", [128, NB * H * 128], F32, kind="ExternalInput").ap()
    norms_i = nc.dram_tensor("norms", [128, H * NB], F32, kind="ExternalInput").ap()
    bw_i = nc.dram_tensor("bw", [H, NB], F32, kind="ExternalInput").ap()
    ident_i = nc.dram_tensor("ident", [128, 128], F32, kind="ExternalInput").ap()
    out_d = nc.dram_tensor("out", [n_chunks, KT, 128, C], F32, kind="ExternalOutput").ap()

    wq_s = nc.dram_tensor("wq_s", [KT, 128, KT, 128], BF16, kind="Internal").ap()
    wo_s = nc.dram_tensor("wo_s", [KT, 128, KT, 128], BF16, kind="Internal").ap()
    wg_s = nc.dram_tensor("wg_s", [NE, IT, 128, KT, 128], BF16, kind="Internal").ap()
    wu_s = nc.dram_tensor("wu_s", [NE, IT, 128, KT, 128], BF16, kind="Internal").ap()
    wd_s = nc.dram_tensor("wd_s", [KT, 128, NE, IT, 128], BF16, kind="Internal").ap()

    from contextlib import ExitStack
    es = ExitStack()
    with es:
        sb = lambda name, shape, dt: es.enter_context(nc.sbuf_tensor(name, shape, dt))
        pool = sb("pool", [128, 24576], BF16)
        xbuf = sb("xbuf", [128, 8192], BF16)
        x1 = sb("x1", [128, KT * C], BF16)
        r2 = sb("r2", [128, KT * C], BF16)
        sigma = sb("sigma", [128, 2 * C], BF16)
        sq = sb("sq", [128, 2 * C], BF16)
        rsb = sb("rsb", [128, 2 * C], F32)
        outt = sb("outt", [128, 2 * C], F32)
        wqb = sb("wqb", [128, 2 * KT * 128], BF16)
        wob = sb("wob", [128, 2 * KT * 128], BF16)
        wgb = sb("wgb", [128, 2 * KT * 128], BF16)
        wub = sb("wub", [128, 2 * KT * 128], BF16)
        wdb = sb("wdb", [128, 2 * NE * IT * 128], BF16)
        memb = sb("memb", [128, NB * H * 128], BF16)
        normb = sb("normb", [128, H * NB], BF16)
        wgtb = sb("wgtb", [128, KT * 8], BF16)
        identb = sb("identb", [128, 128], BF16)
        identf = sb("identf", [128, 128], F32)
        onesc = sb("onesc", [128, 1], BF16)
        onescf = sb("onescf", [128, 1], F32)
        onesrf = sb("onesrf", [1, 128], F32)
        onesr = sb("onesr", [1, 128], BF16)
        normf = sb("normf", [128, H * NB], F32)
        bwf = sb("bwf", [H, NB], F32)
        bsc1 = sb("bsc1", [H, 3 * NB], F32)
        sumr = sb("sumr", [1, H * NB], F32)
        snr = sb("snr", [1, 2 * NB], F32)
        wflat = sb("wflat", [1, H * NB], F32)
        weff = sb("weff", [H, NB], F32)
        wbc = sb("wbc", [128, H * NB], F32)
        call = sb("call", [128, 2 * 16], F32)
        bsc = sb("bsc", [128, 2 * 512], BF16)
        pps = sb("pps", [128, 2 * 256], BF16)
        rsbb = sb("rsbb", [128, 512], BF16)
        apb = sb("apb", [128, 1024], BF16)
        ct = sb("ct", [128, 2 * NT * 128], BF16)
        rowsb = sb("rowsb", [1, 2 * C], F32)
        rows2 = sb("rows2", [1, 2 * C], F32)
        logitsb = sb("logitsb", [8, C], F32)
        rt = sb("rt", [128, NT * 16], F32)
        crow = sb("crow", [1, 2 * C], F32)

        psum = [es.enter_context(nc.psum_tensor(f"P{i}", [128, 512], F32))
                for i in range(8)]

        def mksem(name, step=16):
            return Sem(es.enter_context(nc.semaphore(name)), step)

        SSET = mksem("sset")
        BWS = mksem("bws")
        WFS = mksem("wfs")
        CVA = mksem("cva")
        CVB = mksem("cvb")
        GSET = mksem("gset")
        XS = mksem("xs")
        WQ = [mksem("wq0"), mksem("wq1")]
        WO = [mksem("wo0"), mksem("wo1")]
        WG = [mksem("wgs0"), mksem("wgs1")]
        WU = [mksem("wus0"), mksem("wus1")]
        WD = [mksem("wds0"), mksem("wds1")]
        CR = [mksem("cr0"), mksem("cr1")]
        OS = [mksem("os0"), mksem("os1")]
        PE = mksem("pe", 1)
        DV = mksem("dv", 1)
        AC = mksem("ac", 1)
        PL = mksem("pl", 1)
        TOTAL_SEMS = (SSET, BWS, WFS, CVA, CVB, GSET)
        ALL_DMA_SEMS = [SSET, BWS, WFS, CVA, CVB, GSET, XS] + WQ + WO + WG + WU + WD + CR + OS

        plan = {"t": [], "v": [], "s": [], "g": [], "y": []}
        SEM_OF = {"t": PE, "v": DV, "s": AC, "p": PL}
        PLAN_OF = {"t": "t", "v": "v", "s": "s", "p": "g"}
        QUEUE_PLAN = {"y": "y", "g": "g", "sd": "s"}
        EV = {}
        waited = {k: {} for k in plan}

        def _emit(ep, sem, build_fn, deps, event):
            cap = []
            hw = waited[ep]
            for d in deps:
                s, v = EV[d]
                if s in TOTAL_SEMS:
                    v = s.n
                key = id(s)
                if hw.get(key, -1) >= v:
                    continue
                hw[key] = v
                cap.append((s.h, v))
            if event is not None:
                sem.n += sem.step
                EV[event] = (sem, sem.n)

                def closure(e, cap=cap, build_fn=build_fn, sh=sem.h, st=sem.step):
                    for (h, v) in cap:
                        e.wait_ge(h, v)
                    build_fn(e).then_inc(sh, st)
            else:
                def closure(e, cap=cap, build_fn=build_fn):
                    for (h, v) in cap:
                        e.wait_ge(h, v)
                    build_fn(e)
            plan[ep].append(closure)

        def emit(eng, build_fn, deps=(), event=None):
            _emit(PLAN_OF[eng], SEM_OF[eng], build_fn, deps, event)

        def dma(q, sem, out_ap, in_ap, deps=(), event=None):
            _emit(QUEUE_PLAN[q], sem,
                  lambda e, o=out_ap, i=in_ap: e.dma_start(out=o, in_=i),
                  deps, event)

        dma("y", SSET, normf[:], norms_i, event="ld_normf")
        dma("y", BWS, bwf[:], bw_i, event="ld_bw")
        dma("y", SSET, identf[:], ident_i, event="ld_identf")
        dma("g", GSET, identb[:], ident_i, event="ld_identb")
        dma("g", GSET, wgtb[:], wgt, event="ld_wgt")
        dma("g", GSET, normb[:], norms_i, event="ld_normb")
        mem_stage = pool[:, 0:16384].bitcast(F32)
        dma("y", SSET, mem_stage, mem_i, event="ld_mem")

        emit("v", lambda e: e.memset(onesc[:], 1.0), event="ones1")
        emit("v", lambda e: e.memset(onesr[:], 1.0), event="ones2")
        emit("v", lambda e: e.memset(onesrf[:], 1.0), event="ones3")
        emit("v", lambda e: e.memset(onescf[:], 1.0), event="ones4")

        emit("v", lambda e: e.reduce_max(bsc1[:, 0:1], bwf[:], axis=AX.X),
             deps=["ld_bw"], event="bs_m")
        emit("v", lambda e: e.tensor_scalar(bsc1[:, NB:2 * NB], bwf[:],
                                            bsc1[:, 0:1], None, ALU.subtract),
             deps=["bs_m"], event="bs_d")
        emit("s", lambda e: e.activation(bsc1[:, NB:2 * NB], bsc1[:, NB:2 * NB],
                                         AF.Exp),
             deps=["bs_d"], event="bs_e")
        emit("v", lambda e: e.reduce_sum(bsc1[:, 1:2], bsc1[:, NB:2 * NB],
                                         axis=AX.X),
             deps=["bs_e"], event="bs_s")
        emit("v", lambda e: e.reciprocal(bsc1[:, 2:3], bsc1[:, 1:2]),
             deps=["bs_s"], event="bs_r")
        emit("v", lambda e: e.tensor_scalar(bsc1[:, 2 * NB:3 * NB],
                                            bsc1[:, NB:2 * NB], bsc1[:, 2:3],
                                            None, ALU.mult),
             deps=["bs_r"], event="bs_sm")
        emit("t", lambda t: t.matmul(psum[0][0:1, 0:H * NB], onescf[:], normf[:],
                                     start=True, stop=True),
             deps=["ld_normf", "ones4"], event="act_cs")
        emit("v", lambda e: e.tensor_copy(sumr[:], psum[0][0:1, 0:H * NB]),
             deps=["act_cs"], event="act_cp")
        sum_v = sumr[0:1, :].rearrange("p (h n) -> p n h", h=H, n=NB)
        emit("v", lambda e: e.tensor_reduce(snr[0:1, 0:NB], sum_v, axis=AX.X,
                                            op=ALU.add),
             deps=["act_cp"], event="act_sn")
        emit("v", lambda e: e.tensor_scalar(snr[0:1, NB:2 * NB], snr[0:1, 0:NB],
                                            float(EPS), None, ALU.is_ge),
             deps=["act_sn"], event="act_a")
        emit("t", lambda t: t.matmul(psum[1][0:H, 0:NB], onesrf[0:1, 0:H],
                                     snr[0:1, NB:2 * NB], start=True, stop=True),
             deps=["act_a", "ones3"], event="act_bc")
        emit("v", lambda e: e.tensor_tensor(weff[:], bsc1[:, 2 * NB:3 * NB],
                                            psum[1][0:H, 0:NB], ALU.mult),
             deps=["bs_sm", "act_bc"], event="weff_ok")
        for h_ in range(H):
            dma("y", WFS, wflat[0:1, h_ * NB:(h_ + 1) * NB], weff[h_:h_ + 1, :],
                deps=["weff_ok"], event=f"wflat{h_}")
        EV["wflat_ok"] = EV[f"wflat{H-1}"]
        emit("t", lambda t: t.matmul(psum[2][:, 0:H * NB], onesrf[:],
                                     wflat[0:1, :], start=True, stop=True),
             deps=["wflat_ok", "ones3"], event="wbc_mm")
        emit("v", lambda e: e.tensor_copy(wbc[:], psum[2][:, 0:H * NB]),
             deps=["wbc_mm"], event="wbc_ok")
        for n_ in range(NB):
            for h_ in range(H):
                blk = (h_ * NB + n_) * 128
                col = h_ * NB + n_
                emit("v", lambda e, blk=blk, col=col: e.tensor_scalar(
                    memb[:, blk:blk + 128], mem_stage[:, blk:blk + 128],
                    wbc[:, col:col + 1], None, ALU.mult),
                    deps=["wbc_ok", "ld_mem"],
                    event=("memb_ok" if (n_ == NB - 1 and h_ == H - 1) else None))

        dma("g", XS, xbuf[:, :], xt[0].rearrange("p k t -> p (k t)"),
            deps=[], event="xld@0")

        dma("g", CVA, wq_s, wqt, event="cv_wq")
        dma("g", CVA, wo_s, wot, event="cv_wo")
        EV["cvA"] = EV["cv_wo"]
        for e_ in range(NE):
            dma("g", CVB, wg_s[e_], wg_i[e_], event=f"cv_wg{e_}")
            dma("g", CVB, wu_s[e_], wu_i[e_], event=f"cv_wu{e_}")
        for o_ in range(KT):
            dma("g", CVB, wd_s[o_], wd_i[o_], event=f"cv_wd{o_}")
        EV["cvB"] = EV[f"cv_wd{KT-1}"]

        x_bf = xbuf[:, :]
        comb = pool[:, 0:8192]
        r1 = pool[:, 8192:16384]
        pend_bc2 = []

        for c in range(n_chunks):
            P = c > 0


            def load_wq(h_):
                par = h_ % 2
                deps = [] if P else ["cvA"]
                if P or h_ >= 2:
                    pc, ph = (c, h_ - 2) if h_ >= 2 else (c - 1, h_ + 14)
                    deps.append(f"q{ph}@{pc}")
                dma("sd", WQ[par], wqb[:, par * 2048:(par + 1) * 2048],
                    wq_s[h_].rearrange("p k o -> p (k o)"),
                    deps=deps, event=f"wqld{h_}@{c}")

            def load_wo(o_):
                par = o_ % 2
                deps = [] if P else ["cvA"]
                if P or o_ >= 2:
                    pc, po = (c, o_ - 2) if o_ >= 2 else (c - 1, o_ + 14)
                    deps.append(f"attn{po}@{pc}")
                dma("sd", WO[par], wob[:, par * 2048:(par + 1) * 2048],
                    wo_s[o_].rearrange("p k o -> p (k o)"),
                    deps=deps, event=f"wold{o_}@{c}")

            def load_wgu(pidx):
                e_, it = divmod(pidx, IT)
                par = pidx % 2
                deps, deps2 = [], []
                if not P:
                    deps.append("cvB")
                    deps2.append("cvB")
                if P or pidx >= 2:
                    qidx, pc = pidx - 2, c
                    if qidx < 0:
                        qidx, pc = qidx + NE * IT, c - 1
                    pe_, pit = divmod(qidx, IT)
                    deps.append(f"g{pe_}_{pit}@{pc}")
                    deps2.append(f"u{pe_}_{pit}@{pc}")
                dma("sd", WG[par], wgb[:, par * 2048:(par + 1) * 2048],
                    wg_s[e_, it].rearrange("p k o -> p (k o)"),
                    deps=deps, event=f"wgld{e_}_{it}@{c}")
                dma("sd", WU[par], wub[:, par * 2048:(par + 1) * 2048],
                    wu_s[e_, it].rearrange("p k o -> p (k o)"),
                    deps=deps2, event=f"wuld{e_}_{it}@{c}")

            def load_wd(o_):
                par = o_ % 2
                deps = [] if P else ["cvB"]
                if P or o_ >= 2:
                    pc, po = (c, o_ - 2) if o_ >= 2 else (c - 1, o_ + 14)
                    deps.append(f"y{po}@{pc}")
                dma("y", WD[par], wdb[:, par * 6144:(par + 1) * 6144],
                    wd_s[o_].rearrange("p e i o -> p (e i o)"),
                    deps=deps, event=f"wdld{o_}@{c}")

            load_wq(0)
            load_wq(1)

            def q_head(h_):
                par = h_ % 2
                deps = [f"xld@{c}", f"wqld{h_}@{c}"]
                if P or h_ >= 2:
                    pc, ph = (c, h_ - 2) if h_ >= 2 else (c - 1, h_ + 14)
                    deps.append(f"sigrd{ph}@{pc}")
                    deps.append(f"smin{ph}@{pc}")
                if P and h_ < 2:
                    deps.append(f"sg{NE-1}_{IT-3}@{c-1}" if h_ == 0
                                else f"up{NE-1}_{IT-3}@{c-1}")
                if not P and h_ == 0:
                    deps.append("act_cp")
                if not P and h_ == 1:
                    deps.append("weff_ok")

                def fn(t, par=par):
                    mm = None
                    for kt in range(KT):
                        mm = t.matmul(
                            psum[par][:, 0:512],
                            wqb[:, par * 2048 + kt * 128:par * 2048 + (kt + 1) * 128],
                            x_bf[:, kt * 512:(kt + 1) * 512],
                            start=(kt == 0), stop=(kt == KT - 1))
                    return mm
                emit("t", fn, deps=deps, event=f"q{h_}@{c}")

            def sigma_head(h_):
                par = h_ % 2
                deps = [f"q{h_}@{c}"]
                if h_ > 0:
                    deps.append(f"sig{h_-1}@{c}")
                elif P:
                    deps += [f"st4_{KT-1}@{c-1}", f"sig{H-1}@{c-1}"]
                emit("v", lambda e, par=par: e.tensor_scalar(
                    sq[:, 0:512], psum[par][:, 0:512], 0.0, None, ALU.min),
                    deps=deps, event=f"smin{h_}@{c}")
                emit("s", lambda e: e.activation(sq[:, 0:512], sq[:, 0:512],
                                                 AF.Exp),
                     deps=[f"smin{h_}@{c}"], event=f"sexp{h_}@{c}")
                rdeps = [f"q{h_}@{c}", f"smin{h_}@{c}"]
                if h_ > 0:
                    rdeps.append(f"sig{h_-1}@{c}")
                elif P:
                    rdeps.append(f"sig{H-1}@{c-1}")
                emit("s", lambda e, par=par: e.activation(
                    sq[:, 512:1024], psum[par][:, 0:512], AF.Relu),
                    deps=rdeps, event=f"srelu{h_}@{c}")
                sdeps = [f"sexp{h_}@{c}", f"srelu{h_}@{c}"]
                if P or h_ >= 2:
                    pc, ph = (c, h_ - 2) if h_ >= 2 else (c - 1, h_ + 14)
                    sdeps.append(f"bank{ph}@{pc}")
                emit("v", lambda e, par=par: e.tensor_tensor(
                    sigma[:, par * 512:(par + 1) * 512], sq[:, 0:512],
                    sq[:, 512:1024], ALU.add),
                    deps=sdeps, event=f"sig{h_}@{c}")
                EV[f"sigrd{h_}@{c}"] = EV[f"srelu{h_}@{c}"]

            def bank_head(h_):
                par = h_ % 2
                deps = [f"sig{h_}@{c}"]
                if P or h_ >= 1:
                    pc, ph = (c, h_ - 1) if h_ >= 1 else (c - 1, H - 1)
                    for t_ in range(NT):
                        deps.append(f"scl{ph}_{t_}@{pc}")
                if P or h_ >= 2:
                    pc2, ph2 = (c, h_ - 2) if h_ >= 2 else (c - 1, h_ + 14)
                    deps.append(f"cp2_{ph2}@{pc2}")
                    deps += [f"evac{ph2}_{t_}@{pc2}" for t_ in range(NT)]
                if h_ <= 1:
                    if P:
                        deps += [f"l2@{c-1}_mean", f"l2@{c-1}_msq",
                                 f"R2sb@{c-1}", f"M2sb@{c-1}"]
                        if h_ == 0:
                            deps += [f"sg{NE-1}_{IT-1}@{c-1}",
                                     f"up{NE-1}_{IT-1}@{c-1}",
                                     f"sg{NE-1}_{IT-2}@{c-1}",
                                     f"up{NE-1}_{IT-2}@{c-1}",
                                     f"r2_{KT-2}@{c-1}", f"r2_{KT-1}@{c-1}"]
                    else:
                        if h_ == 0:
                            deps += ["wbc_ok", "memb_ok", "ld_normb"]

                def fn(t, h_=h_, par=par):
                    mm = None
                    for t_ in range(NT):
                        mm = t.matmul(
                            psum[2 + t_][:, 0:512],
                            sigma[:, par * 512 + t_ * 128:par * 512 + (t_ + 1) * 128],
                            memb[:, h_ * 512:(h_ + 1) * 512],
                            start=True, stop=True)
                        mm = t.matmul(
                            psum[6 + par][:, t_ * 4:(t_ + 1) * 4],
                            sigma[:, par * 512 + t_ * 128:par * 512 + (t_ + 1) * 128],
                            normb[:, h_ * NB:(h_ + 1) * NB],
                            start=True, stop=True)
                    return mm
                emit("t", fn, deps=deps, event=f"bank{h_}@{c}")

                cdeps = [f"bank{h_}@{c}"]
                if h_ >= 2:
                    cdeps.append(f"scl{h_-2}_{NT-1}@{c}")
                elif P:
                    cdeps.append(f"scl{H-2+h_}_{NT-1}@{c-1}")
                emit("v", lambda e, par=par: e.tensor_scalar(
                    call[:, par * 16:(par + 1) * 16], psum[6 + par][:, 0:16],
                    float(EPS), None, ALU.max),
                    deps=cdeps, event=f"cprep{h_}@{c}")
                emit("v", lambda e, par=par: e.reciprocal(
                    call[:, par * 16:(par + 1) * 16],
                    call[:, par * 16:(par + 1) * 16]),
                    deps=[f"cprep{h_}@{c}"], event=f"cp2_{h_}@{c}")

                for t_ in range(NT):
                    sp = par * 512
                    cbv = call[:, par * 16 + t_ * 4:par * 16 + (t_ + 1) * 4]
                    Av = psum[2 + t_][:, 0:512].rearrange(
                        "p (n e) -> p n e", n=NB, e=128)
                    Sv = bsc[:, sp:sp + 512].rearrange(
                        "p (n e) -> p n e", n=NB, e=128)
                    sd = [f"cp2_{h_}@{c}"]
                    if t_ > 0:
                        sd.append(f"pa{h_}_{t_-1}@{c}")
                    elif h_ >= 2:
                        sd.append(f"pa{h_-2}_{NT-1}@{c}")
                    elif P:
                        sd.append(f"pa{H-2+h_}_{NT-1}@{c-1}")
                    emit("v", lambda e, Av=Av, Sv=Sv, cbv=cbv: e.tensor_tensor(
                        Sv, Av, cbv.unsqueeze(2).broadcast_to([128, NB, 128]),
                        ALU.mult),
                        deps=sd, event=f"scl{h_}_{t_}@{c}")
                    ppo = par * 256
                    pad = [f"scl{h_}_{t_}@{c}"]
                    if t_ > 0:
                        pad.append(f"comb{h_}_{t_-1}@{c}")
                    elif h_ >= 2:
                        pad.append(f"comb{h_-2}_{NT-1}@{c}")
                    elif P:
                        pad.append(f"comb{H-2+h_}_{NT-1}@{c-1}")
                    emit("p", lambda e, sp=sp, ppo=ppo: e.tensor_tensor(
                        pps[:, ppo:ppo + 256].rearrange("p (n e) -> p n e",
                                                        n=2, e=128),
                        bsc[:, sp:sp + 256].rearrange("p (n e) -> p n e",
                                                      n=2, e=128),
                        bsc[:, sp + 256:sp + 512].rearrange(
                            "p (n e) -> p n e", n=2, e=128), ALU.add),
                        deps=pad, event=f"pa{h_}_{t_}@{c}")
                    cto = par * 512 + t_ * 128
                    combd = [f"pa{h_}_{t_}@{c}"]
                    if h_ >= 2:
                        combd.append(f"tr{h_-2}_{t_}@{c}")
                    elif P:
                        combd.append(f"tr{H-2+h_}_{t_}@{c-1}")
                    emit("p", lambda e, cto=cto, ppo=ppo: e.tensor_tensor(
                        ct[:, cto:cto + 128], pps[:, ppo:ppo + 128],
                        pps[:, ppo + 128:ppo + 256], ALU.add),
                        deps=combd, event=f"comb{h_}_{t_}@{c}")

            def bank_trs(h_):
                par = h_ % 2
                pv = psum[6 + par].ap().bitcast(BF16)
                for t_ in range(NT):
                    trd = [f"comb{h_}_{t_}@{c}", "ld_identb", f"cp2_{h_}@{c}"]
                    if h_ >= 2:
                        trd.append(f"evac{h_-2}_{t_}@{c}")
                    elif P:
                        trd.append(f"evac{h_+14}_{t_}@{c-1}")
                    emit("t", lambda t, t_=t_, par=par, pv=pv: t.transpose(
                        pv[:, 512 + t_ * 128:512 + (t_ + 1) * 128],
                        ct[:, par * 512 + t_ * 128:par * 512 + (t_ + 1) * 128],
                        identb[:]),
                        deps=trd, event=f"tr{h_}_{t_}@{c}")
                for t_ in range(NT):
                    emit("s", lambda e, t_=t_, h_=h_, pv=pv: e.activation(
                        comb[:, h_ * 512 + t_ * 128:h_ * 512 + (t_ + 1) * 128],
                        pv[:, 512 + t_ * 128:512 + (t_ + 1) * 128], AF.Copy),
                        deps=[f"tr{h_}_{NT-1}@{c}"], event=f"evac{h_}_{t_}@{c}")

            q_head(0)
            load_wq(2)
            sigma_head(0)
            q_head(1)
            load_wq(3)
            sigma_head(1)
            tail_pair = None
            if pend_bc2:
                th_, tail_pair = pend_bc2.pop(0)
                th_()
            for h_ in range(H):
                if tail_pair is not None and h_ < KT:
                    tail_pair(h_)
                bank_head(h_)
                if h_ + 2 < H:
                    q_head(h_ + 2)
                if h_ + 4 < H:
                    load_wq(h_ + 4)
                if h_ == 0:
                    load_wo(0)
                    load_wo(1)
                if h_ + 2 < H:
                    sigma_head(h_ + 2)
                if h_ >= 1:
                    bank_trs(h_ - 1)
            bank_trs(H - 1)

            comb_done = [f"evac{H-1}_{t_}@{c}" for t_ in range(NT)]
            for o_ in range(KT):
                par = o_ % 2
                deps = comb_done + [f"wold{o_}@{c}"]
                if o_ < 2:
                    deps.append(f"sigrd{14 + o_}@{c}")
                else:
                    deps.append(f"r1_{o_-2}@{c}")

                def fn(t, par=par):
                    mm = None
                    for kt in range(KT):
                        mm = t.matmul(
                            psum[par][:, 0:512],
                            wob[:, par * 2048 + kt * 128:par * 2048 + (kt + 1) * 128],
                            comb[:, kt * 512:(kt + 1) * 512],
                            start=(kt == 0), stop=(kt == KT - 1))
                    return mm
                emit("t", fn, deps=deps, event=f"attn{o_}@{c}")
                if o_ + 2 < KT:
                    load_wo(o_ + 2)
                emit("v", lambda e, o_=o_, par=par: e.tensor_tensor(
                    r1[:, o_ * 512:(o_ + 1) * 512],
                    x_bf[:, o_ * 512:(o_ + 1) * 512],
                    psum[par][:, 0:512], ALU.add),
                    deps=[f"attn{o_}@{c}"], event=f"r1_{o_}@{c}")
                emit("v", lambda e, o_=o_: e.tensor_tensor(
                    sq[:, (o_ % 2) * 512:(o_ % 2 + 1) * 512],
                    r1[:, o_ * 512:(o_ + 1) * 512],
                    r1[:, o_ * 512:(o_ + 1) * 512], ALU.mult),
                    deps=[f"r1_{o_}@{c}"] +
                         ([f"st2_{o_-2}@{c}"] if o_ >= 2 else []),
                    event=f"sq_{o_}@{c}")
                def emit_st12(o_):
                    sdeps = [f"r1_{o_}@{c}"]
                    if o_ == 0:
                        sdeps += [f"scl{H-1}_0@{c}"]
                    emit("t", lambda t, o_=o_: t.matmul(
                        psum[2][0:1, 0:512], onesc[:],
                        r1[:, o_ * 512:(o_ + 1) * 512],
                        start=(o_ == 0), stop=(o_ == KT - 1)),
                        deps=sdeps, event=f"st1_{o_}@{c}")
                    s2deps = [f"sq_{o_}@{c}"]
                    if o_ == 0:
                        s2deps += [f"scl{H-1}_1@{c}"]
                    emit("t", lambda t, o_=o_: t.matmul(
                        psum[3][0:1, 0:512], onesc[:],
                        sq[:, (o_ % 2) * 512:(o_ % 2 + 1) * 512],
                        start=(o_ == 0), stop=(o_ == KT - 1)),
                        deps=s2deps, event=f"st2_{o_}@{c}")
                if o_ >= 1:
                    emit_st12(o_ - 1)
                if o_ == KT - 1:
                    emit_st12(o_)

            def ln_rows(tag, sum_bank, sq_bank, dep_sum, dep_sq, war_deps):
                emit("v", lambda e: e.tensor_scalar(
                    rows2[0:1, 0:512], psum[sum_bank][0:1, 0:512], LN_INV, None,
                    ALU.mult), deps=[dep_sum], event=f"{tag}_mean")
                emit("v", lambda e: e.tensor_scalar(
                    rows2[0:1, 512:1024], psum[sq_bank][0:1, 0:512], LN_INV,
                    None, ALU.mult), deps=[dep_sq], event=f"{tag}_msq")
                emit("v", lambda e: e.tensor_tensor(
                    rowsb[0:1, 0:512], rows2[0:1, 0:512], rows2[0:1, 0:512],
                    ALU.mult), deps=[f"{tag}_mean"] + war_deps,
                    event=f"{tag}_m2")
                emit("v", lambda e: e.tensor_tensor(
                    rowsb[0:1, 0:512], rows2[0:1, 512:1024], rowsb[0:1, 0:512],
                    ALU.subtract), deps=[f"{tag}_m2", f"{tag}_msq"],
                    event=f"{tag}_var")
                emit("v", lambda e: e.tensor_scalar(
                    rowsb[0:1, 0:512], rowsb[0:1, 0:512], LNEPS, None, ALU.add),
                    deps=[f"{tag}_var"], event=f"{tag}_vare")
                emit("s", lambda e: e.activation(
                    rowsb[0:1, 0:512], rowsb[0:1, 0:512], AF.Ln),
                    deps=[f"{tag}_vare"], event=f"{tag}_sd")
                emit("s", lambda e: e.activation(
                    rowsb[0:1, 0:512], rowsb[0:1, 0:512], AF.Exp, scale=-0.5),
                    deps=[f"{tag}_sd"], event=f"{tag}_rstd")
                emit("v", lambda e: e.tensor_tensor(
                    rowsb[0:1, 512:1024], rows2[0:1, 0:512], rowsb[0:1, 0:512],
                    ALU.mult), deps=[f"{tag}_rstd"] + war_deps,
                    event=f"{tag}_mr")

            war1 = [f"bcR2@{c-1}", f"bcM2@{c-1}"] if P else []
            ln_rows(f"l1@{c}", 2, 3, f"st1_{KT-1}@{c}", f"st2_{KT-1}@{c}", war1)
            emit("t", lambda t: t.matmul(psum[4][:, 0:512], onesrf[:],
                                         rowsb[0:1, 0:512], start=True,
                                         stop=True),
                 deps=[f"l1@{c}_rstd", f"scl{H-1}_2@{c}"] +
                      ([f"r2_{KT-2}@{c-1}"] if P else []),
                 event=f"bcR@{c}")
            emit("t", lambda t: t.matmul(psum[5][:, 0:512], onesrf[:],
                                         rowsb[0:1, 512:1024], start=True,
                                         stop=True),
                 deps=[f"l1@{c}_mr", f"scl{H-1}_3@{c}"] +
                      ([f"r2_{KT-1}@{c-1}"] if P else []),
                 event=f"bcM@{c}")
            emit("v", lambda e: e.tensor_copy(rsbb[:, 0:512],
                                              psum[4][:, 0:512]),
                 deps=[f"bcR@{c}"], event=f"Rsb@{c}")
            emit("v", lambda e: e.tensor_copy(rsb[:, 512:1024],
                                              psum[5][:, 0:512]),
                 deps=[f"bcM@{c}"], event=f"Msb@{c}")
            for kt in range(KT):
                eng = "v" if kt % 2 == 0 else "p"
                apd = [f"Rsb@{c}"]
                if kt >= 2:
                    apd.append(f"x1_{kt-2}@{c}")
                elif P:
                    apd.append(f"o_{14 + kt}@{c-1}")
                emit(eng, lambda e, kt=kt: e.tensor_tensor(
                    apb[:, (kt % 2) * 512:(kt % 2 + 1) * 512],
                    r1[:, kt * 512:(kt + 1) * 512], rsbb[:, 0:512], ALU.mult),
                    deps=apd, event=f"ap1_{kt}@{c}")
                x1d = [f"ap1_{kt}@{c}", f"Msb@{c}"]
                if P:
                    x1d += [f"g{NE-1}_{IT-1}@{c-1}", f"u{NE-1}_{IT-1}@{c-1}",
                            f"r2_{kt}@{c-1}", f"lg@{c-1}"]
                emit(eng, lambda e, kt=kt: e.tensor_tensor(
                    x1[:, kt * 512:(kt + 1) * 512],
                    apb[:, (kt % 2) * 512:(kt % 2 + 1) * 512],
                    rsb[:, 512:1024], ALU.subtract),
                    deps=x1d, event=f"x1_{kt}@{c}")

            PDEPTH = 3
            def ffn_mms(idx):
                e_, it = divmod(idx, IT)
                gpar = idx % PDEPTH
                gb, ub = 2 * gpar, 2 * gpar + 1
                woff = (idx % 2) * 2048
                deps = [f"wgld{e_}_{it}@{c}"]
                if idx >= PDEPTH:
                    pe_, pit = divmod(idx - PDEPTH, IT)
                    deps.append(f"hp{pe_}_{pit}@{c}")
                else:
                    deps += [f"ap1_{KT-1}@{c}"]
                    if gpar == 0:
                        deps += [f"r1_{KT-2}@{c}", f"r1_{KT-1}@{c}"]

                ktw = None
                if idx < PDEPTH:
                    ktw = [EV[f"x1_{kt}@{c}"] for kt in range(KT)]

                def fng(t, gb=gb, woff=woff, ktw=ktw):
                    mm = None
                    for kt in range(KT):
                        if ktw is not None:
                            s, v = ktw[kt]
                            t.wait_ge(s.h, v)
                        mm = t.matmul(
                            psum[gb][:, 0:512],
                            wgb[:, woff + kt * 128:woff + (kt + 1) * 128],
                            x1[:, kt * 512:(kt + 1) * 512],
                            start=(kt == 0), stop=(kt == KT - 1))
                    return mm
                emit("t", fng, deps=deps, event=f"g{e_}_{it}@{c}")
                deps_u = [f"wuld{e_}_{it}@{c}"]
                if idx >= PDEPTH:
                    pe_, pit = divmod(idx - PDEPTH, IT)
                    deps_u.append(f"hp{pe_}_{pit}@{c}")

                def fnu(t, ub=ub, woff=woff):
                    mm = None
                    for kt in range(KT):
                        mm = t.matmul(
                            psum[ub][:, 0:512],
                            wub[:, woff + kt * 128:woff + (kt + 1) * 128],
                            x1[:, kt * 512:(kt + 1) * 512],
                            start=(kt == 0), stop=(kt == KT - 1))
                    return mm
                emit("t", fnu, deps=deps_u, event=f"u{e_}_{it}@{c}")
                if idx + 2 < NE * IT:
                    load_wgu(idx + 2)

            def ffn_cons(idx):
                e_, it = divmod(idx, IT)
                gpar = idx % PDEPTH
                gb, ub = 2 * gpar, 2 * gpar + 1
                spar = idx % 2
                sgd = [f"g{e_}_{it}@{c}"]
                if idx >= 2:
                    pe_, pit = divmod(idx - 2, IT)
                    sgd.append(f"hp{pe_}_{pit}@{c}")
                else:
                    sgd.append(f"st2_{14 + idx}@{c}")
                emit("s", lambda e, gb=gb, spar=spar: e.activation(
                    sq[:, spar * 512:(spar + 1) * 512], psum[gb][:, 0:512],
                    AF.Silu),
                    deps=sgd, event=f"sg{e_}_{it}@{c}")
                upd = [f"u{e_}_{it}@{c}", f"Bev{e_}@{c}"]
                if idx >= 1:
                    pe_, pit = divmod(idx - 1, IT)
                    upd.append(f"hp{pe_}_{pit}@{c}")
                emit("v", lambda e, ub=ub, e_=e_: e.tensor_tensor(
                    outt[:, 0:512], psum[ub][:, 0:512],
                    r2[:, e_ * 512:(e_ + 1) * 512], ALU.mult),
                    deps=upd, event=f"up{e_}_{it}@{c}")
                emit("v", lambda e, e_=e_, it=it, spar=spar: e.tensor_tensor(
                    pool[:, e_ * 3072 + it * 512:e_ * 3072 + (it + 1) * 512],
                    sq[:, spar * 512:(spar + 1) * 512], outt[:, 0:512],
                    ALU.mult),
                    deps=[f"sg{e_}_{it}@{c}", f"up{e_}_{it}@{c}"],
                    event=f"hp{e_}_{it}@{c}")


            if c + 1 < n_chunks:
                dma("g", XS, x_bf, xt[c + 1].rearrange("p k t -> p (k t)"),
                    deps=[f"r1_{KT-1}@{c}", f"q{H-1}@{c}"],
                    event=f"xld@{c+1}")

            x1all = [f"x1_{kt}@{c}" for kt in range(KT)]

            lgw = [EV[f"x1_{kt}@{c}"] for kt in range(KT)]

            def fn_logits(t, lgw=lgw):
                mm = None
                for kt in range(KT):
                    s, v = lgw[kt]
                    t.wait_ge(s.h, v)
                    mm = t.matmul(psum[6][0:8, 0:512],
                                  wgtb[:, kt * 8:(kt + 1) * 8],
                                  x1[:, kt * 512:(kt + 1) * 512],
                                  start=(kt == 0), stop=(kt == KT - 1))
                return mm
            emit("t", fn_logits,
                 deps=["ld_wgt", f"cp2_{H-2}@{c}", f"cp2_{H-1}@{c}"] +
                      [f"evac{H-2}_{t_}@{c}" for t_ in range(NT)],
                 event=f"lg@{c}")
            lgsbd = [f"lg@{c}"]
            if P:
                lgsbd.append(f"crow{NE-1}@{c-1}")
            emit("v", lambda e: e.tensor_copy(logitsb[:], psum[6][0:8, 0:512]),
                 deps=lgsbd, event=f"lgsb@{c}")
            for t_ in range(NT):
                emit("t", lambda t, t_=t_: t.transpose(
                    psum[7][:, t_ * 8:(t_ + 1) * 8],
                    logitsb[:, t_ * 128:(t_ + 1) * 128], identf[0:8, 0:8]),
                    deps=[f"lgsb@{c}", "ld_identf", f"cp2_{H-1}@{c}"] +
                         [f"evac{H-1}_{t2_}@{c}" for t2_ in range(NT)],
                    event=f"lgt{t_}@{c}")
            load_wgu(0)
            load_wgu(1)
            ffn_mms(0)
            ffn_mms(1)
            ffn_mms(2)
            Lv = psum[7][:, 0:32].rearrange("p (t e) -> p t e", t=NT, e=8)
            dv_ = outt[:, 0:32].rearrange("p (t e) -> p t e", t=NT, e=8)
            mk_ = rt[:, 32:64].rearrange("p (t e) -> p t e", t=NT, e=8)
            cbv = outt[:, 32:64].rearrange("p (t e) -> p t e", t=NT, e=8)

            def bc4(ap):
                return ap.unsqueeze(2).broadcast_to([128, NT, 8])
            rmd = [f"lgt{NT-1}@{c}", f"x1_{KT-1}@{c}", f"x1_{KT-2}@{c}"]
            if P:
                rmd += [f"ctp{NT-1}@{c-1}", f"os_{KT-1}@{c-1}",
                        f"os_{KT-2}@{c-1}"]
            emit("v", lambda e: e.reduce_max(rt[:, 0:NT], Lv, axis=AX.X),
                 deps=rmd, event=f"rm1@{c}")
            emit("v", lambda e: e.tensor_tensor(dv_, Lv, bc4(rt[:, 0:NT]),
                                                ALU.subtract),
                 deps=[f"rm1@{c}"], event=f"rd@{c}")
            emit("v", lambda e: e.tensor_scalar(mk_, dv_, 0.0, None, ALU.is_ge),
                 deps=[f"rd@{c}"], event=f"rmask@{c}")
            emit("v", lambda e: e.scalar_tensor_tensor(mk_, mk_, -1e30, dv_,
                                                       ALU.mult, ALU.add),
                 deps=[f"rmask@{c}"], event=f"rmsk2@{c}")
            emit("v", lambda e: e.reduce_max(rt[:, NT:2 * NT], mk_, axis=AX.X),
                 deps=[f"rmsk2@{c}"], event=f"rm2@{c}")
            emit("s", lambda e: e.activation(dv_, dv_, AF.Exp),
                 deps=[f"rm2@{c}"], event=f"re1@{c}")
            emit("s", lambda e: e.activation(rt[:, 2 * NT:3 * NT],
                                             rt[:, NT:2 * NT], AF.Exp),
                 deps=[f"rm2@{c}"], event=f"re2@{c}")
            emit("v", lambda e: e.tensor_scalar(rt[:, 3 * NT:4 * NT],
                                                rt[:, 2 * NT:3 * NT], 1.0, None,
                                                ALU.add),
                 deps=[f"re2@{c}"], event=f"rden@{c}")
            emit("v", lambda e: e.reciprocal(rt[:, 4 * NT:5 * NT],
                                             rt[:, 3 * NT:4 * NT]),
                 deps=[f"rden@{c}"], event=f"rrec@{c}")
            emit("v", lambda e: e.tensor_tensor(mk_, dv_,
                                                bc4(rt[:, 2 * NT:3 * NT]),
                                                ALU.is_ge),
                 deps=[f"re1@{c}", f"re2@{c}"], event=f"rtm@{c}")
            emit("v", lambda e: e.tensor_tensor(dv_, dv_, mk_, ALU.mult),
                 deps=[f"rtm@{c}"], event=f"rcm1@{c}")
            cbd = [f"rcm1@{c}", f"rrec@{c}"]
            if P:
                cbd.append(f"ctp{NT-1}@{c-1}")
            emit("v", lambda e: e.tensor_tensor(cbv, dv_,
                                                bc4(rt[:, 4 * NT:5 * NT]),
                                                ALU.mult),
                 deps=cbd, event=f"rcb@{c}")
            for t_ in range(NT):
                emit("t", lambda t, t_=t_: t.transpose(
                    psum[6][0:8, t_ * 128:(t_ + 1) * 128],
                    outt[:, 32 + t_ * 8:32 + (t_ + 1) * 8], identf[:]),
                    deps=[f"rcb@{c}", f"lgsb@{c}"], event=f"ctp{t_}@{c}")
            emit("v", lambda e: e.tensor_copy(logitsb[:], psum[6][0:8, 0:512]),
                 deps=[f"ctp{t_}@{c}" for t_ in range(NT)], event=f"cmrows@{c}")
            for e_ in range(NE):
                par = e_ % 2
                cdp = [f"cmrows@{c}"]
                if e_ >= 2:
                    cdp.append(f"Bmm{e_-2}@{c}")
                dma("y", CR[par], crow[0:1, par * 512:(par + 1) * 512],
                    logitsb[e_:e_ + 1, :], deps=cdp, event=f"crow{e_}@{c}")
                bmd = [f"crow{e_}@{c}", "ones3"]
                if e_ >= 2:
                    bmd.append(f"Bev{e_-2}@{c}")
                else:
                    bmd += [f"rm2@{c}", f"rcb@{c}"]
                    bmd += [f"ctp{t_}@{c}" for t_ in range(NT)]
                emit("t", lambda t, par=par: t.matmul(
                    psum[6 + par][:, 0:512], onesrf[:],
                    crow[0:1, par * 512:(par + 1) * 512], start=True, stop=True),
                    deps=bmd, event=f"Bmm{e_}@{c}")
                bed = [f"Bmm{e_}@{c}"]
                if P:
                    bed.append(f"ap2_{e_}@{c-1}")
                emit("v", lambda e, e_=e_, par=par: e.tensor_copy(
                    r2[:, e_ * 512:(e_ + 1) * 512], psum[6 + par][:, 0:512]),
                    deps=bed, event=f"Bev{e_}@{c}")

            ffn_cons(0)
            ffn_cons(1)
            ffn_cons(2)
            for idx in range(3, NE * IT):
                ffn_mms(idx)
                ffn_cons(idx)

            load_wd(0)
            load_wd(1)
            hall = [f"hp{e_}_{it}@{c}" for e_ in range(NE) for it in range(IT)]
            for o_ in range(KT):
                par = o_ % 2
                deps = [f"wdld{o_}@{c}"]
                if o_ == 0:
                    deps += hall + [f"Rsb@{c}", f"Msb@{c}",
                                    f"sg{NE-1}_{IT-1}@{c}",
                                    f"up{NE-1}_{IT-1}@{c}"]
                elif o_ == 1:
                    pass
                if o_ >= 2:
                    deps.append(f"r2_{o_-2}@{c}")

                def fny(t, par=par):
                    mm = None
                    for e_ in range(NE):
                        for it in range(IT):
                            mm = t.matmul(
                                psum[4 + par][:, 0:512],
                                wdb[:, par * 6144 + (e_ * IT + it) * 128:
                                    par * 6144 + (e_ * IT + it + 1) * 128],
                                pool[:, e_ * 3072 + it * 512:
                                     e_ * 3072 + (it + 1) * 512],
                                start=(e_ == 0 and it == 0),
                                stop=(e_ == NE - 1 and it == IT - 1))
                    return mm
                emit("t", fny, deps=deps, event=f"y{o_}@{c}")
                if o_ + 2 < KT:
                    load_wd(o_ + 2)
                r2d = [f"y{o_}@{c}"]
                if o_ < NE:
                    r2d += [f"up{o_}_{it}@{c}" for it in range(IT)]
                emit("v", lambda e, o_=o_, par=par: e.tensor_tensor(
                    r2[:, o_ * 512:(o_ + 1) * 512],
                    x1[:, o_ * 512:(o_ + 1) * 512],
                    psum[4 + par][:, 0:512], ALU.add),
                    deps=r2d, event=f"r2_{o_}@{c}")
                emit("v", lambda e, o_=o_: e.tensor_tensor(
                    sq[:, (o_ % 2) * 512:(o_ % 2 + 1) * 512],
                    r2[:, o_ * 512:(o_ + 1) * 512],
                    r2[:, o_ * 512:(o_ + 1) * 512], ALU.mult),
                    deps=[f"r2_{o_}@{c}"] +
                         ([f"st4_{o_-2}@{c}"] if o_ >= 2 else
                          [f"sg{NE-1}_{IT-1}@{c}"]),
                    event=f"sq2_{o_}@{c}")
                def emit_st34(o_):
                    s3d = [f"r2_{o_}@{c}"]
                    if o_ == 0:
                        s3d += [f"Bev{NE-2}@{c}", f"Bev{NE-1}@{c}",
                                f"rm2@{c}", f"rcb@{c}", f"cmrows@{c}"]
                    emit("t", lambda t, o_=o_: t.matmul(
                        psum[6][0:1, 0:512], onesc[:],
                        r2[:, o_ * 512:(o_ + 1) * 512],
                        start=(o_ == 0), stop=(o_ == KT - 1)),
                        deps=s3d, event=f"st3_{o_}@{c}")
                    s4d = [f"sq2_{o_}@{c}"]
                    if o_ == 0:
                        s4d.append(f"Bev{NE-1}@{c}")
                    emit("t", lambda t, o_=o_: t.matmul(
                        psum[7][0:1, 0:512], onesc[:],
                        sq[:, (o_ % 2) * 512:(o_ % 2 + 1) * 512],
                        start=(o_ == 0), stop=(o_ == KT - 1)),
                        deps=s4d, event=f"st4_{o_}@{c}")
                if o_ >= 1:
                    emit_st34(o_ - 1)
                if o_ == KT - 1:
                    emit_st34(o_)
            EV[f"ydone@{c}"] = EV[f"y{KT-1}@{c}"]

            war2 = [f"bcR@{c}", f"bcM@{c}"]
            ln_rows(f"l2@{c}", 6, 7, f"st3_{KT-1}@{c}", f"st4_{KT-1}@{c}", war2)
            def emit_ln2_head(c=c):
                emit("t", lambda t: t.matmul(psum[6][:, 0:512], onesrf[:],
                                             rowsb[0:1, 0:512], start=True,
                                             stop=True),
                     deps=[f"l2@{c}_rstd"],
                     event=f"bcR2@{c}")
                emit("t", lambda t: t.matmul(psum[7][:, 0:512], onesrf[:],
                                             rowsb[0:1, 512:1024], start=True,
                                             stop=True),
                     deps=[f"l2@{c}_mr"], event=f"bcM2@{c}")
                emit("v", lambda e: e.tensor_copy(rsbb[:, 0:512],
                                                  psum[6][:, 0:512]),
                     deps=[f"bcR2@{c}", f"ap1_{KT-1}@{c}"], event=f"R2sb@{c}")
                emit("v", lambda e: e.tensor_copy(rsb[:, 512:1024],
                                                  psum[7][:, 0:512]),
                     deps=[f"bcM2@{c}", f"x1_{KT-1}@{c}"], event=f"M2sb@{c}")

            def emit_ln2_pair(kt, c=c):
                eng = "v" if kt % 2 == 0 else "p"
                a2d = [f"R2sb@{c}", f"up{NE-1}_{IT-1}@{c}"]
                if kt >= 2:
                    a2d.append(f"os_{kt-2}@{c}")
                emit(eng, lambda e, kt=kt: e.tensor_tensor(
                    apb[:, (kt % 2) * 512:(kt % 2 + 1) * 512],
                    r2[:, kt * 512:(kt + 1) * 512], rsbb[:, 0:512],
                    ALU.mult),
                    deps=a2d, event=f"ap2_{kt}@{c}")
                emit(eng, lambda e, kt=kt: e.tensor_tensor(
                    outt[:, (kt % 2) * 512:(kt % 2 + 1) * 512],
                    apb[:, (kt % 2) * 512:(kt % 2 + 1) * 512],
                    rsb[:, 512:1024], ALU.subtract),
                    deps=[f"ap2_{kt}@{c}", f"M2sb@{c}"], event=f"o_{kt}@{c}")
                dma("y", OS[kt % 2], out_d[c, kt],
                    outt[:, (kt % 2) * 512:(kt % 2 + 1) * 512],
                    deps=[f"o_{kt}@{c}"], event=f"os_{kt}@{c}")
            pend_bc2.append((emit_ln2_head, emit_ln2_pair))

        while pend_bc2:
            th_, tp_ = pend_bc2.pop(0)
            th_()
            for kt in range(KT):
                tp_(kt)

        def final(e):
            for s in ALL_DMA_SEMS:
                e.wait_ge(s.h, s.n)
        plan["y"].append(final)

        block = es.enter_context(nc.Block())

        @block.tensor
        def _(t):
            for f in plan["t"]:
                f(t)

        @block.vector
        def _(v):
            for f in plan["v"]:
                f(v)

        @block.scalar
        def _(s):
            for f in plan["s"]:
                f(s)

        @block.gpsimd
        def _(g):
            for f in plan["g"]:
                f(g)

        @block.sync
        def _(y):
            for f in plan["y"]:
                f(y)

    return nc


def layout_inputs(x, Wq, Wo, bank_weights, memories, memory_norms, W_gate,
                  Wg, Wu, Wd, n_cores, tokens_per_core, n_chunks):
    asc = np.ascontiguousarray
    f32 = np.float32
    x = np.asarray(x, f32).reshape(-1, D)
    shared = {
        "wqt": asc(np.asarray(Wq, f32).reshape(KT, 128, KT, 128)
                   .transpose(0, 3, 2, 1)),
        "wot": asc(np.asarray(Wo, f32).reshape(KT, 128, KT, 128)
                   .transpose(0, 3, 2, 1)),
        "wg": asc(np.asarray(Wg, f32).reshape(NE, KT, 128, IT, 128)
                  .transpose(0, 3, 2, 1, 4)),
        "wu": asc(np.asarray(Wu, f32).reshape(NE, KT, 128, IT, 128)
                  .transpose(0, 3, 2, 1, 4)),
        "wd": asc(np.asarray(Wd, f32).reshape(NE, IT, 128, KT, 128)
                  .transpose(3, 2, 0, 1, 4)),
        "wgt": asc(np.asarray(W_gate, f32).T.reshape(KT, 128, 8)
                   .transpose(1, 0, 2).reshape(128, KT * 8)),
        "mem": asc(np.asarray(memories, f32).transpose(2, 1, 0, 3)
                   .reshape(128, NB * H * 128)),
        "norms": asc(np.asarray(memory_norms, f32).transpose(2, 1, 0)
                     .reshape(128, H * NB)),
        "bw": asc(np.asarray(bank_weights, f32)),
        "ident": np.eye(128, dtype=f32),
    }
    in_maps = []
    for i in range(n_cores):
        xc = x[i * tokens_per_core:(i + 1) * tokens_per_core]
        xh = asc(xc.T.reshape(KT, 128, n_chunks, C).transpose(2, 1, 0, 3))
        m = dict(shared)
        m["xt"] = xh
        in_maps.append(m)
    return in_maps


def kernel(L0_attn_hidden_states, L0_attn_Wq, L0_attn_Wk, L0_attn_Wv,
           L0_attn_Wo, L0_attn_bank_weights, L0_attn_memories,
           L0_attn_memory_norms, L0_ffn_W_gate, L0_ffn_Wg, L0_ffn_Wu,
           L0_ffn_Wd):
    n_cores = 8
    total_tokens = int(np.prod(L0_attn_hidden_states.shape[:-1]))
    tokens_per_core = total_tokens // n_cores
    n_chunks = tokens_per_core // C
    in_maps = layout_inputs(L0_attn_hidden_states, L0_attn_Wq, L0_attn_Wo,
                            L0_attn_bank_weights, L0_attn_memories,
                            L0_attn_memory_norms, L0_ffn_W_gate, L0_ffn_Wg,
                            L0_ffn_Wu, L0_ffn_Wd, n_cores, tokens_per_core,
                            n_chunks)
    nc = build_program(n_chunks)
    res = run_bass_kernel_spmd(nc, in_maps, core_ids=list(range(n_cores)))
    parts = []
    for i in range(n_cores):
        arr = np.asarray(res.results[i]["out"])
        parts.append(arr.transpose(0, 3, 1, 2).reshape(tokens_per_core, D))
    full = np.concatenate(parts, axis=0)
    return np.ascontiguousarray(
        full.reshape(L0_attn_hidden_states.shape).astype(np.float32))


# version 08
# speedup vs baseline: 1.2302x; 1.0094x over previous
import os as _os
import numpy as np
import concourse.bass as bass
import concourse.mybir as mybir
from concourse.bass_utils import run_bass_kernel_spmd

F32 = mybir.dt.float32
BF16 = mybir.dt.bfloat16
AF = mybir.ActivationFunctionType
ALU = mybir.AluOpType
AX = mybir.AxisListType

D = 2048
H = 16
NB = 4
FF = 768
NE = 8
KT = D // 128
IT = FF // 128
C = 512
NT = C // 128
EPS = 1e-6
LNEPS = 1e-5
LN_INV = 1.0 / D


class Sem:
    def __init__(self, handle, step):
        self.h = handle
        self.step = step
        self.n = 0


def build_program(n_chunks):
    nc = bass.Bass("TRN2", target_bir_lowering=False,
                   detect_race_conditions=bool(_os.environ.get("K_RACE")))

    xt = nc.dram_tensor("xt", [n_chunks, 128, KT, C], F32, kind="ExternalInput").ap()
    wqt = nc.dram_tensor("wqt", [KT, 128, KT, 128], F32, kind="ExternalInput").ap()
    wot = nc.dram_tensor("wot", [KT, 128, KT, 128], F32, kind="ExternalInput").ap()
    wg_i = nc.dram_tensor("wg", [NE, IT, 128, KT, 128], F32, kind="ExternalInput").ap()
    wu_i = nc.dram_tensor("wu", [NE, IT, 128, KT, 128], F32, kind="ExternalInput").ap()
    wd_i = nc.dram_tensor("wd", [KT, 128, NE, IT, 128], F32, kind="ExternalInput").ap()
    wgt = nc.dram_tensor("wgt", [128, KT * 8], F32, kind="ExternalInput").ap()
    mem_i = nc.dram_tensor("mem", [128, NB * H * 128], F32, kind="ExternalInput").ap()
    norms_i = nc.dram_tensor("norms", [128, H * NB], F32, kind="ExternalInput").ap()
    bw_i = nc.dram_tensor("bw", [H, NB], F32, kind="ExternalInput").ap()
    ident_i = nc.dram_tensor("ident", [128, 128], F32, kind="ExternalInput").ap()
    out_d = nc.dram_tensor("out", [n_chunks, KT, 128, C], F32, kind="ExternalOutput").ap()

    wq_s = nc.dram_tensor("wq_s", [KT, 128, KT, 128], BF16, kind="Internal").ap()
    wo_s = nc.dram_tensor("wo_s", [KT, 128, KT, 128], BF16, kind="Internal").ap()
    wg_s = nc.dram_tensor("wg_s", [NE, IT, 128, KT, 128], BF16, kind="Internal").ap()
    wu_s = nc.dram_tensor("wu_s", [NE, IT, 128, KT, 128], BF16, kind="Internal").ap()
    wd_s = nc.dram_tensor("wd_s", [KT, 128, NE, IT, 128], BF16, kind="Internal").ap()

    from contextlib import ExitStack
    es = ExitStack()
    with es:
        sb = lambda name, shape, dt: es.enter_context(nc.sbuf_tensor(name, shape, dt))
        pool = sb("pool", [128, 24576], BF16)
        xbuf = sb("xbuf", [128, 8192], BF16)
        x1 = sb("x1", [128, KT * C], BF16)
        r2 = sb("r2", [128, KT * C], BF16)
        sigma = sb("sigma", [128, 2 * C], BF16)
        sq = sb("sq", [128, 2 * C], BF16)
        rsb = sb("rsb", [128, 2 * C], F32)
        outt = sb("outt", [128, 2 * C], F32)
        wqb = sb("wqb", [128, 2 * KT * 128], BF16)
        wob = sb("wob", [128, 2 * KT * 128], BF16)
        wgb = sb("wgb", [128, 2 * KT * 128], BF16)
        wub = sb("wub", [128, 2 * KT * 128], BF16)
        wdb = sb("wdb", [128, 2 * NE * IT * 128], BF16)
        memb = sb("memb", [128, NB * H * 128], BF16)
        normb = sb("normb", [128, H * NB], BF16)
        wgtb = sb("wgtb", [128, KT * 8], BF16)
        identb = sb("identb", [128, 128], BF16)
        identf = sb("identf", [128, 128], F32)
        onesc = sb("onesc", [128, 1], BF16)
        onescf = sb("onescf", [128, 1], F32)
        onesrf = sb("onesrf", [1, 128], F32)
        onesr = sb("onesr", [1, 128], BF16)
        normf = sb("normf", [128, H * NB], F32)
        bwf = sb("bwf", [H, NB], F32)
        bsc1 = sb("bsc1", [H, 3 * NB], F32)
        sumr = sb("sumr", [1, H * NB], F32)
        snr = sb("snr", [1, 2 * NB], F32)
        wflat = sb("wflat", [1, H * NB], F32)
        weff = sb("weff", [H, NB], F32)
        wbc = sb("wbc", [128, H * NB], F32)
        call = sb("call", [128, 2 * 16], F32)
        bsc = sb("bsc", [128, 2 * 512], BF16)
        pps = sb("pps", [128, 2 * 256], BF16)
        rsbb = sb("rsbb", [128, 512], BF16)
        apb = sb("apb", [128, 1024], BF16)
        ct = sb("ct", [128, 2 * NT * 128], BF16)
        rowsb = sb("rowsb", [1, 2 * C], F32)
        rows2 = sb("rows2", [1, 2 * C], F32)
        logitsb = sb("logitsb", [8, C], F32)
        rt = sb("rt", [128, NT * 16], F32)
        crow = sb("crow", [1, 2 * C], F32)

        psum = [es.enter_context(nc.psum_tensor(f"P{i}", [128, 512], F32))
                for i in range(8)]

        def mksem(name, step=16):
            return Sem(es.enter_context(nc.semaphore(name)), step)

        SSET = mksem("sset")
        BWS = mksem("bws")
        WFS = mksem("wfs")
        CVA = mksem("cva")
        CVB = mksem("cvb")
        GSET = mksem("gset")
        XS = mksem("xs")
        WQ = [mksem("wq0"), mksem("wq1")]
        WO = [mksem("wo0"), mksem("wo1")]
        WG = [mksem("wgs0"), mksem("wgs1")]
        WU = [mksem("wus0"), mksem("wus1")]
        WD = [mksem("wds0"), mksem("wds1")]
        CR = [mksem("cr0"), mksem("cr1")]
        OS = [mksem("os0"), mksem("os1")]
        PE = mksem("pe", 1)
        DV = mksem("dv", 1)
        AC = mksem("ac", 1)
        PL = mksem("pl", 1)
        TOTAL_SEMS = (SSET, BWS, WFS, CVA, CVB, GSET)
        ALL_DMA_SEMS = [SSET, BWS, WFS, CVA, CVB, GSET, XS] + WQ + WO + WG + WU + WD + CR + OS

        plan = {"t": [], "v": [], "s": [], "g": [], "y": []}
        SEM_OF = {"t": PE, "v": DV, "s": AC, "p": PL}
        PLAN_OF = {"t": "t", "v": "v", "s": "s", "p": "g"}
        QUEUE_PLAN = {"y": "y", "g": "g", "sd": "s"}
        EV = {}
        waited = {k: {} for k in plan}

        def _emit(ep, sem, build_fn, deps, event):
            cap = []
            hw = waited[ep]
            for d in deps:
                s, v = EV[d]
                if s in TOTAL_SEMS:
                    v = s.n
                key = id(s)
                if hw.get(key, -1) >= v:
                    continue
                hw[key] = v
                cap.append((s.h, v))
            if event is not None:
                sem.n += sem.step
                EV[event] = (sem, sem.n)

                def closure(e, cap=cap, build_fn=build_fn, sh=sem.h, st=sem.step):
                    for (h, v) in cap:
                        e.wait_ge(h, v)
                    build_fn(e).then_inc(sh, st)
            else:
                def closure(e, cap=cap, build_fn=build_fn):
                    for (h, v) in cap:
                        e.wait_ge(h, v)
                    build_fn(e)
            plan[ep].append(closure)

        def emit(eng, build_fn, deps=(), event=None):
            _emit(PLAN_OF[eng], SEM_OF[eng], build_fn, deps, event)

        def dma(q, sem, out_ap, in_ap, deps=(), event=None):
            _emit(QUEUE_PLAN[q], sem,
                  lambda e, o=out_ap, i=in_ap: e.dma_start(out=o, in_=i),
                  deps, event)

        dma("y", SSET, normf[:], norms_i, event="ld_normf")
        dma("y", BWS, bwf[:], bw_i, event="ld_bw")
        dma("y", SSET, identf[:], ident_i, event="ld_identf")
        dma("g", GSET, identb[:], ident_i, event="ld_identb")
        dma("g", GSET, wgtb[:], wgt, event="ld_wgt")
        dma("g", GSET, normb[:], norms_i, event="ld_normb")
        mem_stage = pool[:, 0:16384].bitcast(F32)
        dma("y", SSET, mem_stage, mem_i, event="ld_mem")

        emit("v", lambda e: e.memset(onesc[:], 1.0), event="ones1")
        emit("v", lambda e: e.memset(onesr[:], 1.0), event="ones2")
        emit("v", lambda e: e.memset(onesrf[:], 1.0), event="ones3")
        emit("v", lambda e: e.memset(onescf[:], 1.0), event="ones4")

        emit("v", lambda e: e.reduce_max(bsc1[:, 0:1], bwf[:], axis=AX.X),
             deps=["ld_bw"], event="bs_m")
        emit("v", lambda e: e.tensor_scalar(bsc1[:, NB:2 * NB], bwf[:],
                                            bsc1[:, 0:1], None, ALU.subtract),
             deps=["bs_m"], event="bs_d")
        emit("s", lambda e: e.activation(bsc1[:, NB:2 * NB], bsc1[:, NB:2 * NB],
                                         AF.Exp),
             deps=["bs_d"], event="bs_e")
        emit("v", lambda e: e.reduce_sum(bsc1[:, 1:2], bsc1[:, NB:2 * NB],
                                         axis=AX.X),
             deps=["bs_e"], event="bs_s")
        emit("v", lambda e: e.reciprocal(bsc1[:, 2:3], bsc1[:, 1:2]),
             deps=["bs_s"], event="bs_r")
        emit("v", lambda e: e.tensor_scalar(bsc1[:, 2 * NB:3 * NB],
                                            bsc1[:, NB:2 * NB], bsc1[:, 2:3],
                                            None, ALU.mult),
             deps=["bs_r"], event="bs_sm")
        emit("t", lambda t: t.matmul(psum[0][0:1, 0:H * NB], onescf[:], normf[:],
                                     start=True, stop=True),
             deps=["ld_normf", "ones4"], event="act_cs")
        emit("v", lambda e: e.tensor_copy(sumr[:], psum[0][0:1, 0:H * NB]),
             deps=["act_cs"], event="act_cp")
        sum_v = sumr[0:1, :].rearrange("p (h n) -> p n h", h=H, n=NB)
        emit("v", lambda e: e.tensor_reduce(snr[0:1, 0:NB], sum_v, axis=AX.X,
                                            op=ALU.add),
             deps=["act_cp"], event="act_sn")
        emit("v", lambda e: e.tensor_scalar(snr[0:1, NB:2 * NB], snr[0:1, 0:NB],
                                            float(EPS), None, ALU.is_ge),
             deps=["act_sn"], event="act_a")
        emit("t", lambda t: t.matmul(psum[1][0:H, 0:NB], onesrf[0:1, 0:H],
                                     snr[0:1, NB:2 * NB], start=True, stop=True),
             deps=["act_a", "ones3"], event="act_bc")
        emit("v", lambda e: e.tensor_tensor(weff[:], bsc1[:, 2 * NB:3 * NB],
                                            psum[1][0:H, 0:NB], ALU.mult),
             deps=["bs_sm", "act_bc"], event="weff_ok")
        for h_ in range(H):
            dma("y", WFS, wflat[0:1, h_ * NB:(h_ + 1) * NB], weff[h_:h_ + 1, :],
                deps=["weff_ok"], event=f"wflat{h_}")
        EV["wflat_ok"] = EV[f"wflat{H-1}"]
        emit("t", lambda t: t.matmul(psum[2][:, 0:H * NB], onesrf[:],
                                     wflat[0:1, :], start=True, stop=True),
             deps=["wflat_ok", "ones3"], event="wbc_mm")
        emit("v", lambda e: e.tensor_copy(wbc[:], psum[2][:, 0:H * NB]),
             deps=["wbc_mm"], event="wbc_ok")
        for n_ in range(NB):
            for h_ in range(H):
                blk = (h_ * NB + n_) * 128
                col = h_ * NB + n_
                emit("v", lambda e, blk=blk, col=col: e.tensor_scalar(
                    memb[:, blk:blk + 128], mem_stage[:, blk:blk + 128],
                    wbc[:, col:col + 1], None, ALU.mult),
                    deps=["wbc_ok", "ld_mem"],
                    event=("memb_ok" if (n_ == NB - 1 and h_ == H - 1) else None))

        dma("g", XS, xbuf[:, :], xt[0].rearrange("p k t -> p (k t)"),
            deps=[], event="xld@0")

        dma("g", CVA, wq_s, wqt, event="cv_wq")
        dma("g", CVA, wo_s, wot, event="cv_wo")
        EV["cvA"] = EV["cv_wo"]
        for e_ in range(NE):
            dma("g", CVB, wg_s[e_], wg_i[e_], event=f"cv_wg{e_}")
            dma("g", CVB, wu_s[e_], wu_i[e_], event=f"cv_wu{e_}")
        for o_ in range(KT):
            dma("g", CVB, wd_s[o_], wd_i[o_], event=f"cv_wd{o_}")
        EV["cvB"] = EV[f"cv_wd{KT-1}"]

        x_bf = xbuf[:, :]
        comb = pool[:, 0:8192]
        r1 = pool[:, 8192:16384]
        pend_bc2 = []

        for c in range(n_chunks):
            P = c > 0


            def load_wq(h_, cc=None):
                cc = c if cc is None else cc
                par = h_ % 2
                deps = [] if cc > 0 else ["cvA"]
                if cc > 0 or h_ >= 2:
                    pc, ph = (cc, h_ - 2) if h_ >= 2 else (cc - 1, h_ + 14)
                    deps.append(f"q{ph}@{pc}")
                dma("sd", WQ[par], wqb[:, par * 2048:(par + 1) * 2048],
                    wq_s[h_].rearrange("p k o -> p (k o)"),
                    deps=deps, event=f"wqld{h_}@{cc}")

            def load_wo(o_):
                par = o_ % 2
                deps = [] if P else ["cvA"]
                if P or o_ >= 2:
                    pc, po = (c, o_ - 2) if o_ >= 2 else (c - 1, o_ + 14)
                    deps.append(f"attn{po}@{pc}")
                dma("sd", WO[par], wob[:, par * 2048:(par + 1) * 2048],
                    wo_s[o_].rearrange("p k o -> p (k o)"),
                    deps=deps, event=f"wold{o_}@{c}")

            def load_wgu(pidx):
                e_, it = divmod(pidx, IT)
                par = pidx % 2
                deps, deps2 = [], []
                if not P:
                    deps.append("cvB")
                    deps2.append("cvB")
                if P or pidx >= 2:
                    qidx, pc = pidx - 2, c
                    if qidx < 0:
                        qidx, pc = qidx + NE * IT, c - 1
                    pe_, pit = divmod(qidx, IT)
                    deps.append(f"g{pe_}_{pit}@{pc}")
                    deps2.append(f"u{pe_}_{pit}@{pc}")
                dma("sd", WG[par], wgb[:, par * 2048:(par + 1) * 2048],
                    wg_s[e_, it].rearrange("p k o -> p (k o)"),
                    deps=deps, event=f"wgld{e_}_{it}@{c}")
                dma("sd", WU[par], wub[:, par * 2048:(par + 1) * 2048],
                    wu_s[e_, it].rearrange("p k o -> p (k o)"),
                    deps=deps2, event=f"wuld{e_}_{it}@{c}")

            def load_wd(o_):
                par = o_ % 2
                deps = [] if P else ["cvB"]
                if P or o_ >= 2:
                    pc, po = (c, o_ - 2) if o_ >= 2 else (c - 1, o_ + 14)
                    deps.append(f"y{po}@{pc}")
                dma("y", WD[par], wdb[:, par * 6144:(par + 1) * 6144],
                    wd_s[o_].rearrange("p e i o -> p (e i o)"),
                    deps=deps, event=f"wdld{o_}@{c}")

            if not P:
                load_wq(0)
                load_wq(1)

            def q_head(h_):
                par = h_ % 2
                deps = [f"xld@{c}", f"wqld{h_}@{c}"]
                if P or h_ >= 2:
                    pc, ph = (c, h_ - 2) if h_ >= 2 else (c - 1, h_ + 14)
                    deps.append(f"sigrd{ph}@{pc}")
                    deps.append(f"smin{ph}@{pc}")
                if P and h_ < 2:
                    deps.append(f"sg{NE-1}_{IT-3}@{c-1}" if h_ == 0
                                else f"up{NE-1}_{IT-3}@{c-1}")
                if not P and h_ == 0:
                    deps.append("act_cp")
                if not P and h_ == 1:
                    deps.append("weff_ok")

                def fn(t, par=par):
                    mm = None
                    for kt in range(KT):
                        mm = t.matmul(
                            psum[par][:, 0:512],
                            wqb[:, par * 2048 + kt * 128:par * 2048 + (kt + 1) * 128],
                            x_bf[:, kt * 512:(kt + 1) * 512],
                            start=(kt == 0), stop=(kt == KT - 1))
                    return mm
                emit("t", fn, deps=deps, event=f"q{h_}@{c}")

            def sigma_head(h_):
                par = h_ % 2
                deps = [f"q{h_}@{c}"]
                if h_ > 0:
                    deps.append(f"sig{h_-1}@{c}")
                elif P:
                    deps += [f"st4_{KT-1}@{c-1}", f"sig{H-1}@{c-1}"]
                emit("s", lambda e, par=par: e.activation(
                    sq[:, 0:512], psum[par][:, 0:512], AF.Relu, scale=-1.0),
                    deps=deps, event=f"smin{h_}@{c}")
                emit("s", lambda e: e.activation(sq[:, 0:512], sq[:, 0:512],
                                                 AF.Exp, scale=-1.0),
                     deps=[f"smin{h_}@{c}"], event=f"sexp{h_}@{c}")
                rdeps = [f"q{h_}@{c}", f"smin{h_}@{c}"]
                emit("s", lambda e, par=par: e.activation(
                    sq[:, 512:1024], psum[par][:, 0:512], AF.Relu),
                    deps=rdeps, event=f"srelu{h_}@{c}")
                sdeps = [f"sexp{h_}@{c}", f"srelu{h_}@{c}"]
                if P or h_ >= 2:
                    pc, ph = (c, h_ - 2) if h_ >= 2 else (c - 1, h_ + 14)
                    sdeps.append(f"bank{ph}@{pc}")
                emit("p", lambda e, par=par: e.tensor_tensor(
                    sigma[:, par * 512:(par + 1) * 512], sq[:, 0:512],
                    sq[:, 512:1024], ALU.add),
                    deps=sdeps, event=f"sig{h_}@{c}")
                EV[f"sigrd{h_}@{c}"] = EV[f"srelu{h_}@{c}"]

            def denoms_head(h_):
                par = h_ % 2
                deps = [f"sig{h_}@{c}"]
                if P or h_ >= 2:
                    pc2, ph2 = (c, h_ - 2) if h_ >= 2 else (c - 1, h_ + 14)
                    deps.append(f"cp2_{ph2}@{pc2}")
                if P or h_ >= 4:
                    pc4, ph4 = (c, h_ - 4) if h_ >= 4 else (c - 1, h_ + 12)
                    deps += [f"evac{ph4}_{t_}@{pc4}" for t_ in range(NT)]
                if P and h_ < 2:
                    if h_ == 0:
                        deps += [f"l2@{c-1}_mean", f"R2sb@{c-1}"]
                    else:
                        deps += [f"l2@{c-1}_msq", f"M2sb@{c-1}"]
                if not P and h_ == 0:
                    deps.append("ld_normb")

                def fn(t, h_=h_, par=par):
                    mm = None
                    for t_ in range(NT):
                        mm = t.matmul(
                            psum[6 + par][:, t_ * 4:(t_ + 1) * 4],
                            sigma[:, par * 512 + t_ * 128:par * 512 + (t_ + 1) * 128],
                            normb[:, h_ * NB:(h_ + 1) * NB],
                            start=True, stop=True)
                    return mm
                emit("t", fn, deps=deps, event=f"dn{h_}@{c}")
                cdeps = [f"dn{h_}@{c}"]
                if h_ >= 2:
                    cdeps.append(f"scl{h_-2}_{NT-1}@{c}")
                elif P:
                    cdeps.append(f"scl{H-2+h_}_{NT-1}@{c-1}")
                emit("v", lambda e, par=par: e.tensor_scalar(
                    call[:, par * 16:(par + 1) * 16], psum[6 + par][:, 0:16],
                    float(EPS), None, ALU.max),
                    deps=cdeps, event=f"cprep{h_}@{c}")
                emit("v", lambda e, par=par: e.reciprocal(
                    call[:, par * 16:(par + 1) * 16],
                    call[:, par * 16:(par + 1) * 16]),
                    deps=[f"cprep{h_}@{c}"], event=f"cp2_{h_}@{c}")

            def bank_head(h_):
                par = h_ % 2
                deps = [f"sig{h_}@{c}"]
                if P or h_ >= 1:
                    pc, ph = (c, h_ - 1) if h_ >= 1 else (c - 1, H - 1)
                    for t_ in range(NT):
                        deps.append(f"scl{ph}_{t_}@{pc}")
                if h_ <= 1:
                    if P:
                        if h_ == 0:
                            deps += [f"sg{NE-1}_{IT-1}@{c-1}",
                                     f"up{NE-1}_{IT-1}@{c-1}",
                                     f"sg{NE-1}_{IT-2}@{c-1}",
                                     f"up{NE-1}_{IT-2}@{c-1}",
                                     f"r2_{KT-2}@{c-1}", f"r2_{KT-1}@{c-1}"]
                    else:
                        if h_ == 0:
                            deps += ["wbc_ok", "memb_ok"]

                def fn(t, h_=h_, par=par):
                    mm = None
                    for t_ in range(NT):
                        mm = t.matmul(
                            psum[2 + t_][:, 0:512],
                            sigma[:, par * 512 + t_ * 128:par * 512 + (t_ + 1) * 128],
                            memb[:, h_ * 512:(h_ + 1) * 512],
                            start=True, stop=True)
                    return mm
                emit("t", fn, deps=deps, event=f"bank{h_}@{c}")

                for t_ in range(NT):
                    sp = par * 512
                    cbv = call[:, par * 16 + t_ * 4:par * 16 + (t_ + 1) * 4]
                    Av = psum[2 + t_][:, 0:512].rearrange(
                        "p (n e) -> p n e", n=NB, e=128)
                    Sv = bsc[:, sp:sp + 512].rearrange(
                        "p (n e) -> p n e", n=NB, e=128)
                    sd = [f"cp2_{h_}@{c}", f"bank{h_}@{c}"]
                    if t_ > 0:
                        sd.append(f"pa{h_}_{t_-1}@{c}")
                    elif h_ >= 2:
                        sd.append(f"pa{h_-2}_{NT-1}@{c}")
                    elif P:
                        sd.append(f"pa{H-2+h_}_{NT-1}@{c-1}")
                    emit("v", lambda e, Av=Av, Sv=Sv, cbv=cbv: e.tensor_tensor(
                        Sv, Av, cbv.unsqueeze(2).broadcast_to([128, NB, 128]),
                        ALU.mult),
                        deps=sd, event=f"scl{h_}_{t_}@{c}")
                    ppo = par * 256
                    pad = [f"scl{h_}_{t_}@{c}"]
                    if t_ > 0:
                        pad.append(f"comb{h_}_{t_-1}@{c}")
                    elif h_ >= 2:
                        pad.append(f"comb{h_-2}_{NT-1}@{c}")
                    elif P:
                        pad.append(f"comb{H-2+h_}_{NT-1}@{c-1}")
                    emit("p", lambda e, sp=sp, ppo=ppo: e.tensor_tensor(
                        pps[:, ppo:ppo + 256].rearrange("p (n e) -> p n e",
                                                        n=2, e=128),
                        bsc[:, sp:sp + 256].rearrange("p (n e) -> p n e",
                                                      n=2, e=128),
                        bsc[:, sp + 256:sp + 512].rearrange(
                            "p (n e) -> p n e", n=2, e=128), ALU.add),
                        deps=pad, event=f"pa{h_}_{t_}@{c}")
                    cto = par * 512 + t_ * 128
                    combd = [f"pa{h_}_{t_}@{c}"]
                    if h_ >= 2:
                        combd.append(f"tr{h_-2}_{t_}@{c}")
                    elif P:
                        combd.append(f"tr{H-2+h_}_{t_}@{c-1}")
                    emit("p", lambda e, cto=cto, ppo=ppo: e.tensor_tensor(
                        ct[:, cto:cto + 128], pps[:, ppo:ppo + 128],
                        pps[:, ppo + 128:ppo + 256], ALU.add),
                        deps=combd, event=f"comb{h_}_{t_}@{c}")

            def bank_trs(h_):
                par = h_ % 2
                pv = psum[6 + par].ap().bitcast(BF16)
                for t_ in range(NT):
                    trd = [f"comb{h_}_{t_}@{c}", "ld_identb", f"cp2_{h_}@{c}"]
                    if h_ + 2 < H:
                        trd.append(f"cp2_{h_+2}@{c}")
                    if h_ >= 2:
                        trd.append(f"evac{h_-2}_{t_}@{c}")
                    elif P:
                        trd.append(f"evac{h_+14}_{t_}@{c-1}")
                    emit("t", lambda t, t_=t_, par=par, pv=pv: t.transpose(
                        pv[:, 512 + t_ * 128:512 + (t_ + 1) * 128],
                        ct[:, par * 512 + t_ * 128:par * 512 + (t_ + 1) * 128],
                        identb[:]),
                        deps=trd, event=f"tr{h_}_{t_}@{c}")
                for t_ in range(NT):
                    emit("s", lambda e, t_=t_, h_=h_, pv=pv: e.activation(
                        comb[:, h_ * 512 + t_ * 128:h_ * 512 + (t_ + 1) * 128],
                        pv[:, 512 + t_ * 128:512 + (t_ + 1) * 128], AF.Copy),
                        deps=[f"tr{h_}_{NT-1}@{c}"], event=f"evac{h_}_{t_}@{c}")

            q_head(0)
            load_wq(2)
            sigma_head(0)
            q_head(1)
            load_wq(3)
            sigma_head(1)
            tail_pair = None
            if pend_bc2:
                th_, tail_pair = pend_bc2.pop(0)
                th_()
            denoms_head(0)
            for h_ in range(H):
                if tail_pair is not None and h_ < KT:
                    tail_pair(h_)
                bank_head(h_)
                if h_ + 1 < H:
                    denoms_head(h_ + 1)
                if h_ + 2 < H:
                    q_head(h_ + 2)
                if h_ + 4 < H:
                    load_wq(h_ + 4)
                if h_ == 0:
                    load_wo(0)
                    load_wo(1)
                if h_ + 2 < H:
                    sigma_head(h_ + 2)
                if h_ >= 1:
                    bank_trs(h_ - 1)
            bank_trs(H - 1)

            comb_done = [f"evac{H-1}_{t_}@{c}" for t_ in range(NT)]
            for o_ in range(KT):
                par = o_ % 2
                deps = comb_done + [f"wold{o_}@{c}"]
                if o_ < 2:
                    deps.append(f"sigrd{14 + o_}@{c}")
                else:
                    deps.append(f"r1_{o_-2}@{c}")

                def fn(t, par=par):
                    mm = None
                    for kt in range(KT):
                        mm = t.matmul(
                            psum[par][:, 0:512],
                            wob[:, par * 2048 + kt * 128:par * 2048 + (kt + 1) * 128],
                            comb[:, kt * 512:(kt + 1) * 512],
                            start=(kt == 0), stop=(kt == KT - 1))
                    return mm
                emit("t", fn, deps=deps, event=f"attn{o_}@{c}")
                if o_ + 2 < KT:
                    load_wo(o_ + 2)
                emit("v", lambda e, o_=o_, par=par: e.tensor_tensor(
                    r1[:, o_ * 512:(o_ + 1) * 512],
                    x_bf[:, o_ * 512:(o_ + 1) * 512],
                    psum[par][:, 0:512], ALU.add),
                    deps=[f"attn{o_}@{c}"], event=f"r1_{o_}@{c}")
                emit("v", lambda e, o_=o_: e.tensor_tensor(
                    sq[:, (o_ % 2) * 512:(o_ % 2 + 1) * 512],
                    r1[:, o_ * 512:(o_ + 1) * 512],
                    r1[:, o_ * 512:(o_ + 1) * 512], ALU.mult),
                    deps=[f"r1_{o_}@{c}"] +
                         ([f"st2_{o_-2}@{c}"] if o_ >= 2 else []),
                    event=f"sq_{o_}@{c}")
                def emit_st12(o_):
                    sdeps = [f"r1_{o_}@{c}"]
                    if o_ == 0:
                        sdeps += [f"scl{H-1}_0@{c}"]
                    emit("t", lambda t, o_=o_: t.matmul(
                        psum[2][0:1, 0:512], onesc[:],
                        r1[:, o_ * 512:(o_ + 1) * 512],
                        start=(o_ == 0), stop=(o_ == KT - 1)),
                        deps=sdeps, event=f"st1_{o_}@{c}")
                    s2deps = [f"sq_{o_}@{c}"]
                    if o_ == 0:
                        s2deps += [f"scl{H-1}_1@{c}"]
                    emit("t", lambda t, o_=o_: t.matmul(
                        psum[3][0:1, 0:512], onesc[:],
                        sq[:, (o_ % 2) * 512:(o_ % 2 + 1) * 512],
                        start=(o_ == 0), stop=(o_ == KT - 1)),
                        deps=s2deps, event=f"st2_{o_}@{c}")
                if o_ >= 1:
                    emit_st12(o_ - 1)
                if o_ == KT - 1:
                    emit_st12(o_)

            def ln_rows(tag, sum_bank, sq_bank, dep_sum, dep_sq, war_deps):
                emit("v", lambda e: e.tensor_scalar(
                    rows2[0:1, 0:512], psum[sum_bank][0:1, 0:512], LN_INV, None,
                    ALU.mult), deps=[dep_sum], event=f"{tag}_mean")
                emit("v", lambda e: e.tensor_scalar(
                    rows2[0:1, 512:1024], psum[sq_bank][0:1, 0:512], LN_INV,
                    None, ALU.mult), deps=[dep_sq], event=f"{tag}_msq")
                emit("v", lambda e: e.tensor_tensor(
                    rowsb[0:1, 0:512], rows2[0:1, 0:512], rows2[0:1, 0:512],
                    ALU.mult), deps=[f"{tag}_mean"] + war_deps,
                    event=f"{tag}_m2")
                emit("v", lambda e: e.tensor_tensor(
                    rowsb[0:1, 0:512], rows2[0:1, 512:1024], rowsb[0:1, 0:512],
                    ALU.subtract), deps=[f"{tag}_m2", f"{tag}_msq"],
                    event=f"{tag}_var")
                emit("v", lambda e: e.tensor_scalar(
                    rowsb[0:1, 0:512], rowsb[0:1, 0:512], LNEPS, None, ALU.add),
                    deps=[f"{tag}_var"], event=f"{tag}_vare")
                emit("s", lambda e: e.activation(
                    rowsb[0:1, 0:512], rowsb[0:1, 0:512], AF.Ln),
                    deps=[f"{tag}_vare"], event=f"{tag}_sd")
                emit("s", lambda e: e.activation(
                    rowsb[0:1, 0:512], rowsb[0:1, 0:512], AF.Exp, scale=-0.5),
                    deps=[f"{tag}_sd"], event=f"{tag}_rstd")
                emit("v", lambda e: e.tensor_tensor(
                    rowsb[0:1, 512:1024], rows2[0:1, 0:512], rowsb[0:1, 0:512],
                    ALU.mult), deps=[f"{tag}_rstd"] + war_deps,
                    event=f"{tag}_mr")

            war1 = [f"bcR2@{c-1}", f"bcM2@{c-1}"] if P else []
            ln_rows(f"l1@{c}", 2, 3, f"st1_{KT-1}@{c}", f"st2_{KT-1}@{c}", war1)
            emit("t", lambda t: t.matmul(psum[4][:, 0:512], onesrf[:],
                                         rowsb[0:1, 0:512], start=True,
                                         stop=True),
                 deps=[f"l1@{c}_rstd", f"scl{H-1}_2@{c}"] +
                      ([f"r2_{KT-2}@{c-1}"] if P else []),
                 event=f"bcR@{c}")
            emit("t", lambda t: t.matmul(psum[5][:, 0:512], onesrf[:],
                                         rowsb[0:1, 512:1024], start=True,
                                         stop=True),
                 deps=[f"l1@{c}_mr", f"scl{H-1}_3@{c}"] +
                      ([f"r2_{KT-1}@{c-1}"] if P else []),
                 event=f"bcM@{c}")
            emit("v", lambda e: e.tensor_copy(rsbb[:, 0:512],
                                              psum[4][:, 0:512]),
                 deps=[f"bcR@{c}"], event=f"Rsb@{c}")
            emit("v", lambda e: e.tensor_copy(rsb[:, 512:1024],
                                              psum[5][:, 0:512]),
                 deps=[f"bcM@{c}"], event=f"Msb@{c}")
            for kt in range(KT):
                eng = "v" if kt % 2 == 0 else "p"
                apd = [f"Rsb@{c}"]
                if kt >= 2:
                    apd.append(f"x1_{kt-2}@{c}")
                elif P:
                    apd.append(f"o_{14 + kt}@{c-1}")
                emit(eng, lambda e, kt=kt: e.tensor_tensor(
                    apb[:, (kt % 2) * 512:(kt % 2 + 1) * 512],
                    r1[:, kt * 512:(kt + 1) * 512], rsbb[:, 0:512], ALU.mult),
                    deps=apd, event=f"ap1_{kt}@{c}")
                x1d = [f"ap1_{kt}@{c}", f"Msb@{c}"]
                if P:
                    x1d += [f"g{NE-1}_{IT-1}@{c-1}", f"u{NE-1}_{IT-1}@{c-1}",
                            f"r2_{kt}@{c-1}", f"lg@{c-1}"]
                emit(eng, lambda e, kt=kt: e.tensor_tensor(
                    x1[:, kt * 512:(kt + 1) * 512],
                    apb[:, (kt % 2) * 512:(kt % 2 + 1) * 512],
                    rsb[:, 512:1024], ALU.subtract),
                    deps=x1d, event=f"x1_{kt}@{c}")

            PDEPTH = 3
            def ffn_mms(idx):
                e_, it = divmod(idx, IT)
                gpar = idx % PDEPTH
                gb, ub = 2 * gpar, 2 * gpar + 1
                woff = (idx % 2) * 2048
                deps = [f"wgld{e_}_{it}@{c}"]
                if idx >= PDEPTH:
                    pe_, pit = divmod(idx - PDEPTH, IT)
                    deps.append(f"hp{pe_}_{pit}@{c}")
                else:
                    deps += [f"ap1_{KT-1}@{c}"]
                    if gpar == 0:
                        deps += [f"r1_{KT-2}@{c}", f"r1_{KT-1}@{c}"]

                ktw = None
                if idx < PDEPTH:
                    ktw = [EV[f"x1_{kt}@{c}"] for kt in range(KT)]

                def fng(t, gb=gb, woff=woff, ktw=ktw):
                    mm = None
                    for kt in range(KT):
                        if ktw is not None:
                            s, v = ktw[kt]
                            t.wait_ge(s.h, v)
                        mm = t.matmul(
                            psum[gb][:, 0:512],
                            wgb[:, woff + kt * 128:woff + (kt + 1) * 128],
                            x1[:, kt * 512:(kt + 1) * 512],
                            start=(kt == 0), stop=(kt == KT - 1))
                    return mm
                emit("t", fng, deps=deps, event=f"g{e_}_{it}@{c}")
                deps_u = [f"wuld{e_}_{it}@{c}"]
                if idx >= PDEPTH:
                    pe_, pit = divmod(idx - PDEPTH, IT)
                    deps_u.append(f"hp{pe_}_{pit}@{c}")

                def fnu(t, ub=ub, woff=woff):
                    mm = None
                    for kt in range(KT):
                        mm = t.matmul(
                            psum[ub][:, 0:512],
                            wub[:, woff + kt * 128:woff + (kt + 1) * 128],
                            x1[:, kt * 512:(kt + 1) * 512],
                            start=(kt == 0), stop=(kt == KT - 1))
                    return mm
                emit("t", fnu, deps=deps_u, event=f"u{e_}_{it}@{c}")
                if idx + 2 < NE * IT:
                    load_wgu(idx + 2)

            def ffn_cons(idx):
                e_, it = divmod(idx, IT)
                gpar = idx % PDEPTH
                gb, ub = 2 * gpar, 2 * gpar + 1
                spar = idx % 2
                sgd = [f"g{e_}_{it}@{c}"]
                if idx >= 2:
                    pe_, pit = divmod(idx - 2, IT)
                    sgd.append(f"hp{pe_}_{pit}@{c}")
                else:
                    sgd.append(f"st2_{14 + idx}@{c}")
                emit("s", lambda e, gb=gb, spar=spar: e.activation(
                    sq[:, spar * 512:(spar + 1) * 512], psum[gb][:, 0:512],
                    AF.Silu),
                    deps=sgd, event=f"sg{e_}_{it}@{c}")
                upd = [f"u{e_}_{it}@{c}", f"Bev{e_}@{c}"]
                if idx >= 1:
                    pe_, pit = divmod(idx - 1, IT)
                    upd.append(f"hp{pe_}_{pit}@{c}")
                emit("v", lambda e, ub=ub, e_=e_: e.tensor_tensor(
                    outt[:, 0:512], psum[ub][:, 0:512],
                    r2[:, e_ * 512:(e_ + 1) * 512], ALU.mult),
                    deps=upd, event=f"up{e_}_{it}@{c}")
                emit("v", lambda e, e_=e_, it=it, spar=spar: e.tensor_tensor(
                    pool[:, e_ * 3072 + it * 512:e_ * 3072 + (it + 1) * 512],
                    sq[:, spar * 512:(spar + 1) * 512], outt[:, 0:512],
                    ALU.mult),
                    deps=[f"sg{e_}_{it}@{c}", f"up{e_}_{it}@{c}"],
                    event=f"hp{e_}_{it}@{c}")


            if c + 1 < n_chunks:
                dma("g", XS, x_bf, xt[c + 1].rearrange("p k t -> p (k t)"),
                    deps=[f"r1_{KT-1}@{c}", f"q{H-1}@{c}"],
                    event=f"xld@{c+1}")

            x1all = [f"x1_{kt}@{c}" for kt in range(KT)]

            lgw = [EV[f"x1_{kt}@{c}"] for kt in range(KT)]

            def fn_logits(t, lgw=lgw):
                mm = None
                for kt in range(KT):
                    s, v = lgw[kt]
                    t.wait_ge(s.h, v)
                    mm = t.matmul(psum[6][0:8, 0:512],
                                  wgtb[:, kt * 8:(kt + 1) * 8],
                                  x1[:, kt * 512:(kt + 1) * 512],
                                  start=(kt == 0), stop=(kt == KT - 1))
                return mm
            emit("t", fn_logits,
                 deps=["ld_wgt", f"cp2_{H-2}@{c}", f"cp2_{H-1}@{c}"] +
                      [f"evac{H-2}_{t_}@{c}" for t_ in range(NT)],
                 event=f"lg@{c}")
            lgsbd = [f"lg@{c}"]
            if P:
                lgsbd.append(f"crow{NE-1}@{c-1}")
            emit("v", lambda e: e.tensor_copy(logitsb[:], psum[6][0:8, 0:512]),
                 deps=lgsbd, event=f"lgsb@{c}")
            for t_ in range(NT):
                emit("t", lambda t, t_=t_: t.transpose(
                    psum[7][:, t_ * 8:(t_ + 1) * 8],
                    logitsb[:, t_ * 128:(t_ + 1) * 128], identf[0:8, 0:8]),
                    deps=[f"lgsb@{c}", "ld_identf", f"cp2_{H-1}@{c}"] +
                         [f"evac{H-1}_{t2_}@{c}" for t2_ in range(NT)],
                    event=f"lgt{t_}@{c}")
            load_wgu(0)
            load_wgu(1)
            ffn_mms(0)
            ffn_mms(1)
            ffn_mms(2)
            Lv = psum[7][:, 0:32].rearrange("p (t e) -> p t e", t=NT, e=8)
            dv_ = outt[:, 0:32].rearrange("p (t e) -> p t e", t=NT, e=8)
            mk_ = rt[:, 32:64].rearrange("p (t e) -> p t e", t=NT, e=8)
            cbv = outt[:, 32:64].rearrange("p (t e) -> p t e", t=NT, e=8)

            def bc4(ap):
                return ap.unsqueeze(2).broadcast_to([128, NT, 8])
            rmd = [f"lgt{NT-1}@{c}", f"x1_{KT-1}@{c}", f"x1_{KT-2}@{c}"]
            if P:
                rmd += [f"ctp{NT-1}@{c-1}", f"os_{KT-1}@{c-1}",
                        f"os_{KT-2}@{c-1}"]
            emit("v", lambda e: e.reduce_max(rt[:, 0:NT], Lv, axis=AX.X),
                 deps=rmd, event=f"rm1@{c}")
            emit("v", lambda e: e.tensor_tensor(dv_, Lv, bc4(rt[:, 0:NT]),
                                                ALU.subtract),
                 deps=[f"rm1@{c}"], event=f"rd@{c}")
            emit("v", lambda e: e.tensor_scalar(mk_, dv_, 0.0, None, ALU.is_ge),
                 deps=[f"rd@{c}"], event=f"rmask@{c}")
            emit("v", lambda e: e.scalar_tensor_tensor(mk_, mk_, -1e30, dv_,
                                                       ALU.mult, ALU.add),
                 deps=[f"rmask@{c}"], event=f"rmsk2@{c}")
            emit("v", lambda e: e.reduce_max(rt[:, NT:2 * NT], mk_, axis=AX.X),
                 deps=[f"rmsk2@{c}"], event=f"rm2@{c}")
            emit("s", lambda e: e.activation(dv_, dv_, AF.Exp),
                 deps=[f"rm2@{c}"], event=f"re1@{c}")
            emit("s", lambda e: e.activation(rt[:, 2 * NT:3 * NT],
                                             rt[:, NT:2 * NT], AF.Exp),
                 deps=[f"rm2@{c}"], event=f"re2@{c}")
            emit("v", lambda e: e.tensor_scalar(rt[:, 3 * NT:4 * NT],
                                                rt[:, 2 * NT:3 * NT], 1.0, None,
                                                ALU.add),
                 deps=[f"re2@{c}"], event=f"rden@{c}")
            emit("v", lambda e: e.reciprocal(rt[:, 4 * NT:5 * NT],
                                             rt[:, 3 * NT:4 * NT]),
                 deps=[f"rden@{c}"], event=f"rrec@{c}")
            emit("v", lambda e: e.tensor_tensor(mk_, dv_,
                                                bc4(rt[:, 2 * NT:3 * NT]),
                                                ALU.is_ge),
                 deps=[f"re1@{c}", f"re2@{c}"], event=f"rtm@{c}")
            emit("v", lambda e: e.tensor_tensor(dv_, dv_, mk_, ALU.mult),
                 deps=[f"rtm@{c}"], event=f"rcm1@{c}")
            cbd = [f"rcm1@{c}", f"rrec@{c}"]
            if P:
                cbd.append(f"ctp{NT-1}@{c-1}")
            emit("v", lambda e: e.tensor_tensor(cbv, dv_,
                                                bc4(rt[:, 4 * NT:5 * NT]),
                                                ALU.mult),
                 deps=cbd, event=f"rcb@{c}")
            for t_ in range(NT):
                emit("t", lambda t, t_=t_: t.transpose(
                    psum[6][0:8, t_ * 128:(t_ + 1) * 128],
                    outt[:, 32 + t_ * 8:32 + (t_ + 1) * 8], identf[:]),
                    deps=[f"rcb@{c}", f"lgsb@{c}"], event=f"ctp{t_}@{c}")
            emit("v", lambda e: e.tensor_copy(logitsb[:], psum[6][0:8, 0:512]),
                 deps=[f"ctp{t_}@{c}" for t_ in range(NT)], event=f"cmrows@{c}")
            for e_ in range(NE):
                par = e_ % 2
                cdp = [f"cmrows@{c}"]
                if e_ >= 2:
                    cdp.append(f"Bmm{e_-2}@{c}")
                dma("y", CR[par], crow[0:1, par * 512:(par + 1) * 512],
                    logitsb[e_:e_ + 1, :], deps=cdp, event=f"crow{e_}@{c}")
                bmd = [f"crow{e_}@{c}", "ones3"]
                if e_ >= 2:
                    bmd.append(f"Bev{e_-2}@{c}")
                else:
                    bmd += [f"rm2@{c}", f"rcb@{c}"]
                    bmd += [f"ctp{t_}@{c}" for t_ in range(NT)]
                emit("t", lambda t, par=par: t.matmul(
                    psum[6 + par][:, 0:512], onesrf[:],
                    crow[0:1, par * 512:(par + 1) * 512], start=True, stop=True),
                    deps=bmd, event=f"Bmm{e_}@{c}")
                bed = [f"Bmm{e_}@{c}"]
                if P:
                    bed.append(f"ap2_{e_}@{c-1}")
                emit("v", lambda e, e_=e_, par=par: e.tensor_copy(
                    r2[:, e_ * 512:(e_ + 1) * 512], psum[6 + par][:, 0:512]),
                    deps=bed, event=f"Bev{e_}@{c}")

            ffn_cons(0)
            ffn_cons(1)
            ffn_cons(2)
            for idx in range(3, NE * IT):
                ffn_mms(idx)
                ffn_cons(idx)

            load_wd(0)
            load_wd(1)
            if c + 1 < n_chunks:
                load_wq(0, c + 1)
                load_wq(1, c + 1)
            hall = [f"hp{e_}_{it}@{c}" for e_ in range(NE) for it in range(IT)]
            for o_ in range(KT):
                par = o_ % 2
                deps = [f"wdld{o_}@{c}"]
                if o_ == 0:
                    deps += hall + [f"Rsb@{c}", f"Msb@{c}",
                                    f"sg{NE-1}_{IT-1}@{c}",
                                    f"up{NE-1}_{IT-1}@{c}"]
                elif o_ == 1:
                    pass
                if o_ >= 2:
                    deps.append(f"r2_{o_-2}@{c}")

                def fny(t, par=par):
                    mm = None
                    for e_ in range(NE):
                        for it in range(IT):
                            mm = t.matmul(
                                psum[4 + par][:, 0:512],
                                wdb[:, par * 6144 + (e_ * IT + it) * 128:
                                    par * 6144 + (e_ * IT + it + 1) * 128],
                                pool[:, e_ * 3072 + it * 512:
                                     e_ * 3072 + (it + 1) * 512],
                                start=(e_ == 0 and it == 0),
                                stop=(e_ == NE - 1 and it == IT - 1))
                    return mm
                emit("t", fny, deps=deps, event=f"y{o_}@{c}")
                if o_ + 2 < KT:
                    load_wd(o_ + 2)
                r2d = [f"y{o_}@{c}"]
                if o_ < NE:
                    r2d += [f"up{o_}_{it}@{c}" for it in range(IT)]
                emit("v", lambda e, o_=o_, par=par: e.tensor_tensor(
                    r2[:, o_ * 512:(o_ + 1) * 512],
                    x1[:, o_ * 512:(o_ + 1) * 512],
                    psum[4 + par][:, 0:512], ALU.add),
                    deps=r2d, event=f"r2_{o_}@{c}")
                emit("v", lambda e, o_=o_: e.tensor_tensor(
                    sq[:, (o_ % 2) * 512:(o_ % 2 + 1) * 512],
                    r2[:, o_ * 512:(o_ + 1) * 512],
                    r2[:, o_ * 512:(o_ + 1) * 512], ALU.mult),
                    deps=[f"r2_{o_}@{c}"] +
                         ([f"st4_{o_-2}@{c}"] if o_ >= 2 else
                          [f"sg{NE-1}_{IT-1}@{c}"]),
                    event=f"sq2_{o_}@{c}")
                def emit_st34(o_):
                    s3d = [f"r2_{o_}@{c}"]
                    if o_ == 0:
                        s3d += [f"Bev{NE-2}@{c}", f"Bev{NE-1}@{c}",
                                f"rm2@{c}", f"rcb@{c}", f"cmrows@{c}"]
                    emit("t", lambda t, o_=o_: t.matmul(
                        psum[6][0:1, 0:512], onesc[:],
                        r2[:, o_ * 512:(o_ + 1) * 512],
                        start=(o_ == 0), stop=(o_ == KT - 1)),
                        deps=s3d, event=f"st3_{o_}@{c}")
                    s4d = [f"sq2_{o_}@{c}"]
                    if o_ == 0:
                        s4d.append(f"Bev{NE-1}@{c}")
                    emit("t", lambda t, o_=o_: t.matmul(
                        psum[7][0:1, 0:512], onesc[:],
                        sq[:, (o_ % 2) * 512:(o_ % 2 + 1) * 512],
                        start=(o_ == 0), stop=(o_ == KT - 1)),
                        deps=s4d, event=f"st4_{o_}@{c}")
                if o_ >= 1:
                    emit_st34(o_ - 1)
                if o_ == KT - 1:
                    emit_st34(o_)
            EV[f"ydone@{c}"] = EV[f"y{KT-1}@{c}"]

            war2 = [f"bcR@{c}", f"bcM@{c}"]
            ln_rows(f"l2@{c}", 6, 7, f"st3_{KT-1}@{c}", f"st4_{KT-1}@{c}", war2)
            def emit_ln2_head(c=c):
                emit("t", lambda t: t.matmul(psum[6][:, 0:512], onesrf[:],
                                             rowsb[0:1, 0:512], start=True,
                                             stop=True),
                     deps=[f"l2@{c}_rstd"],
                     event=f"bcR2@{c}")
                emit("t", lambda t: t.matmul(psum[7][:, 0:512], onesrf[:],
                                             rowsb[0:1, 512:1024], start=True,
                                             stop=True),
                     deps=[f"l2@{c}_mr"], event=f"bcM2@{c}")
                emit("v", lambda e: e.tensor_copy(rsbb[:, 0:512],
                                                  psum[6][:, 0:512]),
                     deps=[f"bcR2@{c}", f"ap1_{KT-1}@{c}"], event=f"R2sb@{c}")
                emit("v", lambda e: e.tensor_copy(rsb[:, 512:1024],
                                                  psum[7][:, 0:512]),
                     deps=[f"bcM2@{c}", f"x1_{KT-1}@{c}"], event=f"M2sb@{c}")

            def emit_ln2_pair(kt, c=c):
                eng = "v" if kt % 2 == 0 else "p"
                a2d = [f"R2sb@{c}", f"up{NE-1}_{IT-1}@{c}"]
                if kt >= 2:
                    a2d.append(f"os_{kt-2}@{c}")
                emit(eng, lambda e, kt=kt: e.tensor_tensor(
                    apb[:, (kt % 2) * 512:(kt % 2 + 1) * 512],
                    r2[:, kt * 512:(kt + 1) * 512], rsbb[:, 0:512],
                    ALU.mult),
                    deps=a2d, event=f"ap2_{kt}@{c}")
                emit(eng, lambda e, kt=kt: e.tensor_tensor(
                    outt[:, (kt % 2) * 512:(kt % 2 + 1) * 512],
                    apb[:, (kt % 2) * 512:(kt % 2 + 1) * 512],
                    rsb[:, 512:1024], ALU.subtract),
                    deps=[f"ap2_{kt}@{c}", f"M2sb@{c}"], event=f"o_{kt}@{c}")
                dma("y", OS[kt % 2], out_d[c, kt],
                    outt[:, (kt % 2) * 512:(kt % 2 + 1) * 512],
                    deps=[f"o_{kt}@{c}"], event=f"os_{kt}@{c}")
            pend_bc2.append((emit_ln2_head, emit_ln2_pair))

        while pend_bc2:
            th_, tp_ = pend_bc2.pop(0)
            th_()
            for kt in range(KT):
                tp_(kt)

        def final(e):
            for s in ALL_DMA_SEMS:
                e.wait_ge(s.h, s.n)
        plan["y"].append(final)

        block = es.enter_context(nc.Block())

        @block.tensor
        def _(t):
            for f in plan["t"]:
                f(t)

        @block.vector
        def _(v):
            for f in plan["v"]:
                f(v)

        @block.scalar
        def _(s):
            for f in plan["s"]:
                f(s)

        @block.gpsimd
        def _(g):
            for f in plan["g"]:
                f(g)

        @block.sync
        def _(y):
            for f in plan["y"]:
                f(y)

    return nc


def layout_inputs(x, Wq, Wo, bank_weights, memories, memory_norms, W_gate,
                  Wg, Wu, Wd, n_cores, tokens_per_core, n_chunks):
    asc = np.ascontiguousarray
    f32 = np.float32
    x = np.asarray(x, f32).reshape(-1, D)
    shared = {
        "wqt": asc(np.asarray(Wq, f32).reshape(KT, 128, KT, 128)
                   .transpose(0, 3, 2, 1)),
        "wot": asc(np.asarray(Wo, f32).reshape(KT, 128, KT, 128)
                   .transpose(0, 3, 2, 1)),
        "wg": asc(np.asarray(Wg, f32).reshape(NE, KT, 128, IT, 128)
                  .transpose(0, 3, 2, 1, 4)),
        "wu": asc(np.asarray(Wu, f32).reshape(NE, KT, 128, IT, 128)
                  .transpose(0, 3, 2, 1, 4)),
        "wd": asc(np.asarray(Wd, f32).reshape(NE, IT, 128, KT, 128)
                  .transpose(3, 2, 0, 1, 4)),
        "wgt": asc(np.asarray(W_gate, f32).T.reshape(KT, 128, 8)
                   .transpose(1, 0, 2).reshape(128, KT * 8)),
        "mem": asc(np.asarray(memories, f32).transpose(2, 1, 0, 3)
                   .reshape(128, NB * H * 128)),
        "norms": asc(np.asarray(memory_norms, f32).transpose(2, 1, 0)
                     .reshape(128, H * NB)),
        "bw": asc(np.asarray(bank_weights, f32)),
        "ident": np.eye(128, dtype=f32),
    }
    in_maps = []
    for i in range(n_cores):
        xc = x[i * tokens_per_core:(i + 1) * tokens_per_core]
        xh = asc(xc.T.reshape(KT, 128, n_chunks, C).transpose(2, 1, 0, 3))
        m = dict(shared)
        m["xt"] = xh
        in_maps.append(m)
    return in_maps


def kernel(L0_attn_hidden_states, L0_attn_Wq, L0_attn_Wk, L0_attn_Wv,
           L0_attn_Wo, L0_attn_bank_weights, L0_attn_memories,
           L0_attn_memory_norms, L0_ffn_W_gate, L0_ffn_Wg, L0_ffn_Wu,
           L0_ffn_Wd):
    n_cores = 8
    total_tokens = int(np.prod(L0_attn_hidden_states.shape[:-1]))
    tokens_per_core = total_tokens // n_cores
    n_chunks = tokens_per_core // C
    in_maps = layout_inputs(L0_attn_hidden_states, L0_attn_Wq, L0_attn_Wo,
                            L0_attn_bank_weights, L0_attn_memories,
                            L0_attn_memory_norms, L0_ffn_W_gate, L0_ffn_Wg,
                            L0_ffn_Wu, L0_ffn_Wd, n_cores, tokens_per_core,
                            n_chunks)
    nc = build_program(n_chunks)
    res = run_bass_kernel_spmd(nc, in_maps, core_ids=list(range(n_cores)))
    parts = []
    for i in range(n_cores):
        arr = np.asarray(res.results[i]["out"])
        parts.append(arr.transpose(0, 3, 1, 2).reshape(tokens_per_core, D))
    full = np.concatenate(parts, axis=0)
    return np.ascontiguousarray(
        full.reshape(L0_attn_hidden_states.shape).astype(np.float32))


# version 09
# speedup vs baseline: 1.2784x; 1.0392x over previous
import os as _os
import numpy as np
import concourse.bass as bass
import concourse.mybir as mybir
from concourse.bass_utils import run_bass_kernel_spmd

F32 = mybir.dt.float32
BF16 = mybir.dt.bfloat16
AF = mybir.ActivationFunctionType
ALU = mybir.AluOpType
AX = mybir.AxisListType

D = 2048
H = 16
NB = 4
FF = 768
NE = 8
KT = D // 128
IT = FF // 128
C = 512
NT = C // 128
EPS = 1e-6
LNEPS = 1e-5
LN_INV = 1.0 / D


class Sem:
    def __init__(self, handle, step):
        self.h = handle
        self.step = step
        self.n = 0


def build_program(n_chunks):
    nc = bass.Bass("TRN2", target_bir_lowering=False,
                   detect_race_conditions=bool(_os.environ.get("K_RACE")))

    xt = nc.dram_tensor("xt", [n_chunks, 128, KT, C], F32, kind="ExternalInput").ap()
    wqt = nc.dram_tensor("wqt", [KT, 128, KT, 128], F32, kind="ExternalInput").ap()
    wot = nc.dram_tensor("wot", [KT, 128, KT, 128], F32, kind="ExternalInput").ap()
    wg_i = nc.dram_tensor("wg", [NE, IT, 128, KT, 128], F32, kind="ExternalInput").ap()
    wu_i = nc.dram_tensor("wu", [NE, IT, 128, KT, 128], F32, kind="ExternalInput").ap()
    wd_i = nc.dram_tensor("wd", [KT, 128, NE, IT, 128], F32, kind="ExternalInput").ap()
    wgt = nc.dram_tensor("wgt", [128, KT * 8], F32, kind="ExternalInput").ap()
    mem_i = nc.dram_tensor("mem", [128, NB * H * 128], F32, kind="ExternalInput").ap()
    norms_i = nc.dram_tensor("norms", [128, H * NB], F32, kind="ExternalInput").ap()
    bw_i = nc.dram_tensor("bw", [H, NB], F32, kind="ExternalInput").ap()
    ident_i = nc.dram_tensor("ident", [128, 128], F32, kind="ExternalInput").ap()
    out_d = nc.dram_tensor("out", [n_chunks, KT, 128, C], F32, kind="ExternalOutput").ap()

    wq_s = nc.dram_tensor("wq_s", [KT, 128, KT, 128], BF16, kind="Internal").ap()
    wo_s = nc.dram_tensor("wo_s", [KT, 128, KT, 128], BF16, kind="Internal").ap()
    wg_s = nc.dram_tensor("wg_s", [NE, IT, 128, KT, 128], BF16, kind="Internal").ap()
    wu_s = nc.dram_tensor("wu_s", [NE, IT, 128, KT, 128], BF16, kind="Internal").ap()
    wd_s = nc.dram_tensor("wd_s", [KT, 128, NE, IT, 128], BF16, kind="Internal").ap()

    from contextlib import ExitStack
    es = ExitStack()
    with es:
        sb = lambda name, shape, dt: es.enter_context(nc.sbuf_tensor(name, shape, dt))
        pool = sb("pool", [128, 24576], BF16)
        xbuf = sb("xbuf", [128, 8192], BF16)
        x1 = sb("x1", [128, KT * C], BF16)
        r2 = sb("r2", [128, KT * C], BF16)
        sigma = sb("sigma", [128, 2 * C], BF16)
        sq = sb("sq", [128, 2 * C], BF16)
        rsb = sb("rsb", [128, 2 * C], F32)
        outt = sb("outt", [128, 2 * C], F32)
        wqb = sb("wqb", [128, 2 * KT * 128], BF16)
        wob = sb("wob", [128, 2 * KT * 128], BF16)
        wgb = sb("wgb", [128, 2 * KT * 128], BF16)
        wub = sb("wub", [128, 2 * KT * 128], BF16)
        wdb = sb("wdb", [128, 2 * NE * IT * 128], BF16)
        memb = sb("memb", [128, NB * H * 128], BF16)
        normb = sb("normb", [128, H * NB], BF16)
        wgtb = sb("wgtb", [128, KT * 8], BF16)
        identb = sb("identb", [128, 128], BF16)
        identf = sb("identf", [128, 128], F32)
        onesc = sb("onesc", [128, 1], BF16)
        onescf = sb("onescf", [128, 1], F32)
        onesrf = sb("onesrf", [1, 128], F32)
        onesr = sb("onesr", [1, 128], BF16)
        normf = sb("normf", [128, H * NB], F32)
        bwf = sb("bwf", [H, NB], F32)
        bsc1 = sb("bsc1", [H, 3 * NB], F32)
        sumr = sb("sumr", [1, H * NB], F32)
        snr = sb("snr", [1, 2 * NB], F32)
        wflat = sb("wflat", [1, H * NB], F32)
        weff = sb("weff", [H, NB], F32)
        wbc = sb("wbc", [128, H * NB], F32)
        call = sb("call", [128, 2 * 16], F32)
        bsc = sb("bsc", [128, 2 * 512], BF16)
        pps = sb("pps", [128, 2 * 256], BF16)
        rsbb = sb("rsbb", [128, 512], BF16)
        apb = sb("apb", [128, 1024], BF16)
        ct = sb("ct", [128, 2 * NT * 128], BF16)
        rowsb = sb("rowsb", [1, 2 * C], F32)
        rows2 = sb("rows2", [1, 2 * C], F32)
        logitsb = sb("logitsb", [8, C], F32)
        rt = sb("rt", [128, NT * 16], F32)
        crow = sb("crow", [1, 2 * C], F32)

        psum = [es.enter_context(nc.psum_tensor(f"P{i}", [128, 512], F32))
                for i in range(8)]

        def mksem(name, step=16):
            return Sem(es.enter_context(nc.semaphore(name)), step)

        SSET = mksem("sset")
        BWS = mksem("bws")
        WFS = mksem("wfs")
        CVA = mksem("cva")
        CVB = mksem("cvb")
        GSET = mksem("gset")
        XS = mksem("xs")
        WQ = [mksem("wq0"), mksem("wq1")]
        WO = [mksem("wo0"), mksem("wo1")]
        WG = [mksem("wgs0"), mksem("wgs1")]
        WU = [mksem("wus0"), mksem("wus1")]
        WD = [mksem("wds0"), mksem("wds1")]
        CR = [mksem("cr0"), mksem("cr1")]
        OS = [mksem("os0"), mksem("os1")]
        PE = mksem("pe", 1)
        DV = mksem("dv", 1)
        AC = mksem("ac", 1)
        PL = mksem("pl", 1)
        TOTAL_SEMS = (SSET, BWS, WFS, CVA, CVB, GSET)
        ALL_DMA_SEMS = [SSET, BWS, WFS, CVA, CVB, GSET, XS] + WQ + WO + WG + WU + WD + CR + OS

        plan = {"t": [], "v": [], "s": [], "g": [], "y": []}
        SEM_OF = {"t": PE, "v": DV, "s": AC, "p": PL}
        PLAN_OF = {"t": "t", "v": "v", "s": "s", "p": "g"}
        QUEUE_PLAN = {"y": "y", "g": "g", "sd": "s"}
        EV = {}
        waited = {k: {} for k in plan}

        def _emit(ep, sem, build_fn, deps, event):
            cap = []
            hw = waited[ep]
            for d in deps:
                s, v = EV[d]
                if s in TOTAL_SEMS:
                    v = s.n
                key = id(s)
                if hw.get(key, -1) >= v:
                    continue
                hw[key] = v
                cap.append((s.h, v))
            if event is not None:
                sem.n += sem.step
                EV[event] = (sem, sem.n)

                def closure(e, cap=cap, build_fn=build_fn, sh=sem.h, st=sem.step):
                    for (h, v) in cap:
                        e.wait_ge(h, v)
                    build_fn(e).then_inc(sh, st)
            else:
                def closure(e, cap=cap, build_fn=build_fn):
                    for (h, v) in cap:
                        e.wait_ge(h, v)
                    build_fn(e)
            plan[ep].append(closure)

        def emit(eng, build_fn, deps=(), event=None):
            _emit(PLAN_OF[eng], SEM_OF[eng], build_fn, deps, event)

        def dma(q, sem, out_ap, in_ap, deps=(), event=None):
            _emit(QUEUE_PLAN[q], sem,
                  lambda e, o=out_ap, i=in_ap: e.dma_start(out=o, in_=i),
                  deps, event)

        dma("y", SSET, normf[:], norms_i, event="ld_normf")
        dma("y", BWS, bwf[:], bw_i, event="ld_bw")
        dma("y", SSET, identf[:], ident_i, event="ld_identf")
        dma("g", GSET, identb[:], ident_i, event="ld_identb")
        dma("g", GSET, wgtb[:], wgt, event="ld_wgt")
        dma("g", GSET, normb[:], norms_i, event="ld_normb")
        mem_stage = pool[:, 0:16384].bitcast(F32)
        dma("y", SSET, mem_stage, mem_i, event="ld_mem")

        emit("v", lambda e: e.memset(onesc[:], 1.0), event="ones1")
        emit("v", lambda e: e.memset(onesr[:], 1.0), event="ones2")
        emit("v", lambda e: e.memset(onesrf[:], 1.0), event="ones3")
        emit("v", lambda e: e.memset(onescf[:], 1.0), event="ones4")

        emit("v", lambda e: e.reduce_max(bsc1[:, 0:1], bwf[:], axis=AX.X),
             deps=["ld_bw"], event="bs_m")
        emit("v", lambda e: e.tensor_scalar(bsc1[:, NB:2 * NB], bwf[:],
                                            bsc1[:, 0:1], None, ALU.subtract),
             deps=["bs_m"], event="bs_d")
        emit("s", lambda e: e.activation(bsc1[:, NB:2 * NB], bsc1[:, NB:2 * NB],
                                         AF.Exp),
             deps=["bs_d"], event="bs_e")
        emit("v", lambda e: e.reduce_sum(bsc1[:, 1:2], bsc1[:, NB:2 * NB],
                                         axis=AX.X),
             deps=["bs_e"], event="bs_s")
        emit("v", lambda e: e.reciprocal(bsc1[:, 2:3], bsc1[:, 1:2]),
             deps=["bs_s"], event="bs_r")
        emit("v", lambda e: e.tensor_scalar(bsc1[:, 2 * NB:3 * NB],
                                            bsc1[:, NB:2 * NB], bsc1[:, 2:3],
                                            None, ALU.mult),
             deps=["bs_r"], event="bs_sm")
        emit("t", lambda t: t.matmul(psum[0][0:1, 0:H * NB], onescf[:], normf[:],
                                     start=True, stop=True),
             deps=["ld_normf", "ones4"], event="act_cs")
        emit("v", lambda e: e.tensor_copy(sumr[:], psum[0][0:1, 0:H * NB]),
             deps=["act_cs"], event="act_cp")
        sum_v = sumr[0:1, :].rearrange("p (h n) -> p n h", h=H, n=NB)
        emit("v", lambda e: e.tensor_reduce(snr[0:1, 0:NB], sum_v, axis=AX.X,
                                            op=ALU.add),
             deps=["act_cp"], event="act_sn")
        emit("v", lambda e: e.tensor_scalar(snr[0:1, NB:2 * NB], snr[0:1, 0:NB],
                                            float(EPS), None, ALU.is_ge),
             deps=["act_sn"], event="act_a")
        emit("t", lambda t: t.matmul(psum[1][0:H, 0:NB], onesrf[0:1, 0:H],
                                     snr[0:1, NB:2 * NB], start=True, stop=True),
             deps=["act_a", "ones3"], event="act_bc")
        emit("v", lambda e: e.tensor_tensor(weff[:], bsc1[:, 2 * NB:3 * NB],
                                            psum[1][0:H, 0:NB], ALU.mult),
             deps=["bs_sm", "act_bc"], event="weff_ok")
        for h_ in range(H):
            dma("y", WFS, wflat[0:1, h_ * NB:(h_ + 1) * NB], weff[h_:h_ + 1, :],
                deps=["weff_ok"], event=f"wflat{h_}")
        EV["wflat_ok"] = EV[f"wflat{H-1}"]
        emit("t", lambda t: t.matmul(psum[2][:, 0:H * NB], onesrf[:],
                                     wflat[0:1, :], start=True, stop=True),
             deps=["wflat_ok", "ones3"], event="wbc_mm")
        emit("v", lambda e: e.tensor_copy(wbc[:], psum[2][:, 0:H * NB]),
             deps=["wbc_mm"], event="wbc_ok")
        for n_ in range(NB):
            for h_ in range(H):
                blk = (h_ * NB + n_) * 128
                col = h_ * NB + n_
                emit("v", lambda e, blk=blk, col=col: e.tensor_scalar(
                    memb[:, blk:blk + 128], mem_stage[:, blk:blk + 128],
                    wbc[:, col:col + 1], None, ALU.mult),
                    deps=["wbc_ok", "ld_mem"],
                    event=("memb_ok" if (n_ == NB - 1 and h_ == H - 1) else None))

        dma("g", XS, xbuf[:, :], xt[0].rearrange("p k t -> p (k t)"),
            deps=[], event="xld@0")

        dma("g", CVA, wq_s, wqt, event="cv_wq")
        dma("g", CVA, wo_s, wot, event="cv_wo")
        EV["cvA"] = EV["cv_wo"]
        for e_ in range(NE):
            dma("g", CVB, wg_s[e_], wg_i[e_], event=f"cv_wg{e_}")
            dma("g", CVB, wu_s[e_], wu_i[e_], event=f"cv_wu{e_}")
        for o_ in range(KT):
            dma("g", CVB, wd_s[o_], wd_i[o_], event=f"cv_wd{o_}")
        EV["cvB"] = EV[f"cv_wd{KT-1}"]

        x_bf = xbuf[:, :]
        comb = pool[:, 0:8192]
        r1 = pool[:, 8192:16384]
        pend_bc2 = []

        for c in range(n_chunks):
            P = c > 0


            def load_wq(h_):
                par = h_ % 2
                deps = [] if P else ["cvA"]
                if P or h_ >= 2:
                    pc, ph = (c, h_ - 2) if h_ >= 2 else (c - 1, h_ + 14)
                    deps.append(f"q{ph}@{pc}")
                dma("sd", WQ[par], wqb[:, par * 2048:(par + 1) * 2048],
                    wq_s[h_].rearrange("p k o -> p (k o)"),
                    deps=deps, event=f"wqld{h_}@{c}")

            def load_wo(o_):
                par = o_ % 2
                deps = [] if P else ["cvA"]
                if P or o_ >= 2:
                    pc, po = (c, o_ - 2) if o_ >= 2 else (c - 1, o_ + 14)
                    deps.append(f"attn{po}@{pc}")
                dma("sd", WO[par], wob[:, par * 2048:(par + 1) * 2048],
                    wo_s[o_].rearrange("p k o -> p (k o)"),
                    deps=deps, event=f"wold{o_}@{c}")

            def load_wgu(pidx):
                e_, it = divmod(pidx, IT)
                par = pidx % 2
                deps, deps2 = [], []
                if not P:
                    deps.append("cvB")
                    deps2.append("cvB")
                if P or pidx >= 2:
                    qidx, pc = pidx - 2, c
                    if qidx < 0:
                        qidx, pc = qidx + NE * IT, c - 1
                    pe_, pit = divmod(qidx, IT)
                    deps.append(f"g{pe_}_{pit}@{pc}")
                    deps2.append(f"u{pe_}_{pit}@{pc}")
                dma("sd", WG[par], wgb[:, par * 2048:(par + 1) * 2048],
                    wg_s[e_, it].rearrange("p k o -> p (k o)"),
                    deps=deps, event=f"wgld{e_}_{it}@{c}")
                dma("sd", WU[par], wub[:, par * 2048:(par + 1) * 2048],
                    wu_s[e_, it].rearrange("p k o -> p (k o)"),
                    deps=deps2, event=f"wuld{e_}_{it}@{c}")

            def load_wd(o_):
                par = o_ % 2
                deps = [] if P else ["cvB"]
                if P or o_ >= 2:
                    pc, po = (c, o_ - 2) if o_ >= 2 else (c - 1, o_ + 14)
                    deps.append(f"y{po}@{pc}")
                dma("y", WD[par], wdb[:, par * 6144:(par + 1) * 6144],
                    wd_s[o_].rearrange("p e i o -> p (e i o)"),
                    deps=deps, event=f"wdld{o_}@{c}")

            load_wq(0)
            load_wq(1)

            def q_head(h_):
                par = h_ % 2
                deps = [f"xld@{c}", f"wqld{h_}@{c}"]
                if P or h_ >= 2:
                    pc, ph = (c, h_ - 2) if h_ >= 2 else (c - 1, h_ + 14)
                    deps.append(f"sigrd{ph}@{pc}")
                    deps.append(f"smin{ph}@{pc}")
                if P and h_ < 2:
                    deps.append(f"sg{NE-1}_{IT-3}@{c-1}" if h_ == 0
                                else f"up{NE-1}_{IT-3}@{c-1}")
                if not P and h_ == 0:
                    deps.append("act_cp")
                if not P and h_ == 1:
                    deps.append("weff_ok")

                def fn(t, par=par):
                    mm = None
                    for kt in range(KT):
                        mm = t.matmul(
                            psum[par][:, 0:512],
                            wqb[:, par * 2048 + kt * 128:par * 2048 + (kt + 1) * 128],
                            x_bf[:, kt * 512:(kt + 1) * 512],
                            start=(kt == 0), stop=(kt == KT - 1))
                    return mm
                emit("t", fn, deps=deps, event=f"q{h_}@{c}")

            def sigma_head(h_):
                par = h_ % 2
                deps = [f"q{h_}@{c}"]
                if h_ > 0:
                    deps.append(f"sig{h_-1}@{c}")
                elif P:
                    deps += [f"st4_{KT-1}@{c-1}", f"sig{H-1}@{c-1}"]
                emit("v", lambda e, par=par: e.tensor_scalar(
                    sq[:, 0:512], psum[par][:, 0:512], 0.0, None, ALU.min),
                    deps=deps, event=f"smin{h_}@{c}")
                emit("s", lambda e: e.activation(sq[:, 0:512], sq[:, 0:512],
                                                 AF.Exp),
                     deps=[f"smin{h_}@{c}"], event=f"sexp{h_}@{c}")
                rdeps = [f"q{h_}@{c}", f"smin{h_}@{c}"]
                if h_ > 0:
                    rdeps.append(f"sig{h_-1}@{c}")
                elif P:
                    rdeps.append(f"sig{H-1}@{c-1}")
                emit("s", lambda e, par=par: e.activation(
                    sq[:, 512:1024], psum[par][:, 0:512], AF.Relu),
                    deps=rdeps, event=f"srelu{h_}@{c}")
                sdeps = [f"sexp{h_}@{c}", f"srelu{h_}@{c}"]
                if P or h_ >= 2:
                    pc, ph = (c, h_ - 2) if h_ >= 2 else (c - 1, h_ + 14)
                    sdeps.append(f"bank{ph}@{pc}")
                emit("v", lambda e, par=par: e.tensor_tensor(
                    sigma[:, par * 512:(par + 1) * 512], sq[:, 0:512],
                    sq[:, 512:1024], ALU.add),
                    deps=sdeps, event=f"sig{h_}@{c}")
                EV[f"sigrd{h_}@{c}"] = EV[f"srelu{h_}@{c}"]

            def denoms_head(h_):
                par = h_ % 2
                deps = [f"sig{h_}@{c}"]
                if P or h_ >= 2:
                    pc2, ph2 = (c, h_ - 2) if h_ >= 2 else (c - 1, h_ + 14)
                    deps.append(f"cp2_{ph2}@{pc2}")
                if P or h_ >= 4:
                    pc4, ph4 = (c, h_ - 4) if h_ >= 4 else (c - 1, h_ + 12)
                    deps += [f"evac{ph4}_{t_}@{pc4}" for t_ in range(NT)]
                if P and h_ < 2:
                    if h_ == 0:
                        deps += [f"l2@{c-1}_mean", f"R2sb@{c-1}"]
                    else:
                        deps += [f"l2@{c-1}_msq", f"M2sb@{c-1}"]
                if not P and h_ == 0:
                    deps.append("ld_normb")

                def fn(t, h_=h_, par=par):
                    mm = None
                    for t_ in range(NT):
                        mm = t.matmul(
                            psum[6 + par][:, t_ * 4:(t_ + 1) * 4],
                            sigma[:, par * 512 + t_ * 128:par * 512 + (t_ + 1) * 128],
                            normb[:, h_ * NB:(h_ + 1) * NB],
                            start=True, stop=True)
                    return mm
                emit("t", fn, deps=deps, event=f"dn{h_}@{c}")
                cdeps = [f"dn{h_}@{c}"]
                if h_ >= 2:
                    cdeps.append(f"scl{h_-2}_{NT-1}@{c}")
                elif P:
                    cdeps.append(f"scl{H-2+h_}_{NT-1}@{c-1}")
                emit("v", lambda e, par=par: e.tensor_scalar(
                    call[:, par * 16:(par + 1) * 16], psum[6 + par][:, 0:16],
                    float(EPS), None, ALU.max),
                    deps=cdeps, event=f"cprep{h_}@{c}")
                emit("v", lambda e, par=par: e.reciprocal(
                    call[:, par * 16:(par + 1) * 16],
                    call[:, par * 16:(par + 1) * 16]),
                    deps=[f"cprep{h_}@{c}"], event=f"cp2_{h_}@{c}")

            def bank_head(h_):
                par = h_ % 2
                deps = [f"sig{h_}@{c}"]
                if P or h_ >= 1:
                    pc, ph = (c, h_ - 1) if h_ >= 1 else (c - 1, H - 1)
                    for t_ in range(NT):
                        deps.append(f"scl{ph}_{t_}@{pc}")
                if h_ <= 1:
                    if P:
                        if h_ == 0:
                            deps += [f"sg{NE-1}_{IT-1}@{c-1}",
                                     f"up{NE-1}_{IT-1}@{c-1}",
                                     f"sg{NE-1}_{IT-2}@{c-1}",
                                     f"up{NE-1}_{IT-2}@{c-1}",
                                     f"r2_{KT-2}@{c-1}", f"r2_{KT-1}@{c-1}"]
                    else:
                        if h_ == 0:
                            deps += ["wbc_ok", "memb_ok"]

                def fn(t, h_=h_, par=par):
                    mm = None
                    for t_ in range(NT):
                        mm = t.matmul(
                            psum[2 + t_][:, 0:512],
                            sigma[:, par * 512 + t_ * 128:par * 512 + (t_ + 1) * 128],
                            memb[:, h_ * 512:(h_ + 1) * 512],
                            start=True, stop=True)
                    return mm
                emit("t", fn, deps=deps, event=f"bank{h_}@{c}")

                for t_ in range(NT):
                    sp = par * 512
                    cbv = call[:, par * 16 + t_ * 4:par * 16 + (t_ + 1) * 4]
                    Av = psum[2 + t_][:, 0:512].rearrange(
                        "p (n e) -> p n e", n=NB, e=128)
                    Sv = bsc[:, sp:sp + 512].rearrange(
                        "p (n e) -> p n e", n=NB, e=128)
                    sd = [f"cp2_{h_}@{c}", f"bank{h_}@{c}"]
                    if t_ > 0:
                        sd.append(f"pa{h_}_{t_-1}@{c}")
                    elif h_ >= 2:
                        sd.append(f"pa{h_-2}_{NT-1}@{c}")
                    elif P:
                        sd.append(f"pa{H-2+h_}_{NT-1}@{c-1}")
                    emit("v", lambda e, Av=Av, Sv=Sv, cbv=cbv: e.tensor_tensor(
                        Sv, Av, cbv.unsqueeze(2).broadcast_to([128, NB, 128]),
                        ALU.mult),
                        deps=sd, event=f"scl{h_}_{t_}@{c}")
                    ppo = par * 256
                    pad = [f"scl{h_}_{t_}@{c}"]
                    if t_ > 0:
                        pad.append(f"comb{h_}_{t_-1}@{c}")
                    elif h_ >= 2:
                        pad.append(f"comb{h_-2}_{NT-1}@{c}")
                    elif P:
                        pad.append(f"comb{H-2+h_}_{NT-1}@{c-1}")
                    emit("p", lambda e, sp=sp, ppo=ppo: e.tensor_tensor(
                        pps[:, ppo:ppo + 256].rearrange("p (n e) -> p n e",
                                                        n=2, e=128),
                        bsc[:, sp:sp + 256].rearrange("p (n e) -> p n e",
                                                      n=2, e=128),
                        bsc[:, sp + 256:sp + 512].rearrange(
                            "p (n e) -> p n e", n=2, e=128), ALU.add),
                        deps=pad, event=f"pa{h_}_{t_}@{c}")
                    cto = par * 512 + t_ * 128
                    combd = [f"pa{h_}_{t_}@{c}"]
                    if h_ >= 2:
                        combd.append(f"tr{h_-2}_{t_}@{c}")
                    elif P:
                        combd.append(f"tr{H-2+h_}_{t_}@{c-1}")
                    emit("p", lambda e, cto=cto, ppo=ppo: e.tensor_tensor(
                        ct[:, cto:cto + 128], pps[:, ppo:ppo + 128],
                        pps[:, ppo + 128:ppo + 256], ALU.add),
                        deps=combd, event=f"comb{h_}_{t_}@{c}")

            def bank_trs(h_):
                par = h_ % 2
                pv = psum[6 + par].ap().bitcast(BF16)
                for t_ in range(NT):
                    trd = [f"comb{h_}_{t_}@{c}", "ld_identb", f"cp2_{h_}@{c}"]
                    if h_ + 2 < H:
                        trd.append(f"cp2_{h_+2}@{c}")
                    if h_ >= 2:
                        trd.append(f"evac{h_-2}_{t_}@{c}")
                    elif P:
                        trd.append(f"evac{h_+14}_{t_}@{c-1}")
                    emit("t", lambda t, t_=t_, par=par, pv=pv: t.transpose(
                        pv[:, 512 + t_ * 128:512 + (t_ + 1) * 128],
                        ct[:, par * 512 + t_ * 128:par * 512 + (t_ + 1) * 128],
                        identb[:]),
                        deps=trd, event=f"tr{h_}_{t_}@{c}")
                for t_ in range(NT):
                    emit("s", lambda e, t_=t_, h_=h_, pv=pv: e.activation(
                        comb[:, h_ * 512 + t_ * 128:h_ * 512 + (t_ + 1) * 128],
                        pv[:, 512 + t_ * 128:512 + (t_ + 1) * 128], AF.Copy),
                        deps=[f"tr{h_}_{NT-1}@{c}"], event=f"evac{h_}_{t_}@{c}")

            q_head(0)
            load_wq(2)
            sigma_head(0)
            q_head(1)
            load_wq(3)
            sigma_head(1)
            tail_pair = None
            if pend_bc2:
                th_, tail_pair = pend_bc2.pop(0)
                th_()
            denoms_head(0)
            for h_ in range(H):
                if tail_pair is not None and h_ < KT:
                    tail_pair(h_)
                bank_head(h_)
                if h_ + 1 < H:
                    denoms_head(h_ + 1)
                if h_ + 2 < H:
                    q_head(h_ + 2)
                if h_ + 4 < H:
                    load_wq(h_ + 4)
                if h_ == 0:
                    load_wo(0)
                    load_wo(1)
                if h_ + 2 < H:
                    sigma_head(h_ + 2)
                if h_ >= 1:
                    bank_trs(h_ - 1)
            bank_trs(H - 1)

            comb_done = [f"evac{H-1}_{t_}@{c}" for t_ in range(NT)]
            for o_ in range(KT):
                par = o_ % 2
                deps = comb_done + [f"wold{o_}@{c}"]
                if o_ < 2:
                    deps.append(f"sigrd{14 + o_}@{c}")
                else:
                    deps.append(f"r1_{o_-2}@{c}")

                def fn(t, par=par):
                    mm = None
                    for kt in range(KT):
                        mm = t.matmul(
                            psum[par][:, 0:512],
                            wob[:, par * 2048 + kt * 128:par * 2048 + (kt + 1) * 128],
                            comb[:, kt * 512:(kt + 1) * 512],
                            start=(kt == 0), stop=(kt == KT - 1))
                    return mm
                emit("t", fn, deps=deps, event=f"attn{o_}@{c}")
                if o_ + 2 < KT:
                    load_wo(o_ + 2)
                emit("v", lambda e, o_=o_, par=par: e.tensor_tensor(
                    r1[:, o_ * 512:(o_ + 1) * 512],
                    x_bf[:, o_ * 512:(o_ + 1) * 512],
                    psum[par][:, 0:512], ALU.add),
                    deps=[f"attn{o_}@{c}"], event=f"r1_{o_}@{c}")
                emit("v", lambda e, o_=o_: e.tensor_tensor(
                    sq[:, (o_ % 2) * 512:(o_ % 2 + 1) * 512],
                    r1[:, o_ * 512:(o_ + 1) * 512],
                    r1[:, o_ * 512:(o_ + 1) * 512], ALU.mult),
                    deps=[f"r1_{o_}@{c}"] +
                         ([f"st2_{o_-2}@{c}"] if o_ >= 2 else []),
                    event=f"sq_{o_}@{c}")
                def emit_st12(o_):
                    sdeps = [f"r1_{o_}@{c}"]
                    if o_ == 0:
                        sdeps += [f"scl{H-1}_0@{c}"]
                    emit("t", lambda t, o_=o_: t.matmul(
                        psum[2][0:1, 0:512], onesc[:],
                        r1[:, o_ * 512:(o_ + 1) * 512],
                        start=(o_ == 0), stop=(o_ == KT - 1)),
                        deps=sdeps, event=f"st1_{o_}@{c}")
                    s2deps = [f"sq_{o_}@{c}"]
                    if o_ == 0:
                        s2deps += [f"scl{H-1}_1@{c}"]
                    emit("t", lambda t, o_=o_: t.matmul(
                        psum[3][0:1, 0:512], onesc[:],
                        sq[:, (o_ % 2) * 512:(o_ % 2 + 1) * 512],
                        start=(o_ == 0), stop=(o_ == KT - 1)),
                        deps=s2deps, event=f"st2_{o_}@{c}")
                if o_ >= 1:
                    emit_st12(o_ - 1)
                if o_ == KT - 1:
                    emit_st12(o_)

            def ln_rows(tag, sum_bank, sq_bank, dep_sum, dep_sq, war_deps):
                emit("v", lambda e: e.tensor_scalar(
                    rows2[0:1, 0:512], psum[sum_bank][0:1, 0:512], LN_INV, None,
                    ALU.mult), deps=[dep_sum], event=f"{tag}_mean")
                emit("v", lambda e: e.tensor_scalar(
                    rows2[0:1, 512:1024], psum[sq_bank][0:1, 0:512], LN_INV,
                    None, ALU.mult), deps=[dep_sq], event=f"{tag}_msq")
                emit("v", lambda e: e.tensor_tensor(
                    rowsb[0:1, 0:512], rows2[0:1, 0:512], rows2[0:1, 0:512],
                    ALU.mult), deps=[f"{tag}_mean"] + war_deps,
                    event=f"{tag}_m2")
                emit("v", lambda e: e.tensor_tensor(
                    rowsb[0:1, 0:512], rows2[0:1, 512:1024], rowsb[0:1, 0:512],
                    ALU.subtract), deps=[f"{tag}_m2", f"{tag}_msq"],
                    event=f"{tag}_var")
                emit("v", lambda e: e.tensor_scalar(
                    rowsb[0:1, 0:512], rowsb[0:1, 0:512], LNEPS, None, ALU.add),
                    deps=[f"{tag}_var"], event=f"{tag}_vare")
                emit("s", lambda e: e.activation(
                    rowsb[0:1, 0:512], rowsb[0:1, 0:512], AF.Ln),
                    deps=[f"{tag}_vare"], event=f"{tag}_sd")
                emit("s", lambda e: e.activation(
                    rowsb[0:1, 0:512], rowsb[0:1, 0:512], AF.Exp, scale=-0.5),
                    deps=[f"{tag}_sd"], event=f"{tag}_rstd")
                emit("v", lambda e: e.tensor_tensor(
                    rowsb[0:1, 512:1024], rows2[0:1, 0:512], rowsb[0:1, 0:512],
                    ALU.mult), deps=[f"{tag}_rstd"] + war_deps,
                    event=f"{tag}_mr")

            war1 = [f"bcR2@{c-1}", f"bcM2@{c-1}"] if P else []
            ln_rows(f"l1@{c}", 2, 3, f"st1_{KT-1}@{c}", f"st2_{KT-1}@{c}", war1)
            emit("t", lambda t: t.matmul(psum[4][:, 0:512], onesrf[:],
                                         rowsb[0:1, 0:512], start=True,
                                         stop=True),
                 deps=[f"l1@{c}_rstd", f"scl{H-1}_2@{c}"] +
                      ([f"r2_{KT-2}@{c-1}"] if P else []),
                 event=f"bcR@{c}")
            emit("t", lambda t: t.matmul(psum[5][:, 0:512], onesrf[:],
                                         rowsb[0:1, 512:1024], start=True,
                                         stop=True),
                 deps=[f"l1@{c}_mr", f"scl{H-1}_3@{c}"] +
                      ([f"r2_{KT-1}@{c-1}"] if P else []),
                 event=f"bcM@{c}")
            emit("v", lambda e: e.tensor_copy(rsbb[:, 0:512],
                                              psum[4][:, 0:512]),
                 deps=[f"bcR@{c}"], event=f"Rsb@{c}")
            emit("v", lambda e: e.tensor_copy(rsb[:, 512:1024],
                                              psum[5][:, 0:512]),
                 deps=[f"bcM@{c}"], event=f"Msb@{c}")
            for kt in range(KT):
                eng = "v" if kt % 2 == 0 else "p"
                apd = [f"Rsb@{c}"]
                if kt >= 2:
                    apd.append(f"x1_{kt-2}@{c}")
                elif P:
                    apd.append(f"o_{14 + kt}@{c-1}")
                emit(eng, lambda e, kt=kt: e.tensor_tensor(
                    apb[:, (kt % 2) * 512:(kt % 2 + 1) * 512],
                    r1[:, kt * 512:(kt + 1) * 512], rsbb[:, 0:512], ALU.mult),
                    deps=apd, event=f"ap1_{kt}@{c}")
                x1d = [f"ap1_{kt}@{c}", f"Msb@{c}"]
                if P:
                    x1d += [f"g{NE-1}_{IT-1}@{c-1}", f"u{NE-1}_{IT-1}@{c-1}",
                            f"r2_{kt}@{c-1}", f"lg@{c-1}"]
                emit(eng, lambda e, kt=kt: e.tensor_tensor(
                    x1[:, kt * 512:(kt + 1) * 512],
                    apb[:, (kt % 2) * 512:(kt % 2 + 1) * 512],
                    rsb[:, 512:1024], ALU.subtract),
                    deps=x1d, event=f"x1_{kt}@{c}")

            PDEPTH = 3
            def ffn_mms(idx):
                e_, it = divmod(idx, IT)
                gpar = idx % PDEPTH
                gb, ub = 2 * gpar, 2 * gpar + 1
                woff = (idx % 2) * 2048
                deps = [f"wgld{e_}_{it}@{c}"]
                if idx >= PDEPTH:
                    pe_, pit = divmod(idx - PDEPTH, IT)
                    deps.append(f"hp{pe_}_{pit}@{c}")
                else:
                    deps += [f"ap1_{KT-1}@{c}"]
                    if gpar == 0:
                        deps += [f"r1_{KT-2}@{c}", f"r1_{KT-1}@{c}"]

                ktw = None
                if idx < PDEPTH:
                    ktw = [EV[f"x1_{kt}@{c}"] for kt in range(KT)]

                def fng(t, gb=gb, woff=woff, ktw=ktw):
                    mm = None
                    for kt in range(KT):
                        if ktw is not None:
                            s, v = ktw[kt]
                            t.wait_ge(s.h, v)
                        mm = t.matmul(
                            psum[gb][:, 0:512],
                            wgb[:, woff + kt * 128:woff + (kt + 1) * 128],
                            x1[:, kt * 512:(kt + 1) * 512],
                            start=(kt == 0), stop=(kt == KT - 1))
                    return mm
                emit("t", fng, deps=deps, event=f"g{e_}_{it}@{c}")
                deps_u = [f"wuld{e_}_{it}@{c}"]
                if idx >= PDEPTH:
                    pe_, pit = divmod(idx - PDEPTH, IT)
                    deps_u.append(f"hp{pe_}_{pit}@{c}")

                def fnu(t, ub=ub, woff=woff):
                    mm = None
                    for kt in range(KT):
                        mm = t.matmul(
                            psum[ub][:, 0:512],
                            wub[:, woff + kt * 128:woff + (kt + 1) * 128],
                            x1[:, kt * 512:(kt + 1) * 512],
                            start=(kt == 0), stop=(kt == KT - 1))
                    return mm
                emit("t", fnu, deps=deps_u, event=f"u{e_}_{it}@{c}")
                if idx + 2 < NE * IT:
                    load_wgu(idx + 2)

            def ffn_cons(idx):
                e_, it = divmod(idx, IT)
                gpar = idx % PDEPTH
                gb, ub = 2 * gpar, 2 * gpar + 1
                spar = idx % 2
                sgd = [f"g{e_}_{it}@{c}"]
                if idx >= 2:
                    pe_, pit = divmod(idx - 2, IT)
                    sgd.append(f"hp{pe_}_{pit}@{c}")
                else:
                    sgd.append(f"st2_{14 + idx}@{c}")
                emit("s", lambda e, gb=gb, spar=spar: e.activation(
                    sq[:, spar * 512:(spar + 1) * 512], psum[gb][:, 0:512],
                    AF.Silu),
                    deps=sgd, event=f"sg{e_}_{it}@{c}")
                upd = [f"u{e_}_{it}@{c}", f"Bev{e_}@{c}"]
                if idx >= 1:
                    pe_, pit = divmod(idx - 1, IT)
                    upd.append(f"hp{pe_}_{pit}@{c}")
                emit("v", lambda e, ub=ub, e_=e_: e.tensor_tensor(
                    outt[:, 0:512], psum[ub][:, 0:512],
                    r2[:, e_ * 512:(e_ + 1) * 512], ALU.mult),
                    deps=upd, event=f"up{e_}_{it}@{c}")
                emit("v", lambda e, e_=e_, it=it, spar=spar: e.tensor_tensor(
                    pool[:, e_ * 3072 + it * 512:e_ * 3072 + (it + 1) * 512],
                    sq[:, spar * 512:(spar + 1) * 512], outt[:, 0:512],
                    ALU.mult),
                    deps=[f"sg{e_}_{it}@{c}", f"up{e_}_{it}@{c}"],
                    event=f"hp{e_}_{it}@{c}")


            if c + 1 < n_chunks:
                dma("g", XS, x_bf, xt[c + 1].rearrange("p k t -> p (k t)"),
                    deps=[f"r1_{KT-1}@{c}", f"q{H-1}@{c}"],
                    event=f"xld@{c+1}")

            x1all = [f"x1_{kt}@{c}" for kt in range(KT)]

            lgw = [EV[f"x1_{kt}@{c}"] for kt in range(KT)]

            def fn_logits(t, lgw=lgw):
                mm = None
                for kt in range(KT):
                    s, v = lgw[kt]
                    t.wait_ge(s.h, v)
                    mm = t.matmul(psum[6][0:8, 0:512],
                                  wgtb[:, kt * 8:(kt + 1) * 8],
                                  x1[:, kt * 512:(kt + 1) * 512],
                                  start=(kt == 0), stop=(kt == KT - 1))
                return mm
            emit("t", fn_logits,
                 deps=["ld_wgt", f"cp2_{H-2}@{c}", f"cp2_{H-1}@{c}"] +
                      [f"evac{H-2}_{t_}@{c}" for t_ in range(NT)],
                 event=f"lg@{c}")
            lgsbd = [f"lg@{c}"]
            if P:
                lgsbd.append(f"crow{NE-1}@{c-1}")
            emit("v", lambda e: e.tensor_copy(logitsb[:], psum[6][0:8, 0:512]),
                 deps=lgsbd, event=f"lgsb@{c}")
            for t_ in range(NT):
                emit("t", lambda t, t_=t_: t.transpose(
                    psum[7][:, t_ * 8:(t_ + 1) * 8],
                    logitsb[:, t_ * 128:(t_ + 1) * 128], identf[0:8, 0:8]),
                    deps=[f"lgsb@{c}", "ld_identf", f"cp2_{H-1}@{c}"] +
                         [f"evac{H-1}_{t2_}@{c}" for t2_ in range(NT)],
                    event=f"lgt{t_}@{c}")
            load_wgu(0)
            load_wgu(1)
            ffn_mms(0)
            ffn_mms(1)
            ffn_mms(2)
            Lv = psum[7][:, 0:32].rearrange("p (t e) -> p t e", t=NT, e=8)
            dv_ = outt[:, 0:32].rearrange("p (t e) -> p t e", t=NT, e=8)
            mk_ = rt[:, 32:64].rearrange("p (t e) -> p t e", t=NT, e=8)
            cbv = outt[:, 32:64].rearrange("p (t e) -> p t e", t=NT, e=8)

            def bc4(ap):
                return ap.unsqueeze(2).broadcast_to([128, NT, 8])
            rmd = [f"lgt{NT-1}@{c}", f"x1_{KT-1}@{c}", f"x1_{KT-2}@{c}"]
            if P:
                rmd += [f"ctp{NT-1}@{c-1}", f"os_{KT-1}@{c-1}",
                        f"os_{KT-2}@{c-1}"]
            emit("v", lambda e: e.reduce_max(rt[:, 0:NT], Lv, axis=AX.X),
                 deps=rmd, event=f"rm1@{c}")
            emit("v", lambda e: e.tensor_tensor(dv_, Lv, bc4(rt[:, 0:NT]),
                                                ALU.subtract),
                 deps=[f"rm1@{c}"], event=f"rd@{c}")
            emit("v", lambda e: e.tensor_scalar(mk_, dv_, 0.0, None, ALU.is_ge),
                 deps=[f"rd@{c}"], event=f"rmask@{c}")
            emit("v", lambda e: e.scalar_tensor_tensor(mk_, mk_, -1e30, dv_,
                                                       ALU.mult, ALU.add),
                 deps=[f"rmask@{c}"], event=f"rmsk2@{c}")
            emit("v", lambda e: e.reduce_max(rt[:, NT:2 * NT], mk_, axis=AX.X),
                 deps=[f"rmsk2@{c}"], event=f"rm2@{c}")
            emit("s", lambda e: e.activation(dv_, dv_, AF.Exp),
                 deps=[f"rm2@{c}"], event=f"re1@{c}")
            emit("s", lambda e: e.activation(rt[:, 2 * NT:3 * NT],
                                             rt[:, NT:2 * NT], AF.Exp),
                 deps=[f"rm2@{c}"], event=f"re2@{c}")
            emit("v", lambda e: e.tensor_scalar(rt[:, 3 * NT:4 * NT],
                                                rt[:, 2 * NT:3 * NT], 1.0, None,
                                                ALU.add),
                 deps=[f"re2@{c}"], event=f"rden@{c}")
            emit("v", lambda e: e.reciprocal(rt[:, 4 * NT:5 * NT],
                                             rt[:, 3 * NT:4 * NT]),
                 deps=[f"rden@{c}"], event=f"rrec@{c}")
            emit("v", lambda e: e.tensor_tensor(mk_, dv_,
                                                bc4(rt[:, 2 * NT:3 * NT]),
                                                ALU.is_ge),
                 deps=[f"re1@{c}", f"re2@{c}"], event=f"rtm@{c}")
            emit("v", lambda e: e.tensor_tensor(dv_, dv_, mk_, ALU.mult),
                 deps=[f"rtm@{c}"], event=f"rcm1@{c}")
            cbd = [f"rcm1@{c}", f"rrec@{c}"]
            if P:
                cbd.append(f"ctp{NT-1}@{c-1}")
            emit("v", lambda e: e.tensor_tensor(cbv, dv_,
                                                bc4(rt[:, 4 * NT:5 * NT]),
                                                ALU.mult),
                 deps=cbd, event=f"rcb@{c}")
            for t_ in range(NT):
                emit("t", lambda t, t_=t_: t.transpose(
                    psum[6][0:8, t_ * 128:(t_ + 1) * 128],
                    outt[:, 32 + t_ * 8:32 + (t_ + 1) * 8], identf[:]),
                    deps=[f"rcb@{c}", f"lgsb@{c}"], event=f"ctp{t_}@{c}")
            emit("v", lambda e: e.tensor_copy(logitsb[:], psum[6][0:8, 0:512]),
                 deps=[f"ctp{t_}@{c}" for t_ in range(NT)], event=f"cmrows@{c}")
            for e_ in range(NE):
                par = e_ % 2
                cdp = [f"cmrows@{c}"]
                if e_ >= 2:
                    cdp.append(f"Bmm{e_-2}@{c}")
                dma("y", CR[par], crow[0:1, par * 512:(par + 1) * 512],
                    logitsb[e_:e_ + 1, :], deps=cdp, event=f"crow{e_}@{c}")
                bmd = [f"crow{e_}@{c}", "ones3"]
                if e_ >= 2:
                    bmd.append(f"Bev{e_-2}@{c}")
                else:
                    bmd += [f"rm2@{c}", f"rcb@{c}"]
                    bmd += [f"ctp{t_}@{c}" for t_ in range(NT)]
                emit("t", lambda t, par=par: t.matmul(
                    psum[6 + par][:, 0:512], onesrf[:],
                    crow[0:1, par * 512:(par + 1) * 512], start=True, stop=True),
                    deps=bmd, event=f"Bmm{e_}@{c}")
                bed = [f"Bmm{e_}@{c}"]
                if P:
                    bed.append(f"ap2_{e_}@{c-1}")
                emit("v", lambda e, e_=e_, par=par: e.tensor_copy(
                    r2[:, e_ * 512:(e_ + 1) * 512], psum[6 + par][:, 0:512]),
                    deps=bed, event=f"Bev{e_}@{c}")

            ffn_cons(0)
            ffn_cons(1)
            ffn_cons(2)
            for idx in range(3, NE * IT):
                ffn_mms(idx)
                ffn_cons(idx)

            load_wd(0)
            load_wd(1)
            hall = [f"hp{e_}_{it}@{c}" for e_ in range(NE) for it in range(IT)]
            for o_ in range(KT):
                par = o_ % 2
                deps = [f"wdld{o_}@{c}"]
                if o_ == 0:
                    deps += hall + [f"Rsb@{c}", f"Msb@{c}",
                                    f"sg{NE-1}_{IT-1}@{c}",
                                    f"up{NE-1}_{IT-1}@{c}"]
                elif o_ == 1:
                    pass
                if o_ >= 2:
                    deps.append(f"r2_{o_-2}@{c}")

                def fny(t, par=par):
                    mm = None
                    for e_ in range(NE):
                        for it in range(IT):
                            mm = t.matmul(
                                psum[4 + par][:, 0:512],
                                wdb[:, par * 6144 + (e_ * IT + it) * 128:
                                    par * 6144 + (e_ * IT + it + 1) * 128],
                                pool[:, e_ * 3072 + it * 512:
                                     e_ * 3072 + (it + 1) * 512],
                                start=(e_ == 0 and it == 0),
                                stop=(e_ == NE - 1 and it == IT - 1))
                    return mm
                emit("t", fny, deps=deps, event=f"y{o_}@{c}")
                if o_ + 2 < KT:
                    load_wd(o_ + 2)
                r2d = [f"y{o_}@{c}"]
                if o_ < NE:
                    r2d += [f"up{o_}_{it}@{c}" for it in range(IT)]
                emit("v", lambda e, o_=o_, par=par: e.tensor_tensor(
                    r2[:, o_ * 512:(o_ + 1) * 512],
                    x1[:, o_ * 512:(o_ + 1) * 512],
                    psum[4 + par][:, 0:512], ALU.add),
                    deps=r2d, event=f"r2_{o_}@{c}")
                emit("v", lambda e, o_=o_: e.tensor_tensor(
                    sq[:, (o_ % 2) * 512:(o_ % 2 + 1) * 512],
                    r2[:, o_ * 512:(o_ + 1) * 512],
                    r2[:, o_ * 512:(o_ + 1) * 512], ALU.mult),
                    deps=[f"r2_{o_}@{c}"] +
                         ([f"st4_{o_-2}@{c}"] if o_ >= 2 else
                          [f"sg{NE-1}_{IT-1}@{c}"]),
                    event=f"sq2_{o_}@{c}")
                def emit_st34(o_):
                    s3d = [f"r2_{o_}@{c}"]
                    if o_ == 0:
                        s3d += [f"Bev{NE-2}@{c}", f"Bev{NE-1}@{c}",
                                f"rm2@{c}", f"rcb@{c}", f"cmrows@{c}"]
                    emit("t", lambda t, o_=o_: t.matmul(
                        psum[6][0:1, 0:512], onesc[:],
                        r2[:, o_ * 512:(o_ + 1) * 512],
                        start=(o_ == 0), stop=(o_ == KT - 1)),
                        deps=s3d, event=f"st3_{o_}@{c}")
                    s4d = [f"sq2_{o_}@{c}"]
                    if o_ == 0:
                        s4d.append(f"Bev{NE-1}@{c}")
                    emit("t", lambda t, o_=o_: t.matmul(
                        psum[7][0:1, 0:512], onesc[:],
                        sq[:, (o_ % 2) * 512:(o_ % 2 + 1) * 512],
                        start=(o_ == 0), stop=(o_ == KT - 1)),
                        deps=s4d, event=f"st4_{o_}@{c}")
                if o_ >= 1:
                    emit_st34(o_ - 1)
                if o_ == KT - 1:
                    emit_st34(o_)
            EV[f"ydone@{c}"] = EV[f"y{KT-1}@{c}"]

            war2 = [f"bcR@{c}", f"bcM@{c}"]
            ln_rows(f"l2@{c}", 6, 7, f"st3_{KT-1}@{c}", f"st4_{KT-1}@{c}", war2)
            def emit_ln2_head(c=c):
                emit("t", lambda t: t.matmul(psum[6][:, 0:512], onesrf[:],
                                             rowsb[0:1, 0:512], start=True,
                                             stop=True),
                     deps=[f"l2@{c}_rstd"],
                     event=f"bcR2@{c}")
                emit("t", lambda t: t.matmul(psum[7][:, 0:512], onesrf[:],
                                             rowsb[0:1, 512:1024], start=True,
                                             stop=True),
                     deps=[f"l2@{c}_mr"], event=f"bcM2@{c}")
                emit("v", lambda e: e.tensor_copy(rsbb[:, 0:512],
                                                  psum[6][:, 0:512]),
                     deps=[f"bcR2@{c}", f"ap1_{KT-1}@{c}"], event=f"R2sb@{c}")
                emit("v", lambda e: e.tensor_copy(rsb[:, 512:1024],
                                                  psum[7][:, 0:512]),
                     deps=[f"bcM2@{c}", f"x1_{KT-1}@{c}"], event=f"M2sb@{c}")

            def emit_ln2_pair(kt, c=c):
                eng = "v" if kt % 2 == 0 else "p"
                a2d = [f"R2sb@{c}", f"up{NE-1}_{IT-1}@{c}"]
                if kt >= 2:
                    a2d.append(f"os_{kt-2}@{c}")
                emit(eng, lambda e, kt=kt: e.tensor_tensor(
                    apb[:, (kt % 2) * 512:(kt % 2 + 1) * 512],
                    r2[:, kt * 512:(kt + 1) * 512], rsbb[:, 0:512],
                    ALU.mult),
                    deps=a2d, event=f"ap2_{kt}@{c}")
                emit(eng, lambda e, kt=kt: e.tensor_tensor(
                    outt[:, (kt % 2) * 512:(kt % 2 + 1) * 512],
                    apb[:, (kt % 2) * 512:(kt % 2 + 1) * 512],
                    rsb[:, 512:1024], ALU.subtract),
                    deps=[f"ap2_{kt}@{c}", f"M2sb@{c}"], event=f"o_{kt}@{c}")
                dma("y", OS[kt % 2], out_d[c, kt],
                    outt[:, (kt % 2) * 512:(kt % 2 + 1) * 512],
                    deps=[f"o_{kt}@{c}"], event=f"os_{kt}@{c}")
            pend_bc2.append((emit_ln2_head, emit_ln2_pair))

        while pend_bc2:
            th_, tp_ = pend_bc2.pop(0)
            th_()
            for kt in range(KT):
                tp_(kt)

        def final(e):
            for s in ALL_DMA_SEMS:
                e.wait_ge(s.h, s.n)
        plan["y"].append(final)

        block = es.enter_context(nc.Block())

        @block.tensor
        def _(t):
            for f in plan["t"]:
                f(t)

        @block.vector
        def _(v):
            for f in plan["v"]:
                f(v)

        @block.scalar
        def _(s):
            for f in plan["s"]:
                f(s)

        @block.gpsimd
        def _(g):
            for f in plan["g"]:
                f(g)

        @block.sync
        def _(y):
            for f in plan["y"]:
                f(y)

    return nc


def layout_inputs(x, Wq, Wo, bank_weights, memories, memory_norms, W_gate,
                  Wg, Wu, Wd, n_cores, tokens_per_core, n_chunks):
    asc = np.ascontiguousarray
    f32 = np.float32
    x = np.asarray(x, f32).reshape(-1, D)
    shared = {
        "wqt": asc(np.asarray(Wq, f32).reshape(KT, 128, KT, 128)
                   .transpose(0, 3, 2, 1)),
        "wot": asc(np.asarray(Wo, f32).reshape(KT, 128, KT, 128)
                   .transpose(0, 3, 2, 1)),
        "wg": asc(np.asarray(Wg, f32).reshape(NE, KT, 128, IT, 128)
                  .transpose(0, 3, 2, 1, 4)),
        "wu": asc(np.asarray(Wu, f32).reshape(NE, KT, 128, IT, 128)
                  .transpose(0, 3, 2, 1, 4)),
        "wd": asc(np.asarray(Wd, f32).reshape(NE, IT, 128, KT, 128)
                  .transpose(3, 2, 0, 1, 4)),
        "wgt": asc(np.asarray(W_gate, f32).T.reshape(KT, 128, 8)
                   .transpose(1, 0, 2).reshape(128, KT * 8)),
        "mem": asc(np.asarray(memories, f32).transpose(2, 1, 0, 3)
                   .reshape(128, NB * H * 128)),
        "norms": asc(np.asarray(memory_norms, f32).transpose(2, 1, 0)
                     .reshape(128, H * NB)),
        "bw": asc(np.asarray(bank_weights, f32)),
        "ident": np.eye(128, dtype=f32),
    }
    in_maps = []
    for i in range(n_cores):
        xc = x[i * tokens_per_core:(i + 1) * tokens_per_core]
        xh = asc(xc.T.reshape(KT, 128, n_chunks, C).transpose(2, 1, 0, 3))
        m = dict(shared)
        m["xt"] = xh
        in_maps.append(m)
    return in_maps


def kernel(L0_attn_hidden_states, L0_attn_Wq, L0_attn_Wk, L0_attn_Wv,
           L0_attn_Wo, L0_attn_bank_weights, L0_attn_memories,
           L0_attn_memory_norms, L0_ffn_W_gate, L0_ffn_Wg, L0_ffn_Wu,
           L0_ffn_Wd):
    n_cores = 8
    total_tokens = int(np.prod(L0_attn_hidden_states.shape[:-1]))
    tokens_per_core = total_tokens // n_cores
    n_chunks = tokens_per_core // C
    in_maps = layout_inputs(L0_attn_hidden_states, L0_attn_Wq, L0_attn_Wo,
                            L0_attn_bank_weights, L0_attn_memories,
                            L0_attn_memory_norms, L0_ffn_W_gate, L0_ffn_Wg,
                            L0_ffn_Wu, L0_ffn_Wd, n_cores, tokens_per_core,
                            n_chunks)
    nc = build_program(n_chunks)
    res = run_bass_kernel_spmd(nc, in_maps, core_ids=list(range(n_cores)))
    parts = []
    for i in range(n_cores):
        arr = np.asarray(res.results[i]["out"])
        parts.append(arr.transpose(0, 3, 1, 2).reshape(tokens_per_core, D))
    full = np.concatenate(parts, axis=0)
    return np.ascontiguousarray(
        full.reshape(L0_attn_hidden_states.shape).astype(np.float32))
